# version 27
# baseline (speedup 1.0000x reference)
import numpy as np
import ml_dtypes
import bass_rust
import concourse.bass as bass
import concourse.mybir as mybir
from concourse.bass_utils import run_bass_kernel_spmd
from contextlib import ExitStack

F32 = mybir.dt.float32
BF16 = mybir.dt.bfloat16
AF = mybir.ActivationFunctionType
ALU = mybir.AluOpType
AX = mybir.AxisListType
BFNP = ml_dtypes.bfloat16

NCORES = 8
T = 8192
D = 2048
TOK = 1024
NH = 16
LC = 128
NKV = 8
HD = 128
WINDOW = 1024
INTER = 8192
NCTRL = 130156
NSAMP = INTER * D
RUN = 128
NRUNS = NSAMP // RUN
TPM = NSAMP // (128 * 128)
NTILES = 3 * TPM
TPC = NTILES // NCORES
GEN_BATCH = 96
NEG = -1.0e9
SCALE = float(1.0 / np.sqrt(128.0))
EPS = 1e-5

STOP_AFTER = 4
DBG_CCW = False
_LAST_RES = None
ENGMAP = {"t": "tensor", "v": "vector", "s": "scalar", "g": "gpsimd", "y": "sync"}


def gpos(core, row):
    return 1024 * (row // 128) + 128 * core + (row % 128)


def grow(p):
    return ((p % 1024) // 128) * 1024 + (p // 1024) * 128 + (p % 128)


def make_rope_a():
    half = LC // 2
    inv = 10000.0 ** (-np.arange(half, dtype=np.float64) / half)
    freq = np.repeat(inv, 2)
    ang = freq[:, None] * np.arange(T, dtype=np.float64)[None, :]
    cos = np.cos(ang).astype(np.float32)
    sin = np.sin(ang).astype(np.float32)
    Pi = np.zeros((LC, LC), np.float32)
    for i in range(half):
        Pi[2 * i, 2 * i + 1] = -1.0
        Pi[2 * i + 1, 2 * i] = 1.0
    return cos, sin, np.ascontiguousarray(Pi.T)


def make_rope_b():
    half = HD // 2
    inv = 1.0 / (100000.0 ** (np.arange(0, HD, 2, dtype=np.float64) / HD))
    pos = np.arange(T, dtype=np.float64)
    cos = np.zeros((HD, T), np.float64)
    ssin = np.zeros((HD, T), np.float64)
    for d in range(HD):
        f = inv[d % half]
        cos[d] = np.cos(pos * f)
        ssin[d] = np.sin(pos * f) * (-1.0 if d < half else 1.0)
    Pi2 = np.zeros((HD, HD), np.float32)
    for d in range(half):
        Pi2[d, d + half] = 1.0
        Pi2[d + half, d] = 1.0
    return cos.astype(np.float32), ssin.astype(np.float32), np.ascontiguousarray(Pi2.T)


_SPL = {}


def spline_meta():
    if "m" not in _SPL:
        t = np.linspace(0.0, NCTRL - 1.0, NSAMP, dtype=np.float64)
        n0 = np.arange(NRUNS, dtype=np.int64) * RUN
        t0 = t[n0]
        i0 = np.clip(np.floor(t0).astype(np.int64), 0, NCTRL - 2)
        phi = (t0 - i0).astype(np.float32)
        p = np.arange(128)
        runmap = np.zeros((NTILES, 128), np.int64)
        for tt in range(TPM):
            oc, hb = tt // 16, tt % 16
            runmap[tt] = (128 * oc + p) * 16 + hb
            runmap[TPM + tt] = (128 * oc + p) * 16 + hb
        for tt in range(TPM):
            hc, ic = tt // 64, tt % 64
            runmap[2 * TPM + tt] = (128 * hc + p) * 64 + ic
        _SPL["m"] = (i0, phi, runmap)
    return _SPL["m"]


def spline_core_tables(core, cps):
    i0, phi, runmap = spline_meta()
    lo = core * TPC
    cp6 = np.zeros((TPC, 128, 8), np.float32)
    phv = np.zeros((TPC, 128), np.float32)
    for j in range(TPC):
        gt = lo + j
        runs = runmap[gt]
        ii = i0[runs]
        idx = np.clip(ii[:, None] - 1 + np.arange(8)[None, :], 0, NCTRL - 1)
        cp6[j] = cps[gt // TPM][idx]
        phv[j] = phi[runs]
    cp6_dev = np.ascontiguousarray(cp6.transpose(1, 0, 2).reshape(128, TPC * 8))
    phi_dev = np.ascontiguousarray(phv.T)
    return cp6_dev, phi_dev


def make_mask_a(core):
    r = np.arange(128)[:, None]
    j = np.arange(1024)[None, :]
    return np.where(j <= 128 * core + r, 0.0, NEG).astype(BFNP)


def make_masks_b(core):
    m = np.zeros((8, 128, 2048), np.float32)
    for qt in range(8):
        kbase = 1024 * (qt - 1) if qt >= 1 else 0
        gq = 1024 * qt + 128 * core
        r = np.arange(128)[:, None]
        j = np.arange(2048)[None, :]
        k = kbase + j
        q = gq + r
        m[qt] = np.where((k <= q) & (k > q - WINDOW), 0.0, NEG)
    return np.ascontiguousarray(m.reshape(8 * 128, 2048)).astype(BFNP)


class Buf:
    __slots__ = ("wtok", "rtoks", "name")

    def __init__(self, name=""):
        self.wtok = None
        self.rtoks = []
        self.name = name


class Prog:
    def __init__(self):
        self.nc = bass.Bass("TRN2", target_bir_lowering=False)
        self.stack = ExitStack()
        self._sems = {}
        self._waited = {}
        self.bufs = {}

    def B(self, name):
        if name not in self.bufs:
            self.bufs[name] = Buf(name)
        return self.bufs[name]

    def sem(self, name):
        if name not in self._sems:
            h = self.stack.enter_context(self.nc.semaphore(name))
            self._sems[name] = [h, 0]
        return self._sems[name]

    def wait(self, engname, toks):
        eng = getattr(self.nc, ENGMAP[engname])
        best = {}
        for (name, v) in toks:
            if best.get(name, -1) < v:
                best[name] = v
        for name, v in best.items():
            key = (engname, name)
            if self._waited.get(key, 0) >= v:
                continue
            eng.wait_ge(self.sem(name)[0], v)
            self._waited[key] = v

    def op(self, engname, emit, reads=(), writes=(), sem=None, n=1):
        toks = []
        for b in reads:
            if b.wtok is not None:
                toks.append(b.wtok)
        for b in writes:
            if b.wtok is not None:
                toks.append(b.wtok)
            toks.extend(b.rtoks)
        if engname == "t":
            toks = [t for t in toks if t[0] != "c_t"]
        self.wait(engname, toks)
        instr = emit()
        se = self.sem(sem or ("c_" + engname))
        se[1] += n
        instr.then_inc(se[0], n)
        tok = (sem or ("c_" + engname), se[1])
        for b in reads:
            b.rtoks.append(tok)
            if len(b.rtoks) > 12:
                best = {}
                for (nm, v) in b.rtoks:
                    if best.get(nm, -1) < v:
                        best[nm] = v
                b.rtoks = list(best.items())
        for b in writes:
            b.wtok = tok
            b.rtoks = []
        return tok

    def dma(self, q, out, in_, reads=(), writes=(), **kw):
        eng = getattr(self.nc, ENGMAP[q])
        return self.op(q, lambda: eng.dma_start(out=out, in_=in_, **kw),
                       reads=reads, writes=writes, sem="d_" + q, n=16)

    def din(self, name, shape, dt=F32):
        return self.nc.dram_tensor(name, shape, dt, kind="ExternalInput").ap()

    def sb(self, name, shape, dt=F32, st=None):
        return (st or self.stack).enter_context(self.nc.sbuf_tensor(name, shape, dt))[:]

    def ts(self, e, out, in0, s1, s2, op0, op1, reads, writes, accum_out=None):
        eng = getattr(self.nc, ENGMAP[e])
        return self.op(e, lambda: eng.tensor_scalar(out, in0, s1, s2, op0=op0, op1=op1,
                                                    accum_out=accum_out), reads, writes)

    def tt(self, e, out, in0, in1, alu, reads, writes):
        eng = getattr(self.nc, ENGMAP[e])
        return self.op(e, lambda: eng.tensor_tensor(out, in0, in1, op=alu), reads, writes)

    def stt(self, e, out, in0, scalar, in1, op0, op1, reads, writes):
        eng = getattr(self.nc, ENGMAP[e])
        return self.op(e, lambda: eng.scalar_tensor_tensor(out, in0, scalar, in1,
                                                           op0=op0, op1=op1), reads, writes)

    def act(self, out, in_, func, reads, writes, **kw):
        return self.op("s", lambda: self.s.activation(out, in_, func, **kw), reads, writes)

    def copy(self, e, out, in_, reads, writes):
        if e == "s":
            return self.op("s", lambda: self.s.copy(out, in_), reads, writes)
        eng = getattr(self.nc, ENGMAP[e])
        return self.op(e, lambda: eng.tensor_copy(out, in_), reads, writes)

    def mm(self, out, lhsT, rhs, start, stop, reads, writes):
        return self.op("t", lambda: self.t.matmul(out, lhsT, rhs, start=start, stop=stop),
                       reads, writes)

    def reduce(self, out, in_, alu, reads, writes, negate=False):
        return self.op("v", lambda: self.v.tensor_reduce(out, in_, axis=AX.X, op=alu,
                                                         negate=negate), reads, writes)

    def tgroup(self, items, dst_ap, dst_buf, evac_eng="s", dtype="b"):
        n = len(items)
        if dtype == "b":
            h = self._tg_flip = 1 - getattr(self, "_tg_flip", 1)
            ps = self.psT_h[h]
            pbuf = self.bT[h]
            ident, ib = self.identb, self.B("identb")
        else:
            ps = self.psB_b
            pbuf = self.bBb
            ident, ib = self.identf, self.B("identf")
        for i, (ap, rb) in enumerate(items):
            self.op("t", lambda ap=ap, i=i: self.t.transpose(
                ps[:, i * 128:(i + 1) * 128], ap, ident),
                reads=list(rb) + [ib], writes=[pbuf])
        self.copy(evac_eng, dst_ap, ps[:, 0:n * 128], [pbuf], [dst_buf])

    def tsingle(self, in_ap, in_bufs, dst_ap, dst_buf, evac_eng="s", dtype="b"):
        self.tgroup([(in_ap, in_bufs)], dst_ap, dst_buf, evac_eng, dtype)

    def build(self):
        nc = self.nc
        self.t, self.v, self.s, self.g, self.y = (nc.tensor, nc.vector, nc.scalar,
                                                  nc.gpsimd, nc.sync)
        B = self.B

        self.x_in = self.din("x", [TOK, D])
        self.w_latent = self.din("w_latent", [D, LC])
        self.w_d = self.din("w_d", [D, NH * LC])
        self.w_proj = self.din("w_proj", [NH * LC, D])
        self.w_pack = self.din("w_pack", [D, 4096])
        self.w_o = self.din("w_o", [NH * HD, D])
        self.b_latent = self.din("b_latent", [128, 1])
        self.b_d = self.din("b_d", [128, NH])
        self.b_proj = self.din("b_proj", [128, 16])
        self.conv_k = self.din("conv_k", [NKV * 128, 2])
        self.conv_v = self.din("conv_v", [NKV * 128, 2])
        self.cp6_d = self.din("cp6", [128, TPC * 8])
        self.phi_d = self.din("phi", [128, TPC])
        self.f0c_d = self.din("f0c", [128, 128])
        self.cos_a = self.din("cos_a", [128, T])
        self.sin_a = self.din("sin_a", [128, T])
        self.cos_aq = self.din("cos_aq", [128, TOK])
        self.sin_aq = self.din("sin_aq", [128, TOK])
        self.cos_bq = self.din("cos_bq", [128, TOK])
        self.sin_bq = self.din("sin_bq", [128, TOK])
        self.cos_bk = self.din("cos_bk", [128, TOK])
        self.sin_bk = self.din("sin_bk", [128, TOK])
        self.piT_a = self.din("piT_a", [128, 128])
        self.piT_b = self.din("piT_b", [128, 128])
        self.ident_fd = self.din("ident_f", [128, 128])
        self.ident_bd = self.din("ident_b", [128, 128], BF16)
        self.mask_a_d = self.din("mask_a", [128, 1024], BF16)
        self.mask_b_d = self.din("mask_b", [8 * 128, 2048], BF16)

        self.out_d = nc.dram_tensor("out", [TOK, D], F32, kind="ExternalOutput").ap()
        self.resid = nc.dram_tensor("resid", [TOK, D], F32).ap()
        self.contrib = nc.dram_tensor("contrib", [TOK, D], F32).ap()
        self.cc1_in = nc.dram_tensor("cc1_in", [TOK, LC], F32).ap()
        self.cc1_out = nc.dram_tensor("cc1_out", [T, LC], F32, addr_space="Shared").ap()
        self.cc1f_in = nc.dram_tensor("cc1f_in", [128, TOK], F32).ap()
        self.cc1f_out = nc.dram_tensor("cc1f_out", [8 * 128, TOK], F32).ap()
        self.cc2_in = [nc.dram_tensor(f"cc2_in{k}", [128, 2048], BF16).ap()
                       for k in range(NKV)]
        self.cc2_out = [nc.dram_tensor(f"cc2_out{k}", [8 * 128, 2048], BF16).ap()
                        for k in range(NKV)]
        self.ccw_in = nc.dram_tensor("ccw_in", [TPC * 128, 128], BF16).ap()
        self.ccw_out = nc.dram_tensor("ccw_out", [NTILES * 128, 128], BF16, addr_space="Shared").ap()
        if DBG_CCW:
            self.dbg1 = nc.dram_tensor("dbg1", [TPC * 128, 128], BF16,
                                       kind="ExternalOutput").ap()
        self.bar_in = nc.dram_tensor("bar_in", [1, 128], F32).ap()
        self.bar_out = nc.dram_tensor("bar_out", [1, 128], F32).ap()

        sb = self.sb
        self.identb = sb("identb", [128, 128], BF16)
        self.identf = sb("identf", [128, 128], F32)
        self.piTa_s = sb("piTa_s", [128, 128], F32)
        self.piTb_s = sb("piTb_s", [128, 128], F32)
        self.st1 = sb("st1", [128, 64], F32)
        self.ast = sb("ast", [128, 128], F32)
        self.scrA = sb("scrA", [128, D], F32)
        self.scrB = sb("scrB", [128, D], F32)
        self.xntok = sb("xntok", [128, D], BF16)
        self.bias_sb = sb("bias_sb", [128, 33], F32)

        psA = self.stack.enter_context(nc.psum_tensor("psA", [128, 2048], F32))[:]
        psB = self.stack.enter_context(nc.psum_tensor("psB", [128, 1024], F32))[:]
        psT = self.stack.enter_context(nc.psum_tensor("psT", [128, 2048], BF16))[:]
        self.psA_h = [psA[:, 0:1024], psA[:, 1024:2048]]
        self.psA4 = [psA[:, k * 512:(k + 1) * 512] for k in range(4)]
        self.psB = psB
        self.psB_a = psB[:, 0:512]
        self.psB_b = psB[:, 512:1024]
        self.psT_h = [psT[:, 0:1024], psT[:, 1024:2048]]
        self.bA = [B("psA0"), B("psA1")]
        self.bBa, self.bBb = B("psB_a"), B("psB_b")
        self.bT = [B("psT0"), B("psT1")]

        self.dma("y", self.identb, self.ident_bd, writes=[B("identb")])
        self.dma("y", self.identf, self.ident_fd, writes=[B("identf")])
        self.dma("y", self.piTa_s, self.piT_a, writes=[B("piTa")])
        self.dma("y", self.piTb_s, self.piT_b, writes=[B("piTb")])
        self.dma("y", self.bias_sb[:, 0:1], self.b_latent, writes=[B("bias")])
        self.dma("y", self.bias_sb[:, 1:17], self.b_d, writes=[B("bias")])
        self.dma("y", self.bias_sb[:, 17:33], self.b_proj, writes=[B("bias")])

        self.op("v", lambda: self.v.memset(self.st1[:, 63:64], EPS), [], [B("lnst")])
        self._spline_iter = self.spline_generator()
        self._spline_done = False
        self.pump_spline(1)

        self.sub1()
        if STOP_AFTER >= 2:
            self.pump_spline(1 << 30)
            self._spline_stack.close()
            self.sub_ffn(first=True)
        if STOP_AFTER >= 3:
            self.sub3()
        if STOP_AFTER >= 4:
            self.sub_ffn(first=False)
        self.finish()
        return nc

    def spline_generator(self):
        st = ExitStack()
        self._spline_stack = st
        B = self.B
        sbl = lambda n, s, d=F32: self.sb(n, s, d, st=st)
        cp6 = sbl("sp_cp6", [128, GEN_BATCH * 8])
        phi = sbl("sp_phi", [128, TPC])
        f0c = sbl("sp_f0c", [128, 128])
        co = sbl("sp_co", [128, GEN_BATCH * 12])
        NDEP = 3
        gm = [sbl(f"sp_m{i}", [128, 128]) for i in range(NDEP)]
        gf = [sbl(f"sp_f{i}", [128, 128]) for i in range(NDEP)]
        gf2 = [sbl(f"sp_f2{i}", [128, 128]) for i in range(NDEP)]
        gf3 = [sbl(f"sp_f3{i}", [128, 128]) for i in range(NDEP)]
        gu = [sbl(f"sp_u{i}", [128, 128]) for i in range(NDEP)]
        gv = [sbl(f"sp_v{i}", [128, 128]) for i in range(NDEP)]
        gw = [sbl(f"sp_w{i}", [128, 128], BF16) for i in range(NDEP)]
        gtmp = sbl("sp_tmp", [128, GEN_BATCH])
        gstage = sbl("sp_stage", [128, 1024], BF16)
        bco, bphi, bf0, bcp = B("sp_co"), B("sp_phi"), B("sp_f0c"), B("sp_cp6")
        btmp, bstage = B("sp_tmp"), B("sp_stage")
        self.dma("y", phi, self.phi_d, writes=[bphi])
        self.dma("y", f0c, self.f0c_d, writes=[bf0])
        cp3 = cp6.rearrange("p (t e) -> p t e", e=8)
        co3 = co.rearrange("p (k t) -> p k t", k=12)
        gt1 = gtmp.unsqueeze(1)

        def pk(q, k):
            return cp3[:, :, q + k:q + k + 1].transpose([0, 2, 1])

        NB = GEN_BATCH
        flush_prev = [lambda: None]
        for batch in range(TPC // NB):
            t0 = batch * NB
            self.dma("y", cp6, self.cp6_d[:, t0 * 8:(t0 + NB) * 8], writes=[bcp])
            for q in (0, 1):
                o = q * 4
                A = co3[:, o + 0:o + 1, :]
                Bc = co3[:, o + 1:o + 2, :]
                C = co3[:, o + 2:o + 3, :]
                Dc = co3[:, o + 3:o + 4, :]
                self.copy("v", A, pk(q, 1), [bcp], [bco])
                self.tt("v", gt1, pk(q, 2), pk(q, 0), ALU.subtract, [bcp], [btmp])
                self.ts("v", Bc, gt1, 0.5, 0.0, ALU.mult, ALU.add, [btmp], [bco])
                self.stt("v", gt1, pk(q, 1), -2.5, pk(q, 0), ALU.mult, ALU.add, [bcp], [btmp])
                self.stt("v", gt1, pk(q, 2), 2.0, gt1, ALU.mult, ALU.add, [bcp, btmp], [btmp])
                self.stt("v", C, pk(q, 3), -0.5, gt1, ALU.mult, ALU.add, [bcp, btmp], [bco])
                self.stt("v", gt1, pk(q, 1), 3.0, pk(q, 0), ALU.mult, ALU.subtract, [bcp], [btmp])
                self.stt("v", gt1, pk(q, 2), -3.0, gt1, ALU.mult, ALU.add, [bcp, btmp], [btmp])
                self.tt("v", gt1, gt1, pk(q, 3), ALU.add, [bcp, btmp], [btmp])
                self.ts("v", Dc, gt1, 0.5, 0.0, ALU.mult, ALU.add, [btmp], [bco])
            for k in range(4):
                self.tt("v", co3[:, 8 + k:9 + k, :], co3[:, 4 + k:5 + k, :],
                        co3[:, k:k + 1, :], ALU.subtract, [bco], [bco])
            if batch >= 1:
                flush_prev[0]()
            yield
            bm = [B(f"sp_bm{i}") for i in range(NDEP)]
            bfb = [B(f"sp_bf{i}") for i in range(NDEP)]
            bf2 = [B(f"sp_bf2{i}") for i in range(NDEP)]
            bf3 = [B(f"sp_bf3{i}") for i in range(NDEP)]
            bu = [B(f"sp_bu{i}") for i in range(NDEP)]
            bv = [B(f"sp_bv{i}") for i in range(NDEP)]
            bw = [B(f"sp_bw{i}") for i in range(NDEP)]
            def flush_tile(j, t0=t0):
                tloc = t0 + j
                i = j % NDEP
                gslot = j % 8
                self.tsingle(gw[i], [bw[i]], gstage[:, gslot * 128:(gslot + 1) * 128],
                             bstage, evac_eng="v")
                if gslot == 7:
                    g0 = tloc - 7
                    dst = bass_rust.AP(self.ccw_in.tensor, g0 * 128 * 128,
                                       [[128, 128], [128 * 128, 8], [1, 128]])
                    self.dma("g", dst, gstage.rearrange("p (t f) -> p t f", f=128),
                             reads=[bstage], writes=[B("ccw_in")])

            for j in range(NB):
                tloc = t0 + j
                i = j % NDEP
                ph = phi[:, tloc:tloc + 1]

                def cc(k, j=j):
                    return co3[:, k:k + 1, j:j + 1].squeeze(1)

                if j >= 1:
                    flush_tile(j - 1)
                self.ts("v", gm[i], f0c, ph, 1.0, ALU.add, ALU.is_ge, [bf0, bphi], [bm[i]])
                self.stt("v", gf[i], f0c, ph, gm[i], ALU.add, ALU.subtract,
                         [bf0, bphi, bm[i]], [bfb[i]])
                self.tt("g", gf2[i], gf[i], gf[i], ALU.mult, [bfb[i]], [bf2[i]])
                self.tt("g", gf3[i], gf2[i], gf[i], ALU.mult, [bfb[i], bf2[i]], [bf3[i]])
                self.act(gu[i], gf[i], AF.Identity, [bfb[i], bco], [bu[i]],
                         scale=cc(1), bias=cc(0))
                self.stt("v", gu[i], gf2[i], cc(2), gu[i], ALU.mult, ALU.add,
                         [bf2[i], bco, bu[i]], [bu[i]])
                self.stt("v", gu[i], gf3[i], cc(3), gu[i], ALU.mult, ALU.add,
                         [bf3[i], bco, bu[i]], [bu[i]])
                self.act(gv[i], gf[i], AF.Identity, [bfb[i], bco], [bv[i]],
                         scale=cc(9), bias=cc(8))
                self.stt("v", gv[i], gf2[i], cc(10), gv[i], ALU.mult, ALU.add,
                         [bf2[i], bco, bv[i]], [bv[i]])
                self.stt("v", gv[i], gf3[i], cc(11), gv[i], ALU.mult, ALU.add,
                         [bf3[i], bco, bv[i]], [bv[i]])
                self.tt("g", gv[i], gv[i], gm[i], ALU.mult, [bv[i], bm[i]], [bv[i]])
                self.tt("g", gw[i], gu[i], gv[i], ALU.add, [bu[i], bv[i]], [bw[i]])
                yield
            flush_prev[0] = (lambda NBm1=NB - 1: flush_tile(NBm1))
        flush_prev[0]()
        self.op("g", lambda: self.g.collective_compute(
            "AllGather", ALU.bypass, replica_groups=[list(range(NCORES))],
            ins=[self.ccw_in.opt()], outs=[self.ccw_out.opt()]),
            reads=[B("ccw_in")], writes=[B("ccw_out_src")], sem="ag_w", n=1)
        self.cc_barrier("ccw_out")
        if DBG_CCW:
            self.dma("y", self.dbg1, self.ccw_in, reads=[B("ccw_in")],
                     writes=[B("dbg1")])
        self._spline_done = True

    def cc_barrier(self, name):
        self.op("g", lambda: self.g.collective_compute(
            "AllReduce", ALU.add, replica_groups=[list(range(NCORES))],
            ins=[self.bar_in.opt()], outs=[self.bar_out.opt()]),
            reads=[self.B(name + "_src")], writes=[self.B(name)], sem=name + "_b", n=1)

    def pump_spline(self, n):
        if self._spline_done:
            return
        for _ in range(n):
            try:
                next(self._spline_iter)
            except StopIteration:
                self._spline_done = True
                break

    def layernorm(self, src_dram, src_buf, xn, xn_buf, store_to=None, store_buf=None,
                  combine=False):
        B = self.B
        rt = [self.scrA, self.scrB]
        rb = [B("scrA"), B("scrB")]
        stct = ExitStack()
        tag = self._ln_tag = getattr(self, "_ln_tag", 0) + 1
        if combine:
            ct0 = self.sb(f"ctA{tag}", [128, D], F32, st=stct)
            ct = [ct0, ct0]
            cb = [B("ctA"), B("ctA")]
        xnt = [self.xntok[:, 0:1024], self.xntok[:, 1024:2048]]
        bxnt = [B("xntok0"), B("xntok1")]
        for tt in range(8):
            i = tt % 2
            c = self.st1[:, i * 16: i * 16 + 16]
            bst = B(f"lnst{i}")
            self.dma("y", rt[i], src_dram[tt * 128:(tt + 1) * 128, :],
                     reads=[src_buf], writes=[rb[i]])
            if combine:
                self.dma("g", ct[i], self.contrib[tt * 128:(tt + 1) * 128, :],
                         reads=[B("contrib")], writes=[cb[i]])
                self.tt("v", rt[i], rt[i], ct[i], ALU.add, [rb[i], cb[i]], [rb[i]])
                self.dma("y", src_dram[tt * 128:(tt + 1) * 128, :], rt[i],
                         reads=[rb[i]], writes=[src_buf])
            if store_to is not None:
                self.dma("y", store_to[tt * 128:(tt + 1) * 128, :], rt[i],
                         reads=[rb[i]], writes=[store_buf])
            self.act(xnt[i], rt[i][:, 0:1024], AF.Square, [rb[i]], [bxnt[i], bst],
                     accum_out=c[:, 0:1])
            self.act(xnt[i], rt[i][:, 1024:2048], AF.Square, [rb[i]], [bxnt[i], bst],
                     accum_out=c[:, 9:10])
            self.op("s", lambda c=c: self.s.copy(c[:, 10:11], c[:, 9:10]), [bst], [bst])
            self.reduce(c[:, 1:2], rt[i], ALU.add, [rb[i]], [bst])
            self.tt("v", c[:, 0:1], c[:, 0:1], c[:, 10:11], ALU.add, [bst], [bst])
            self.ts("v", c[:, 2:3], c[:, 1:2], 1.0 / D, 0.0, ALU.mult, ALU.add, [bst], [bst])
            self.ts("v", c[:, 3:4], c[:, 0:1], 1.0 / D, 0.0, ALU.mult, ALU.add, [bst], [bst])
            self.tt("v", c[:, 4:5], c[:, 2:3], c[:, 2:3], ALU.mult, [bst], [bst])
            self.tt("v", c[:, 5:6], c[:, 3:4], c[:, 4:5], ALU.subtract, [bst], [bst])
            self.act(c[:, 6:7], c[:, 5:6], AF.Ln, [bst], [bst], bias=self.st1[:, 63:64])
            self.act(c[:, 7:8], c[:, 6:7], AF.Exp, [bst], [bst], scale=-0.5)
            self.stt("v", c[:, 8:9], c[:, 2:3], -1.0, c[:, 7:8], ALU.mult, ALU.mult,
                     [bst], [bst])
            for half in range(2):
                self.act(xnt[i], rt[i][:, half * 1024:(half + 1) * 1024], AF.Identity,
                         [rb[i], bst], [bxnt[i]], scale=c[:, 7:8], bias=c[:, 8:9])
                for fc4 in range(2):
                    h = self._tg_flip = 1 - getattr(self, "_tg_flip", 1)
                    ps = self.psT_h[h]
                    pbuf = self.bT[h]
                    for k in range(4):
                        fc = half * 8 + fc4 * 4 + k
                        ap = xnt[i][:, (fc4 * 4 + k) * 128:(fc4 * 4 + k + 1) * 128]
                        self.op("t", lambda ap=ap, k=k: self.t.transpose(
                            ps[:, k * 128:(k + 1) * 128], ap, self.identb),
                            reads=[bxnt[i], B("identb")], writes=[pbuf])
                    for k in range(4):
                        fc = half * 8 + fc4 * 4 + k
                        dst = xn[:, fc * TOK + tt * 128: fc * TOK + (tt + 1) * 128]
                        self.copy("s" if k % 2 == 0 else "v", dst,
                                  ps[:, k * 128:(k + 1) * 128], [pbuf], [xn_buf])
            self.pump_spline(2)
        stct.close()

    def proj_load(self, w_dram, w_cols0, no):
        OD = w_dram.tensor.shape[1]
        self._wstc = (getattr(self, "_wstc", 0) + 1) % 2
        gw = self.wst[self._wstc]
        gwb = self.B(f"wstream{self._wstc}")
        gw3 = gw.rearrange("p (f c) -> p f c", f=16)
        src = bass_rust.AP(w_dram.tensor, w_cols0,
                           [[OD, 128], [128 * OD, 16], [1, no * 128]])
        self.dma("g", gw3[:, :, 0:no * 128], src, writes=[gwb])
        return gw3, gwb

    def proj(self, xn, xn_buf, w_dram, w_cols0, n_out_chunks, dst_sb, dst_buf, dst_tok,
             bias_col=None, rope=None, pump=0, preloaded=None, hook=None):
        B = self.B
        for grp in range((n_out_chunks + 1) // 2):
            no = min(2, n_out_chunks - grp * 2)
            if preloaded is not None and grp == 0:
                gw3, gwb = preloaded
            else:
                gw3, gwb = self.proj_load(w_dram, w_cols0 + grp * 256, no)
            for o2 in range(no):
                oc = grp * 2 + o2
                ps = self.psA_h[o2]
                pb = self.bA[o2]
                for fc in range(16):
                    lhsT = gw3[:, fc:fc + 1, o2 * 128:(o2 + 1) * 128].squeeze(1)
                    for th in range(2):
                        self.mm(ps[:, th * 512:(th + 1) * 512], lhsT,
                                xn[:, fc * TOK + th * 512: fc * TOK + (th + 1) * 512],
                                start=(fc == 0), stop=(fc == 15),
                                reads=[gwb, xn_buf], writes=[pb])
                dst = dst_sb[:, (dst_tok + oc) * TOK: (dst_tok + oc + 1) * TOK]
                if rope is None:
                    if bias_col is not None:
                        self.act(dst, ps, AF.Identity, [pb, B("bias")], [dst_buf],
                                 bias=self.bias_sb[:, bias_col + oc:bias_col + oc + 1],
                                 scale=1.0)
                    else:
                        self.copy("s", dst, ps, [pb], [dst_buf])
                else:
                    piT_sb, piT_buf, cosq, sinq, tbb = rope
                    qlin = self.scrA[:, 0:1024]
                    qb = B("scrA")
                    if bias_col is not None:
                        self.act(qlin, ps, AF.Identity, [pb, B("bias")], [qb],
                                 bias=self.bias_sb[:, bias_col + oc:bias_col + oc + 1],
                                 scale=1.0)
                    else:
                        self.copy("s", qlin, ps, [pb], [qb])
                    for th in range(2):
                        self.mm(self.psB[:, th * 512:(th + 1) * 512], piT_sb,
                                qlin[:, th * 512:(th + 1) * 512], start=True, stop=True,
                                reads=[piT_buf, qb],
                                writes=[self.bBa if th == 0 else self.bBb])
                    self.tt("v", self.scrB[:, 0:1024], qlin, cosq, ALU.mult,
                            [qb, tbb], [B("scrB")])
                    self.tt("v", self.scrC[:, 0:1024], self.psB, sinq, ALU.mult,
                            [self.bBa, self.bBb, tbb], [B("scrC")])
                    self.tt("v", dst, self.scrB[:, 0:1024], self.scrC[:, 0:1024],
                            ALU.add, [B("scrB"), B("scrC")], [dst_buf])
                self.pump_spline(pump)
            if hook is not None:
                hook(grp)

    def attn_qt(self, cs):
        B = self.B
        c0 = cs[0]
        E = c0["E"]
        nQ = (E + 1023) // 1024
        mstart, mwidth, mask_ap, mbuf = c0["mask"]

        def stq_of(c):
            return self.ast[:, c["par"] * 32:(c["par"] + 1) * 32]

        def phaseA(c, Q):
            stq = stq_of(c)
            bstq = B(f"ast{c['par']}q{Q}")
            w = min(1024, E - Q * 1024)
            ps = self.psA_h[c["slot"]]
            pb = self.bA[c["slot"]]
            lhsT = c["Q_sb"][:, c["q_col"]:c["q_col"] + 128]
            for ch in range((w + 511) // 512):
                cw = min(512, w - ch * 512)
                col = Q * 1024 + ch * 512
                masked = (mstart <= col < mstart + mwidth)
                self.mm(ps[:, ch * 512: ch * 512 + cw], lhsT,
                        c["krot_sb"][:, c["kcol_of"](Q) + ch * 512:
                                     c["kcol_of"](Q) + ch * 512 + cw],
                        start=True, stop=not masked,
                        reads=[c["Q_buf"], c["krot_buf"]], writes=[pb])
                if masked:
                    self.mm(ps[:, ch * 512: ch * 512 + cw], self.identb,
                            mask_ap[:, col - mstart: col - mstart + cw],
                            start=False, stop=True,
                            reads=[B("identb"), mbuf], writes=[pb])
            self.reduce(stq[:, Q:Q + 1], ps[:, 0:w], ALU.max, [pb], [bstq], negate=True)
            sl = Q % c["Pslots"]
            pqb = B(f"{c['Pn']}_{sl}")
            self.act(c["P"][:, sl * 1024: sl * 1024 + w], ps[:, 0:w], AF.Exp,
                     [pb, bstq], [pqb, bstq],
                     scale=1.0, bias=stq[:, Q:Q + 1], accum_out=stq[:, 8 + Q:9 + Q])

        def phaseBtrans(c, Q):
            w = min(1024, E - Q * 1024)
            nkt = w // 128
            sl = Q % c["Pslots"]
            pqb = B(f"{c['Pn']}_{sl}")
            ptb = B(f"{c['PTn']}{Q % 2}")
            PT = c["PT"]
            h = self._tg_flip = 1 - getattr(self, "_tg_flip", 1)
            ps = self.psT_h[h]
            pbuf = self.bT[h]
            for i in range(nkt):
                ap = c["P"][:, (sl * 8 + i) * 128:(sl * 8 + i + 1) * 128]
                self.op("t", lambda ap=ap, i=i, ps=ps: self.t.transpose(
                    ps[:, i * 128:(i + 1) * 128], ap, self.identb),
                    reads=[pqb, B("identb")], writes=[pbuf])
            self.copy("v", PT[:, (Q % 2) * 1024:(Q % 2) * 1024 + nkt * 128],
                      ps[:, 0:nkt * 128], [pbuf], [ptb])

        def phaseBpv(c, Q):
            w = min(1024, E - Q * 1024)
            nkt = w // 128
            ptb = B(f"{c['PTn']}{Q % 2}")
            PT = c["PT"]
            ops = self.psB_a[:, 0:128] if c["slot"] == 0 else self.psB_b[:, 0:128]
            opb = self.bBa if c["slot"] == 0 else self.bBb
            for i in range(nkt):
                kt = Q * 8 + i
                self.mm(ops, PT[:, (Q % 2) * 1024 + i * 128:(Q % 2) * 1024 + (i + 1) * 128],
                        c["V_sb"][:, c["vcol_of"](kt): c["vcol_of"](kt) + 128],
                        start=(i == 0), stop=(i == nkt - 1),
                        reads=[ptb, c["V_buf"]], writes=[opb])
            self.copy("s", c["outq"][:, Q * 128:(Q + 1) * 128], ops, [opb],
                      [B(c["outq_n"])])

        for Q in range(nQ + 3):
            for c in cs:
                if Q < nQ:
                    phaseA(c, Q)
            for c in cs:
                if 0 <= Q - 2 < nQ:
                    phaseBtrans(c, Q - 2)
            for c in cs:
                if 0 <= Q - 3 < nQ:
                    phaseBpv(c, Q - 3)
            self.pump_spline(c0.get("pump", 0))

        for c in cs:
            stq = stq_of(c)
            bstq = [B(f"ast{c['par']}q{q}") for q in range(8)]
            bst = B(f"ast{c['par']}")
            outq = c["outq"]
            boutq = B(c["outq_n"])
            self.reduce(stq[:, 24:25], stq[:, 0:nQ], ALU.min, bstq[:nQ] + [bst], [bst])
            self.ts("v", stq[:, 0:nQ], stq[:, 0:nQ], stq[:, 24:25], -1.0,
                    ALU.subtract, ALU.mult, bstq[:nQ] + [bst], [bst] + bstq[:nQ])
            self.act(stq[:, 0:nQ], stq[:, 0:nQ], AF.Exp, [bst], [bst])
            self.tt("v", stq[:, 16:16 + nQ], stq[:, 0:nQ], stq[:, 8:8 + nQ], ALU.mult,
                    [bst, boutq], [bst])
            self.reduce(stq[:, 25:26], stq[:, 16:16 + nQ], ALU.add, [bst], [bst])
            self.op("v", lambda s=stq: self.v.reciprocal(s[:, 26:27], s[:, 25:26]),
                    [bst], [bst])
            self.ts("v", stq[:, 0:nQ], stq[:, 0:nQ], stq[:, 26:27], 0.0,
                    ALU.mult, ALU.bypass, [bst], [bst])
            if nQ == 1:
                self.ts("v", c["xa_ap"], outq[:, 0:128], stq[:, 0:1], 0.0,
                        ALU.mult, ALU.bypass, [boutq, bst], [B("xattn")])
            else:
                par = c["par"]
                acc = self.scrC[:, 1024 + par * 128:1152 + par * 128]
                bacc = B(f"merge_acc{par}")
                self.ts("v", acc, outq[:, 0:128], stq[:, 0:1], 0.0,
                        ALU.mult, ALU.bypass, [boutq, bst], [bacc])
                for Q in range(1, nQ - 1):
                    self.stt("v", acc, outq[:, Q * 128:(Q + 1) * 128], stq[:, Q:Q + 1],
                             acc, ALU.mult, ALU.add, [boutq, bst, bacc], [bacc])
                Q = nQ - 1
                self.stt("v", c["xa_ap"], outq[:, Q * 128:(Q + 1) * 128], stq[:, Q:Q + 1],
                         acc, ALU.mult, ALU.add, [boutq, bst, bacc], [B("xattn")])

    def xattn_tm_to_fm(self, heads=None):
        B = self.B
        for h in (range(NH) if heads is None else heads):
            hh = self._tg_flip = 1 - getattr(self, "_tg_flip", 1)
            ps = self.psT_h[hh]
            pbuf = self.bT[hh]
            for qt in range(8):
                sl = self.xattn[:, h * TOK + qt * 128: h * TOK + (qt + 1) * 128]
                self.op("t", lambda sl=sl, ps=ps, qt=qt: self.t.transpose(
                    ps[:, qt * 128:(qt + 1) * 128], sl, self.identb),
                    reads=[B("xattn"), B("identb")], writes=[pbuf])
            self.copy("v", self.xattn[:, h * TOK:(h + 1) * TOK], ps[:, 0:1024],
                      [pbuf], [B("xattn")])

    def out_proj(self, w_dram, bias_col, dst_dram, dst_buf, pump=0):
        B = self.B
        OD = w_dram.tensor.shape[1]
        for grp in range(8):
            self._wstc = (getattr(self, "_wstc", 0) + 1) % 2
            gw = self.wst[self._wstc]
            gwb = B(f"wstream{self._wstc}")
            gw3 = gw.rearrange("p (f c) -> p f c", f=16)
            src = bass_rust.AP(w_dram.tensor, grp * 256,
                               [[OD, 128], [128 * OD, 16], [1, 256]])
            self.dma("g", gw3, src, writes=[gwb])
            for o2 in range(2):
                oc = grp * 2 + o2
                ps = self.psA_h[o2]
                pb = self.bA[o2]
                for fc in range(16):
                    lhsT = gw3[:, fc:fc + 1, o2 * 128:(o2 + 1) * 128].squeeze(1)
                    for th in range(2):
                        self.mm(ps[:, th * 512:(th + 1) * 512], lhsT,
                                self.xattn[:, fc * TOK + th * 512: fc * TOK + (th + 1) * 512],
                                start=(fc == 0), stop=(fc == 15),
                                reads=[gwb, B("xattn")], writes=[pb])
                if bias_col is not None:
                    self.act(self.scrA[:, 0:1024], ps, AF.Identity, [pb, B("bias")],
                             [B("scrA")],
                             bias=self.bias_sb[:, bias_col + oc:bias_col + oc + 1], scale=1.0)
                else:
                    self.copy("s", self.scrA[:, 0:1024], ps, [pb], [B("scrA")])
                for tq in range(2):
                    items = [(self.scrA[:, (tq * 4 + k) * 128:(tq * 4 + k + 1) * 128],
                              [B("scrA")]) for k in range(4)]
                    self.tgroup(items, self.scrB[:, tq * 512:(tq + 1) * 512], B("scrB"),
                                "v", dtype="f")
                dst = bass_rust.AP(self.contrib.tensor, oc * 128,
                                   [[D, 128], [128 * D, 8], [1, 128]])
                self.dma("g", dst, self.scrB[:, 0:1024].rearrange("p (t f) -> p t f", f=128),
                         reads=[B("scrB")], writes=[B("contrib")])
                self.pump_spline(pump)

    def sub1(self):
        B = self.B
        st_sub = ExitStack()
        self.xattn = self.sb("xattn1", [128, 16 * TOK], BF16, st=st_sub)
        self.scrC = self.sb("scrC1", [128, D], F32, st=st_sub)
        self.wst = [self.sb("wstr1a", [128, 16 * 256], BF16, st=st_sub),
                    self.sb("wstr1b", [128, 16 * 256], BF16, st=st_sub)]
        self.bufs.pop("xattn", None)
        st_xn = ExitStack()
        xn = self.sb("xn1", [128, 16 * TOK], BF16, st=st_xn)
        bxn = B("xn1")
        self.layernorm(self.x_in, B("x_in"), xn, bxn,
                       store_to=self.resid, store_buf=B("resid"))
        st = ExitStack()
        sbl = lambda n, s, d=F32: self.sb(n, s, d, st=st)
        kvfm = sbl("kvfm", [128, 1024])
        kvtok = sbl("kvtok", [128, 1024])
        wl = sbl("wl", [128, 16 * 128], BF16)
        wl3 = wl.rearrange("p (f c) -> p f c", f=16)
        self.dma("g", wl3,
                 bass_rust.AP(self.w_latent.tensor, 0, [[LC, 128], [128 * LC, 16], [1, 128]]),
                 writes=[B("wl")])
        for fc in range(16):
            for th in range(2):
                self.mm(self.psA_h[0][:, th * 512:(th + 1) * 512],
                        wl3[:, fc:fc + 1, :].squeeze(1),
                        xn[:, fc * TOK + th * 512: fc * TOK + (th + 1) * 512],
                        start=(fc == 0), stop=(fc == 15), reads=[B("wl"), bxn],
                        writes=[self.bA[0]])
        self.act(kvfm, self.psA_h[0], AF.Identity, [self.bA[0], B("bias")], [B("kvfm")],
                 bias=self.bias_sb[:, 0:1], scale=1.0)
        self.dma("y", self.cc1f_in, kvfm, reads=[B("kvfm")], writes=[B("cc1f_in")])
        self.op("g", lambda: self.g.collective_compute(
            "AllGather", ALU.bypass, replica_groups=[list(range(NCORES))],
            ins=[self.cc1f_in.opt()], outs=[self.cc1f_out.opt()]),
            reads=[B("cc1f_in")], writes=[B("cc1f_out_src")], sem="ag_1f", n=1)
        for tq in range(2):
            items = [(kvfm[:, (tq * 4 + k) * 128:(tq * 4 + k + 1) * 128], [B("kvfm")])
                     for k in range(4)]
            self.tgroup(items, kvtok[:, tq * 512:(tq + 1) * 512], B("kvtok"), "v", dtype="f")
        self.dma("y", bass_rust.AP(self.cc1_in.tensor, 0, [[LC, 128], [128 * LC, 8], [1, LC]]),
                 kvtok.rearrange("p (t f) -> p t f", f=128), reads=[B("kvtok")],
                 writes=[B("cc1_in")])
        self.op("g", lambda: self.g.collective_compute(
            "AllGather", ALU.bypass, replica_groups=[list(range(NCORES))],
            ins=[self.cc1_in.opt()], outs=[self.cc1_out.opt()]),
            reads=[B("cc1_in")], writes=[B("cc1_out_src")], sem="ag_1", n=1)
        self.cc_barrier("cc1_out")
        B("cc1f_out").wtok = B("cc1_out").wtok
        st.close()

        st = ExitStack()
        sbl = lambda n, s, d=F32: self.sb(n, s, d, st=st)
        V = sbl("Vsb", [128, 64 * 128], BF16)
        krot = sbl("krot", [128, T], BF16)
        cosq = sbl("cosq", [128, 1024])
        sinq = sbl("sinq", [128, 1024])
        Qrot = sbl("Qrot", [128, 2048], BF16)
        maska = sbl("maska", [128, 1024], BF16)
        st_kb = ExitStack()
        kvx = self.sb("kvx", [128, 512], F32, st=st_kb)
        cosk = self.sb("cosk", [128, 512], F32, st=st_kb)
        sink = self.sb("sink", [128, 512], F32, st=st_kb)
        self.dma("y", cosq, self.cos_aq, writes=[B("ropeq")])
        self.dma("y", sinq, self.sin_aq, writes=[B("ropeq")])
        self.dma("y", maska, self.mask_a_d, writes=[B("maska")])
        ld0 = self.proj_load(self.w_d, 0, 2)
        self.proj(xn, bxn, self.w_d, 0, 2, Qrot, B("Qrot"), 0, bias_col=1,
                  rope=(self.piTa_s, B("piTa"), cosq, sinq, B("ropeq")), pump=1,
                  preloaded=ld0)
        nld = self.proj_load(self.w_d, 256, 2)
        for m in range(64):
            g0 = grow(m * 128)
            self.dma("g", V[:, m * 128:(m + 1) * 128],
                     bass_rust.AP(self.cc1_out.tensor, g0 * LC, [[LC, 128], [1, LC]]),
                     reads=[B("cc1_out")], writes=[B("V")])
        for ch in range(16):
            self.dma("y", cosk, self.cos_a[:, ch * 512:(ch + 1) * 512], writes=[B("ropek")])
            self.dma("y", sink, self.sin_a[:, ch * 512:(ch + 1) * 512], writes=[B("ropek")])
            src = bass_rust.AP(self.cc1f_out.tensor,
                               (4 * (ch % 2)) * 128 * TOK + (ch // 2) * 128,
                               [[TOK, 128], [128 * TOK, 4], [1, 128]])
            self.dma("y", kvx.rearrange("p (a j) -> p a j", j=128), src,
                     reads=[B("cc1f_out")], writes=[B("kvx")])
            self.mm(self.psB[:, 0:512], self.piTa_s, kvx[:, 0:512], start=True, stop=True,
                    reads=[B("piTa"), B("kvx")], writes=[self.bBa, self.bBb])
            self.tt("v", self.scrA[:, 0:512], kvx[:, 0:512], cosk, ALU.mult,
                    [B("kvx"), B("ropek")], [B("scrA")])
            self.tt("v", self.scrB[:, 0:512], self.psB[:, 0:512], sink, ALU.mult,
                    [self.bBa, self.bBb, B("ropek")], [B("scrB")])
            self.tt("v", krot[:, ch * 512:(ch + 1) * 512], self.scrA[:, 0:512],
                    self.scrB[:, 0:512], ALU.add, [B("scrA"), B("scrB")], [B("krot")])
            self.pump_spline(2)
        st_kb.close()
        P2 = [sbl(f"Pa{sx}", [128, 3072], BF16) for sx in range(2)]
        PT2 = [sbl(f"PTa{sx}", [128, 2048], BF16) for sx in range(2)]
        outq2 = [sbl(f"outq{sx}", [128, 8 * 128]) for sx in range(2)]

        it = 0
        for hp in range(8):
            if hp > 0:
                self.proj(xn, bxn, self.w_d, hp * 256, 2, Qrot, B("Qrot"), 0,
                          bias_col=1 + hp * 2,
                          rope=(self.piTa_s, B("piTa"), cosq, sinq, B("ropeq")), pump=1,
                          preloaded=nld)
                if hp + 1 < 8:
                    nld = self.proj_load(self.w_d, (hp + 1) * 256, 2)
            for qt in range(8):
                cs = []
                for sx in range(2):
                    h = hp * 2 + sx
                    cs.append(dict(
                        qt=qt, E=1024 * (qt + 1), par=(it % 2) * 2 + sx, slot=sx,
                        kcol_of=lambda Q: Q * 1024,
                        Q_sb=Qrot, Q_buf=B("Qrot"), q_col=sx * 1024 + qt * 128,
                        krot_sb=krot, krot_buf=B("krot"),
                        V_sb=V, V_buf=B("V"), vcol_of=lambda kt: kt * 128,
                        mask=(1024 * qt, 1024, maska, B("maska")),
                        P=P2[sx], Pn=f"Pa{sx}", Pslots=3, PT=PT2[sx], PTn=f"PTa{sx}",
                        outq=outq2[sx], outq_n=f"outq{sx}",
                        xa_ap=self.xattn[:, h * TOK + qt * 128:
                                         h * TOK + (qt + 1) * 128],
                        pump=(3 if qt < 2 else 2 if qt == 2 else 1)))
                self.attn_qt(cs)
                it += 1
            self.xattn_tm_to_fm([hp * 2, hp * 2 + 1])
        st.close()
        st_xn.close()
        self.out_proj(self.w_proj, 17, self.resid, B("resid"), pump=2)
        st_sub.close()

    def sub_ffn(self, first):
        B = self.B
        st_xn = ExitStack()
        xn = self.sb("xnf1" if first else "xnf2", [128, 16 * TOK], BF16, st=st_xn)
        bxn = B("xnf" + ("1" if first else "2"))
        self.layernorm(self.resid, B("resid"), xn, bxn, combine=True)
        st = ExitStack()
        sbl = lambda n, s, d=F32: self.sb(n, s, d, st=st)
        inter = sbl("inter" + ("1" if first else "2"), [128, 64 * TOK], BF16)
        wg = [sbl(f"wg0_{first}", [128, 16 * 128], BF16), sbl(f"wg1_{first}", [128, 16 * 128], BF16)]
        wu = [sbl(f"wu0_{first}", [128, 16 * 128], BF16), sbl(f"wu1_{first}", [128, 16 * 128], BF16)]
        gi = sbl(f"gi_{first}", [128, TOK], BF16)
        ui = sbl(f"ui_{first}", [128, TOK], BF16)

        def wtile_ap(gt, n=16):
            return bass_rust.AP(self.ccw_out.tensor, gt * 128 * 128,
                                [[128, 128], [128 * 128, n], [1, 128]])

        bA4 = [B(f"psA4_{k}") for k in range(4)]
        for ic in range(64):
            i2 = ic % 2
            wg3 = wg[i2].rearrange("p (f c) -> p f c", f=16)
            wu3 = wu[i2].rearrange("p (f c) -> p f c", f=16)
            self.dma("y", wg3, wtile_ap(ic * 16), reads=[B("ccw_out")], writes=[B(f"wg{i2}")])
            self.dma("g", wu3, wtile_ap(TPM + ic * 16), reads=[B("ccw_out")],
                     writes=[B(f"wu{i2}")])
            for th in range(2):
                j = ic * 2 + th
                sg, su = (2 * j) % 4, (2 * j + 1) % 4
                for fc in range(16):
                    lg = wg3[:, fc:fc + 1, :].squeeze(1)
                    self.mm(self.psA4[sg], lg,
                            xn[:, fc * TOK + th * 512: fc * TOK + (th + 1) * 512],
                            start=(fc == 0), stop=(fc == 15),
                            reads=[B(f"wg{i2}"), bxn], writes=[bA4[sg]])
                for fc in range(16):
                    lu = wu3[:, fc:fc + 1, :].squeeze(1)
                    self.mm(self.psA4[su], lu,
                            xn[:, fc * TOK + th * 512: fc * TOK + (th + 1) * 512],
                            start=(fc == 0), stop=(fc == 15),
                            reads=[B(f"wu{i2}"), bxn], writes=[bA4[su]])
                self.act(gi[:, th * 512:(th + 1) * 512], self.psA4[sg], AF.Silu,
                         [bA4[sg]], [B("gi")])
                self.copy("v", ui[:, th * 512:(th + 1) * 512], self.psA4[su],
                          [bA4[su]], [B("ui")])
                self.tt("v", inter[:, ic * TOK + th * 512: ic * TOK + (th + 1) * 512],
                        gi[:, th * 512:(th + 1) * 512], ui[:, th * 512:(th + 1) * 512],
                        ALU.mult, [B("gi"), B("ui")], [B("inter")])
        if not first:
            rres = [self.sb("rres0_f", [128, 1024], F32, st=st)] * 2
        for hc in range(16):
            h2 = hc % 2
            for icg in range(4):
                i2 = icg % 2
                wd3 = wg[i2].rearrange("p (f c) -> p f c", f=16)
                self.dma("y" if icg % 2 == 0 else "g", wd3,
                         wtile_ap(2 * TPM + hc * 64 + icg * 16),
                         reads=[B("ccw_out")], writes=[B(f"wg{i2}")])
                for i16 in range(16):
                    ic = icg * 16 + i16
                    lhsT = wd3[:, i16:i16 + 1, :].squeeze(1)
                    for th in range(2):
                        self.mm(self.psA_h[h2][:, th * 512:(th + 1) * 512], lhsT,
                                inter[:, ic * TOK + th * 512: ic * TOK + (th + 1) * 512],
                                start=(ic == 0), stop=(ic == 63),
                                reads=[B(f"wg{i2}"), B("inter")], writes=[self.bA[h2]])
            self.copy("s", self.scrA[:, 0:1024], self.psA_h[h2], [self.bA[h2]], [B("scrA")])
            for tq in range(2):
                items = [(self.scrA[:, (tq * 4 + k) * 128:(tq * 4 + k + 1) * 128],
                          [B("scrA")]) for k in range(4)]
                self.tgroup(items, self.scrB[:, tq * 512:(tq + 1) * 512], B("scrB"),
                            "v", dtype="f")
            if first:
                dst = bass_rust.AP(self.contrib.tensor, hc * 128,
                                   [[D, 128], [128 * D, 8], [1, 128]])
                self.dma("g", dst,
                         self.scrB[:, 0:1024].rearrange("p (t f) -> p t f", f=128),
                         reads=[B("scrB")], writes=[B("contrib")])
            else:
                rap = bass_rust.AP(self.resid.tensor, hc * 128,
                                   [[D, 128], [128 * D, 8], [1, 128]])
                self.dma("g", rres[h2].rearrange("p (t f) -> p t f", f=128), rap,
                         reads=[B("resid")], writes=[B("rres0")])
                self.tt("v", self.scrB[:, 0:1024], self.scrB[:, 0:1024], rres[h2],
                        ALU.add, [B("scrB"), B("rres0")], [B("scrB")])
                dst = bass_rust.AP(self.out_d.tensor, hc * 128,
                                   [[D, 128], [128 * D, 8], [1, 128]])
                self.dma("g", dst,
                         self.scrB[:, 0:1024].rearrange("p (t f) -> p t f", f=128),
                         reads=[B("scrB")], writes=[B("out")])
        st.close()
        st_xn.close()

    def sub3(self):
        B = self.B
        st0 = ExitStack()
        Q2 = self.sb("Q2", [128, 16 * TOK], BF16, st=st0)
        self.xattn = self.sb("xattn3", [128, 16 * TOK], BF16, st=st0)
        self.scrC = self.sb("scrC3", [128, D], F32, st=st0)
        self.wst = [self.sb("wstr3a", [128, 16 * 256], BF16, st=st0),
                    self.sb("wstr3b", [128, 16 * 256], BF16, st=st0)]
        self.bufs.pop("xattn", None)
        st = ExitStack()
        sbl = lambda n, s, d=F32: self.sb(n, s, d, st=st)
        kown = sbl("kown", [128, 8 * TOK], BF16)
        vown = sbl("vown", [128, 8 * TOK], BF16)
        cosq = sbl("cosq3", [128, 1024])
        sinq = sbl("sinq3", [128, 1024])
        cosk3 = sbl("cosk3", [128, 1024])
        sink3 = sbl("sink3", [128, 1024])
        st_xn = ExitStack()
        xn = self.sb("xn3", [128, 16 * TOK], BF16, st=st_xn)
        bxn = B("xn3")
        self.layernorm(self.resid, B("resid"), xn, bxn, combine=True)
        self.dma("y", cosq, self.cos_bq, writes=[B("ropeq3")])
        self.dma("y", sinq, self.sin_bq, writes=[B("ropeq3")])
        self.dma("y", cosk3, self.cos_bk, writes=[B("ropek3")])
        self.dma("y", sink3, self.sin_bk, writes=[B("ropek3")])
        rope3 = (self.piTb_s, B("piTb"), cosq, sinq, B("ropeq3"))
        rope3k = (self.piTb_s, B("piTb"), cosk3, sink3, B("ropek3"))
        self.proj(xn, bxn, self.w_pack, 2048, 8, kown, B("kown"), 0, rope=rope3k)

        def vhook(grp):
            for kvh in (2 * grp, 2 * grp + 1):
                self.dma("y", self.cc2_in[kvh][:, 0:1024],
                         kown[:, kvh * TOK:(kvh + 1) * TOK],
                         reads=[B("kown")], writes=[B(f"cc2_in{kvh}")])
                self.dma("y", self.cc2_in[kvh][:, 1024:2048],
                         vown[:, kvh * TOK:(kvh + 1) * TOK],
                         reads=[B("vown")], writes=[B(f"cc2_in{kvh}")])
                self.op("g", lambda kvh=kvh: self.g.collective_compute(
                    "AllGather", ALU.bypass, replica_groups=[list(range(NCORES))],
                    ins=[self.cc2_in[kvh].opt()], outs=[self.cc2_out[kvh].opt()]),
                    reads=[B(f"cc2_in{kvh}")], writes=[B(f"cc2_out{kvh}_src")],
                    sem="ag_2", n=1)

        self.proj(xn, bxn, self.w_pack, 3072, 8, vown, B("vown"), 0, hook=vhook)
        self.proj(xn, bxn, self.w_pack, 0, 16, Q2, B("Q2"), 0, rope=rope3)
        st_xn.close()
        self.cc_barrier("cc2x")
        for kvh in range(NKV):
            B(f"cc2_out{kvh}").wtok = B("cc2x").wtok
        st.close()

        st = ExitStack()
        sbl = lambda n, s, d=F32: self.sb(n, s, d, st=st)
        kc = sbl("kc", [128, NKV * 2048], BF16)
        vtok = sbl("vtok", [128, NKV * 2048], BF16)
        kprev = sbl("kprev", [128, NKV], F32)
        vprev = sbl("vprev", [128, NKV], F32)
        cscr = sbl("cscr", [128, 1024], BF16)
        kraw = [sbl(f"kraw{i}", [128, 1024], BF16) for i in range(2)]
        vraw = [sbl(f"vraw{i}", [128, 1024], BF16) for i in range(2)]
        maskb = sbl("maskb", [128, 2048], BF16)
        Pb2 = [sbl(f"Pb{sx}", [128, 2048], BF16) for sx in range(2)]
        PTb2 = [sbl(f"PTb{sx}", [128, 2048], BF16) for sx in range(2)]
        outq32 = [sbl(f"outq3{sx}", [128, 2 * 128]) for sx in range(2)]
        cw = sbl("cw_exp", [128, 32])
        bq, bc = B("qscr"), B("cscr")
        for kvh in range(NKV):
            self.dma("y", cw[:, kvh * 2:(kvh + 1) * 2],
                     self.conv_k[kvh * 128:(kvh + 1) * 128, :], writes=[B("cwexp")])
            self.dma("y", cw[:, 16 + kvh * 2:16 + (kvh + 1) * 2],
                     self.conv_v[kvh * 128:(kvh + 1) * 128, :], writes=[B("cwexp")])

        def conv_unit(qt, kvh):
            half = qt % 2
            for which, co in (("k", 0), ("v", 16)):
                colofs = 0 if which == "k" else 1024
                ri = kvh % 2
                raw = (kraw if which == "k" else vraw)[ri]
                rb = B(("krawb" if which == "k" else "vrawb") + str(ri))
                src = bass_rust.AP(self.cc2_out[kvh].tensor,
                                   colofs + qt * 128,
                                   [[2048, 128], [128 * 2048, 8], [1, 128]])
                self.dma("y", raw.rearrange("p (m f) -> p m f", f=128), src,
                         reads=[B(f"cc2_out{kvh}")], writes=[rb])
                pv = (kprev if which == "k" else vprev)[:, kvh:kvh + 1]
                pvb = B("kprev" if which == "k" else "vprev")
                if which == "k":
                    dst = kc[:, kvh * 2048 + half * 1024:
                             kvh * 2048 + half * 1024 + 1024]
                    db = B("kc")
                else:
                    dst = cscr
                    db = bc
                self.ts("v", dst, raw, cw[:, co + kvh * 2 + 1:co + kvh * 2 + 2], 0.0,
                        ALU.mult, ALU.add, [rb, B("cwexp")], [db])
                self.stt("v", dst[:, 1:1024], raw[:, 0:1023],
                         cw[:, co + kvh * 2:co + kvh * 2 + 1], dst[:, 1:1024],
                         ALU.mult, ALU.add, [rb, B("cwexp"), db], [db])
                if qt > 0:
                    self.stt("v", dst[:, 0:1], pv,
                             cw[:, co + kvh * 2:co + kvh * 2 + 1], dst[:, 0:1],
                             ALU.mult, ALU.add, [pvb, B("cwexp"), db], [db])
                self.copy("v", pv, raw[:, 1023:1024], [rb], [pvb])
                if which == "v":
                    for m4 in range(2):
                        items = [(cscr[:, (m4 * 4 + k) * 128:(m4 * 4 + k + 1) * 128],
                                  [bc]) for k in range(4)]
                        self.tgroup(items,
                                    vtok[:, kvh * 2048 + half * 1024 + m4 * 512:
                                         kvh * 2048 + half * 1024 + (m4 + 1) * 512],
                                    B("vtok"), "v")

        for qt in range(8):
            conv_unit(qt, 0)
            conv_unit(qt, 1)
            self.dma("y", maskb, self.mask_b_d[qt * 128:(qt + 1) * 128, :],
                     writes=[B("maskb")])
            kbase_q = qt - 1 if qt >= 1 else 0
            E = 2048 if qt >= 1 else 1024
            for hp in range(8):
                kvh = hp
                cs = []
                for sx in range(2):
                    h = hp * 2 + sx
                    cs.append(dict(
                        qt=qt, E=E, par=(hp % 2) * 2 + sx, slot=sx,
                        kcol_of=lambda Q, kvh=kvh: kvh * 2048
                        + ((kbase_q + Q) % 2) * 1024,
                        Q_sb=Q2, Q_buf=B("Q2"), q_col=h * TOK + qt * 128,
                        krot_sb=kc, krot_buf=B("kc"),
                        V_sb=vtok, V_buf=B("vtok"),
                        vcol_of=lambda kt, kvh=kvh: kvh * 2048
                        + ((kbase_q + kt // 8) % 2) * 1024 + (kt % 8) * 128,
                        mask=(0, 2048, maskb, B("maskb")),
                        P=Pb2[sx], Pn=f"Pb{sx}", Pslots=2, PT=PTb2[sx], PTn=f"PTb{sx}",
                        outq=outq32[sx], outq_n=f"outq3{sx}",
                        xa_ap=self.xattn[:, h * TOK + qt * 128:
                                         h * TOK + (qt + 1) * 128]))
                self.attn_qt(cs)
                if hp + 2 < NKV:
                    conv_unit(qt, hp + 2)
                if qt == 7:
                    self.xattn_tm_to_fm([hp * 2, hp * 2 + 1])
        st.close()
        self.out_proj(self.w_o, None, self.resid, B("resid"))
        st0.close()

    def finish(self):
        B = self.B
        ob = B("out")
        toks = ([ob.wtok] if ob.wtok else []) + ob.rtoks
        if DBG_CCW:
            db = B("dbg1")
            toks += ([db.wtok] if db.wtok else [])
        self.wait("y", toks)


def kernel(L0_attn_a_x, L0_attn_a_W_latent, L0_attn_a_b_latent, L0_attn_a_W_d,
           L0_attn_a_b_d, L0_attn_a_W_proj, L0_attn_a_b_proj, L0_attn_b_W_pack,
           L0_attn_b_W_o, L0_attn_b_conv_k, L0_attn_b_conv_v, L0_ffn_gate_cp,
           L0_ffn_up_cp, L0_ffn_down_cp):
    x = np.ascontiguousarray(np.asarray(L0_attn_a_x, np.float32).reshape(T, D))
    cos_a, sin_a, piT_a = make_rope_a()
    cos_b, sin_b, piT_b = make_rope_b()
    ident_f = np.eye(128, dtype=np.float32)
    ident_b = np.eye(128, dtype=np.float32).astype(BFNP)
    f0c = np.broadcast_to(
        (np.arange(128, dtype=np.float64) * (NCTRL - 1.0) / (NSAMP - 1.0)).astype(np.float32),
        (128, 128)).copy()
    cps = [np.asarray(L0_ffn_gate_cp, np.float32),
           np.asarray(L0_ffn_up_cp, np.float32),
           np.asarray(L0_ffn_down_cp, np.float32)]
    conv_k = np.ascontiguousarray(
        np.repeat(np.asarray(L0_attn_b_conv_k, np.float32)[:, None, :], 128, axis=1)
        .reshape(NKV * 128, 2))
    conv_v = np.ascontiguousarray(
        np.repeat(np.asarray(L0_attn_b_conv_v, np.float32)[:, None, :], 128, axis=1)
        .reshape(NKV * 128, 2))
    b_lat = np.ascontiguousarray(np.asarray(L0_attn_a_b_latent, np.float32).reshape(LC, 1))
    b_d = np.ascontiguousarray(np.asarray(L0_attn_a_b_d, np.float32).reshape(NH, 128).T)
    b_pr = np.ascontiguousarray(np.asarray(L0_attn_a_b_proj, np.float32).reshape(16, 128).T)

    in_maps = []
    for core in range(NCORES):
        rows = np.array([gpos(core, r) for r in range(TOK)])
        cp6_dev, phi_dev = spline_core_tables(core, cps)
        in_maps.append({
            "x": np.ascontiguousarray(x[rows]),
            "w_latent": np.asarray(L0_attn_a_W_latent, np.float32),
            "w_d": np.asarray(L0_attn_a_W_d, np.float32),
            "w_proj": np.asarray(L0_attn_a_W_proj, np.float32),
            "w_pack": np.asarray(L0_attn_b_W_pack, np.float32),
            "w_o": np.asarray(L0_attn_b_W_o, np.float32),
            "b_latent": b_lat, "b_d": b_d, "b_proj": b_pr,
            "conv_k": conv_k, "conv_v": conv_v,
            "cp6": cp6_dev, "phi": phi_dev, "f0c": f0c,
            "cos_a": cos_a, "sin_a": sin_a,
            "cos_aq": np.ascontiguousarray(cos_a[:, rows] * SCALE),
            "sin_aq": np.ascontiguousarray(sin_a[:, rows] * SCALE),
            "cos_bq": np.ascontiguousarray(cos_b[:, rows] * SCALE),
            "sin_bq": np.ascontiguousarray(sin_b[:, rows] * SCALE),
            "cos_bk": np.ascontiguousarray(cos_b[:, rows]),
            "sin_bk": np.ascontiguousarray(sin_b[:, rows]),
            "piT_a": piT_a, "piT_b": piT_b,
            "ident_f": ident_f, "ident_b": ident_b,
            "mask_a": make_mask_a(core), "mask_b": make_masks_b(core),
        })
    prog = Prog()
    nc = prog.build()
    res = run_bass_kernel_spmd(nc, in_maps, core_ids=list(range(NCORES)))
    global _LAST_RES
    _LAST_RES = res
    out = np.zeros((T, D), np.float32)
    for core in range(NCORES):
        o = np.asarray(res.results[core]["out"])
        rows = np.array([gpos(core, r) for r in range(TOK)])
        out[rows] = o
    try:
        prog.stack.close()
    except Exception:
        pass
    return out.reshape(1, T, D)


# version 28
# speedup vs baseline: 1.0066x; 1.0066x over previous
import numpy as np
import ml_dtypes
import bass_rust
import concourse.bass as bass
import concourse.mybir as mybir
from concourse.bass_utils import run_bass_kernel_spmd
from contextlib import ExitStack

F32 = mybir.dt.float32
BF16 = mybir.dt.bfloat16
AF = mybir.ActivationFunctionType
ALU = mybir.AluOpType
AX = mybir.AxisListType
BFNP = ml_dtypes.bfloat16

NCORES = 8
T = 8192
D = 2048
TOK = 1024
NH = 16
LC = 128
NKV = 8
HD = 128
WINDOW = 1024
INTER = 8192
NCTRL = 130156
NSAMP = INTER * D
RUN = 128
NRUNS = NSAMP // RUN
TPM = NSAMP // (128 * 128)
NTILES = 3 * TPM
TPC = NTILES // NCORES
GEN_BATCH = 96
NEG = -1.0e9
SCALE = float(1.0 / np.sqrt(128.0))
EPS = 1e-5

STOP_AFTER = 4
DBG_CCW = False
_LAST_RES = None
ENGMAP = {"t": "tensor", "v": "vector", "s": "scalar", "g": "gpsimd", "y": "sync"}


def gpos(core, row):
    return 1024 * (row // 128) + 128 * core + (row % 128)


def grow(p):
    return ((p % 1024) // 128) * 1024 + (p // 1024) * 128 + (p % 128)


def make_rope_a():
    half = LC // 2
    inv = 10000.0 ** (-np.arange(half, dtype=np.float64) / half)
    freq = np.repeat(inv, 2)
    ang = freq[:, None] * np.arange(T, dtype=np.float64)[None, :]
    cos = np.cos(ang).astype(np.float32)
    sin = np.sin(ang).astype(np.float32)
    Pi = np.zeros((LC, LC), np.float32)
    for i in range(half):
        Pi[2 * i, 2 * i + 1] = -1.0
        Pi[2 * i + 1, 2 * i] = 1.0
    return cos, sin, np.ascontiguousarray(Pi.T)


def make_rope_b():
    half = HD // 2
    inv = 1.0 / (100000.0 ** (np.arange(0, HD, 2, dtype=np.float64) / HD))
    pos = np.arange(T, dtype=np.float64)
    cos = np.zeros((HD, T), np.float64)
    ssin = np.zeros((HD, T), np.float64)
    for d in range(HD):
        f = inv[d % half]
        cos[d] = np.cos(pos * f)
        ssin[d] = np.sin(pos * f) * (-1.0 if d < half else 1.0)
    Pi2 = np.zeros((HD, HD), np.float32)
    for d in range(half):
        Pi2[d, d + half] = 1.0
        Pi2[d + half, d] = 1.0
    return cos.astype(np.float32), ssin.astype(np.float32), np.ascontiguousarray(Pi2.T)


_SPL = {}


def spline_meta():
    if "m" not in _SPL:
        t = np.linspace(0.0, NCTRL - 1.0, NSAMP, dtype=np.float64)
        n0 = np.arange(NRUNS, dtype=np.int64) * RUN
        t0 = t[n0]
        i0 = np.clip(np.floor(t0).astype(np.int64), 0, NCTRL - 2)
        phi = (t0 - i0).astype(np.float32)
        p = np.arange(128)
        runmap = np.zeros((NTILES, 128), np.int64)
        for tt in range(TPM):
            oc, hb = tt // 16, tt % 16
            runmap[tt] = (128 * oc + p) * 16 + hb
            runmap[TPM + tt] = (128 * oc + p) * 16 + hb
        for tt in range(TPM):
            hc, ic = tt // 64, tt % 64
            runmap[2 * TPM + tt] = (128 * hc + p) * 64 + ic
        _SPL["m"] = (i0, phi, runmap)
    return _SPL["m"]


def spline_core_tables(core, cps):
    i0, phi, runmap = spline_meta()
    lo = core * TPC
    cp6 = np.zeros((TPC, 128, 8), np.float32)
    phv = np.zeros((TPC, 128), np.float32)
    for j in range(TPC):
        gt = lo + j
        runs = runmap[gt]
        ii = i0[runs]
        idx = np.clip(ii[:, None] - 1 + np.arange(8)[None, :], 0, NCTRL - 1)
        cp6[j] = cps[gt // TPM][idx]
        phv[j] = phi[runs]
    cp6_dev = np.ascontiguousarray(cp6.transpose(1, 0, 2).reshape(128, TPC * 8))
    phi_dev = np.ascontiguousarray(phv.T)
    return cp6_dev, phi_dev


def make_mask_a(core):
    r = np.arange(128)[:, None]
    j = np.arange(1024)[None, :]
    return np.where(j <= 128 * core + r, 0.0, NEG).astype(BFNP)


def make_masks_b(core):
    m = np.zeros((8, 128, 2048), np.float32)
    for qt in range(8):
        kbase = 1024 * (qt - 1) if qt >= 1 else 0
        gq = 1024 * qt + 128 * core
        r = np.arange(128)[:, None]
        j = np.arange(2048)[None, :]
        k = kbase + j
        q = gq + r
        m[qt] = np.where((k <= q) & (k > q - WINDOW), 0.0, NEG)
    return np.ascontiguousarray(m.reshape(8 * 128, 2048)).astype(BFNP)


class Buf:
    __slots__ = ("wtok", "rtoks", "name")

    def __init__(self, name=""):
        self.wtok = None
        self.rtoks = []
        self.name = name


class Prog:
    def __init__(self):
        self.nc = bass.Bass("TRN2", target_bir_lowering=False)
        self.stack = ExitStack()
        self._sems = {}
        self._waited = {}
        self.bufs = {}

    def B(self, name):
        if name not in self.bufs:
            self.bufs[name] = Buf(name)
        return self.bufs[name]

    def sem(self, name):
        if name not in self._sems:
            h = self.stack.enter_context(self.nc.semaphore(name))
            self._sems[name] = [h, 0]
        return self._sems[name]

    def wait(self, engname, toks):
        eng = getattr(self.nc, ENGMAP[engname])
        best = {}
        for (name, v) in toks:
            if best.get(name, -1) < v:
                best[name] = v
        for name, v in best.items():
            key = (engname, name)
            if self._waited.get(key, 0) >= v:
                continue
            eng.wait_ge(self.sem(name)[0], v)
            self._waited[key] = v

    def op(self, engname, emit, reads=(), writes=(), sem=None, n=1):
        toks = []
        for b in reads:
            if b.wtok is not None:
                toks.append(b.wtok)
        for b in writes:
            if b.wtok is not None:
                toks.append(b.wtok)
            toks.extend(b.rtoks)
        if engname == "t":
            toks = [t for t in toks if t[0] != "c_t"]
        self.wait(engname, toks)
        instr = emit()
        se = self.sem(sem or ("c_" + engname))
        se[1] += n
        instr.then_inc(se[0], n)
        tok = (sem or ("c_" + engname), se[1])
        for b in reads:
            b.rtoks.append(tok)
            if len(b.rtoks) > 12:
                best = {}
                for (nm, v) in b.rtoks:
                    if best.get(nm, -1) < v:
                        best[nm] = v
                b.rtoks = list(best.items())
        for b in writes:
            b.wtok = tok
            b.rtoks = []
        return tok

    def dma(self, q, out, in_, reads=(), writes=(), **kw):
        eng = getattr(self.nc, ENGMAP[q])
        return self.op(q, lambda: eng.dma_start(out=out, in_=in_, **kw),
                       reads=reads, writes=writes, sem="d_" + q, n=16)

    def din(self, name, shape, dt=F32):
        return self.nc.dram_tensor(name, shape, dt, kind="ExternalInput").ap()

    def sb(self, name, shape, dt=F32, st=None):
        return (st or self.stack).enter_context(self.nc.sbuf_tensor(name, shape, dt))[:]

    def ts(self, e, out, in0, s1, s2, op0, op1, reads, writes, accum_out=None):
        eng = getattr(self.nc, ENGMAP[e])
        return self.op(e, lambda: eng.tensor_scalar(out, in0, s1, s2, op0=op0, op1=op1,
                                                    accum_out=accum_out), reads, writes)

    def tt(self, e, out, in0, in1, alu, reads, writes):
        eng = getattr(self.nc, ENGMAP[e])
        return self.op(e, lambda: eng.tensor_tensor(out, in0, in1, op=alu), reads, writes)

    def stt(self, e, out, in0, scalar, in1, op0, op1, reads, writes):
        eng = getattr(self.nc, ENGMAP[e])
        return self.op(e, lambda: eng.scalar_tensor_tensor(out, in0, scalar, in1,
                                                           op0=op0, op1=op1), reads, writes)

    def act(self, out, in_, func, reads, writes, **kw):
        return self.op("s", lambda: self.s.activation(out, in_, func, **kw), reads, writes)

    def copy(self, e, out, in_, reads, writes):
        if e == "s":
            return self.op("s", lambda: self.s.copy(out, in_), reads, writes)
        eng = getattr(self.nc, ENGMAP[e])
        return self.op(e, lambda: eng.tensor_copy(out, in_), reads, writes)

    def mm(self, out, lhsT, rhs, start, stop, reads, writes):
        return self.op("t", lambda: self.t.matmul(out, lhsT, rhs, start=start, stop=stop),
                       reads, writes)

    def reduce(self, out, in_, alu, reads, writes, negate=False):
        return self.op("v", lambda: self.v.tensor_reduce(out, in_, axis=AX.X, op=alu,
                                                         negate=negate), reads, writes)

    def tgroup(self, items, dst_ap, dst_buf, evac_eng="s", dtype="b"):
        n = len(items)
        if dtype == "b":
            h = self._tg_flip = 1 - getattr(self, "_tg_flip", 1)
            ps = self.psT_h[h]
            pbuf = self.bT[h]
            ident, ib = self.identb, self.B("identb")
        else:
            ps = self.psB_b
            pbuf = self.bBb
            ident, ib = self.identf, self.B("identf")
        for i, (ap, rb) in enumerate(items):
            self.op("t", lambda ap=ap, i=i: self.t.transpose(
                ps[:, i * 128:(i + 1) * 128], ap, ident),
                reads=list(rb) + [ib], writes=[pbuf])
        self.copy(evac_eng, dst_ap, ps[:, 0:n * 128], [pbuf], [dst_buf])

    def tsingle(self, in_ap, in_bufs, dst_ap, dst_buf, evac_eng="s", dtype="b"):
        self.tgroup([(in_ap, in_bufs)], dst_ap, dst_buf, evac_eng, dtype)

    def build(self):
        nc = self.nc
        self.t, self.v, self.s, self.g, self.y = (nc.tensor, nc.vector, nc.scalar,
                                                  nc.gpsimd, nc.sync)
        B = self.B

        self.x_in = self.din("x", [TOK, D])
        self.w_latent = self.din("w_latent", [D, LC])
        self.w_d = self.din("w_d", [D, NH * LC])
        self.w_proj = self.din("w_proj", [NH * LC, D])
        self.w_pack = self.din("w_pack", [D, 4096])
        self.w_o = self.din("w_o", [NH * HD, D])
        self.b_latent = self.din("b_latent", [128, 1])
        self.b_d = self.din("b_d", [128, NH])
        self.b_proj = self.din("b_proj", [128, 16])
        self.conv_k = self.din("conv_k", [NKV * 128, 2])
        self.conv_v = self.din("conv_v", [NKV * 128, 2])
        self.cp6_d = self.din("cp6", [128, TPC * 8])
        self.phi_d = self.din("phi", [128, TPC])
        self.f0c_d = self.din("f0c", [128, 128])
        self.cos_a = self.din("cos_a", [128, T])
        self.sin_a = self.din("sin_a", [128, T])
        self.cos_aq = self.din("cos_aq", [128, TOK])
        self.sin_aq = self.din("sin_aq", [128, TOK])
        self.cos_bq = self.din("cos_bq", [128, TOK])
        self.sin_bq = self.din("sin_bq", [128, TOK])
        self.cos_bk = self.din("cos_bk", [128, TOK])
        self.sin_bk = self.din("sin_bk", [128, TOK])
        self.piT_a = self.din("piT_a", [128, 128])
        self.piT_b = self.din("piT_b", [128, 128])
        self.ident_fd = self.din("ident_f", [128, 128])
        self.ident_bd = self.din("ident_b", [128, 128], BF16)
        self.mask_a_d = self.din("mask_a", [128, 1024], BF16)
        self.mask_b_d = self.din("mask_b", [8 * 128, 2048], BF16)

        self.out_d = nc.dram_tensor("out", [TOK, D], F32, kind="ExternalOutput").ap()
        self.resid = nc.dram_tensor("resid", [TOK, D], F32).ap()
        self.contrib = nc.dram_tensor("contrib", [TOK, D], F32).ap()
        self.cc1_in = nc.dram_tensor("cc1_in", [TOK, LC], F32).ap()
        self.cc1_out = nc.dram_tensor("cc1_out", [T, LC], F32, addr_space="Shared").ap()
        self.cc1f_in = nc.dram_tensor("cc1f_in", [128, TOK], F32).ap()
        self.cc1f_out = nc.dram_tensor("cc1f_out", [8 * 128, TOK], F32).ap()
        self.cc2_in = [nc.dram_tensor(f"cc2_in{k}", [128, 2048], BF16).ap()
                       for k in range(NKV)]
        self.cc2_out = [nc.dram_tensor(f"cc2_out{k}", [8 * 128, 2048], BF16).ap()
                        for k in range(NKV)]
        self.ccw_in = nc.dram_tensor("ccw_in", [TPC * 128, 128], BF16).ap()
        self.ccw_out = nc.dram_tensor("ccw_out", [NTILES * 128, 128], BF16, addr_space="Shared").ap()
        if DBG_CCW:
            self.dbg1 = nc.dram_tensor("dbg1", [TPC * 128, 128], BF16,
                                       kind="ExternalOutput").ap()
        self.bar_in = nc.dram_tensor("bar_in", [1, 128], F32).ap()
        self.bar_out = nc.dram_tensor("bar_out", [1, 128], F32).ap()

        sb = self.sb
        self.identb = sb("identb", [128, 128], BF16)
        self.identf = sb("identf", [128, 128], F32)
        self.piTa_s = sb("piTa_s", [128, 128], F32)
        self.piTb_s = sb("piTb_s", [128, 128], F32)
        self.st1 = sb("st1", [128, 64], F32)
        self.ast = sb("ast", [128, 128], F32)
        self.scrA = sb("scrA", [128, D], F32)
        self.scrB = sb("scrB", [128, D], F32)
        self.xntok = sb("xntok", [128, D], BF16)
        self.bias_sb = sb("bias_sb", [128, 33], F32)

        psA = self.stack.enter_context(nc.psum_tensor("psA", [128, 2048], F32))[:]
        psB = self.stack.enter_context(nc.psum_tensor("psB", [128, 1024], F32))[:]
        psT = self.stack.enter_context(nc.psum_tensor("psT", [128, 2048], BF16))[:]
        self.psA_h = [psA[:, 0:1024], psA[:, 1024:2048]]
        self.psA4 = [psA[:, k * 512:(k + 1) * 512] for k in range(4)]
        self.psB = psB
        self.psB_a = psB[:, 0:512]
        self.psB_b = psB[:, 512:1024]
        self.psT_h = [psT[:, 0:1024], psT[:, 1024:2048]]
        self.bA = [B("psA0"), B("psA1")]
        self.bBa, self.bBb = B("psB_a"), B("psB_b")
        self.bT = [B("psT0"), B("psT1")]

        self.dma("y", self.identb, self.ident_bd, writes=[B("identb")])
        self.dma("y", self.identf, self.ident_fd, writes=[B("identf")])
        self.dma("y", self.piTa_s, self.piT_a, writes=[B("piTa")])
        self.dma("y", self.piTb_s, self.piT_b, writes=[B("piTb")])
        self.dma("y", self.bias_sb[:, 0:1], self.b_latent, writes=[B("bias")])
        self.dma("y", self.bias_sb[:, 1:17], self.b_d, writes=[B("bias")])
        self.dma("y", self.bias_sb[:, 17:33], self.b_proj, writes=[B("bias")])

        self.op("v", lambda: self.v.memset(self.st1[:, 63:64], EPS), [], [B("lnst")])
        self._spline_iter = self.spline_generator()
        self._spline_done = False
        self.pump_spline(1)

        self.sub1()
        if STOP_AFTER >= 2:
            self.pump_spline(1 << 30)
            self._spline_stack.close()
            self.sub_ffn(first=True)
        if STOP_AFTER >= 3:
            self.sub3()
        if STOP_AFTER >= 4:
            self.sub_ffn(first=False)
        self.finish()
        return nc

    def spline_generator(self):
        st = ExitStack()
        self._spline_stack = st
        B = self.B
        sbl = lambda n, s, d=F32: self.sb(n, s, d, st=st)
        cp6 = sbl("sp_cp6", [128, GEN_BATCH * 8])
        phi = sbl("sp_phi", [128, TPC])
        f0c = sbl("sp_f0c", [128, 128])
        co = sbl("sp_co", [128, GEN_BATCH * 12])
        NDEP = 3
        gm = [sbl(f"sp_m{i}", [128, 128]) for i in range(NDEP)]
        gf = [sbl(f"sp_f{i}", [128, 128]) for i in range(NDEP)]
        gf2 = [sbl(f"sp_f2{i}", [128, 128]) for i in range(NDEP)]
        gf3 = [sbl(f"sp_f3{i}", [128, 128]) for i in range(NDEP)]
        gu = [sbl(f"sp_u{i}", [128, 128]) for i in range(NDEP)]
        gv = [sbl(f"sp_v{i}", [128, 128]) for i in range(NDEP)]
        gw = [sbl(f"sp_w{i}", [128, 128], BF16) for i in range(NDEP)]
        gtmp = sbl("sp_tmp", [128, GEN_BATCH])
        gstage = sbl("sp_stage", [128, 1024], BF16)
        bco, bphi, bf0, bcp = B("sp_co"), B("sp_phi"), B("sp_f0c"), B("sp_cp6")
        btmp, bstage = B("sp_tmp"), B("sp_stage")
        self.dma("y", phi, self.phi_d, writes=[bphi])
        self.dma("y", f0c, self.f0c_d, writes=[bf0])
        cp3 = cp6.rearrange("p (t e) -> p t e", e=8)
        co3 = co.rearrange("p (k t) -> p k t", k=12)
        gt1 = gtmp.unsqueeze(1)

        def pk(q, k):
            return cp3[:, :, q + k:q + k + 1].transpose([0, 2, 1])

        NB = GEN_BATCH
        flush_prev = [lambda: None]
        for batch in range(TPC // NB):
            t0 = batch * NB
            self.dma("y", cp6, self.cp6_d[:, t0 * 8:(t0 + NB) * 8], writes=[bcp])
            for q in (0, 1):
                o = q * 4
                A = co3[:, o + 0:o + 1, :]
                Bc = co3[:, o + 1:o + 2, :]
                C = co3[:, o + 2:o + 3, :]
                Dc = co3[:, o + 3:o + 4, :]
                self.copy("v", A, pk(q, 1), [bcp], [bco])
                self.tt("v", gt1, pk(q, 2), pk(q, 0), ALU.subtract, [bcp], [btmp])
                self.ts("v", Bc, gt1, 0.5, 0.0, ALU.mult, ALU.add, [btmp], [bco])
                self.stt("v", gt1, pk(q, 1), -2.5, pk(q, 0), ALU.mult, ALU.add, [bcp], [btmp])
                self.stt("v", gt1, pk(q, 2), 2.0, gt1, ALU.mult, ALU.add, [bcp, btmp], [btmp])
                self.stt("v", C, pk(q, 3), -0.5, gt1, ALU.mult, ALU.add, [bcp, btmp], [bco])
                self.stt("v", gt1, pk(q, 1), 3.0, pk(q, 0), ALU.mult, ALU.subtract, [bcp], [btmp])
                self.stt("v", gt1, pk(q, 2), -3.0, gt1, ALU.mult, ALU.add, [bcp, btmp], [btmp])
                self.tt("v", gt1, gt1, pk(q, 3), ALU.add, [bcp, btmp], [btmp])
                self.ts("v", Dc, gt1, 0.5, 0.0, ALU.mult, ALU.add, [btmp], [bco])
            for k in range(4):
                self.tt("v", co3[:, 8 + k:9 + k, :], co3[:, 4 + k:5 + k, :],
                        co3[:, k:k + 1, :], ALU.subtract, [bco], [bco])
            if batch >= 1:
                flush_prev[0]()
            yield
            bm = [B(f"sp_bm{i}") for i in range(NDEP)]
            bfb = [B(f"sp_bf{i}") for i in range(NDEP)]
            bf2 = [B(f"sp_bf2{i}") for i in range(NDEP)]
            bf3 = [B(f"sp_bf3{i}") for i in range(NDEP)]
            bu = [B(f"sp_bu{i}") for i in range(NDEP)]
            bv = [B(f"sp_bv{i}") for i in range(NDEP)]
            bw = [B(f"sp_bw{i}") for i in range(NDEP)]
            def flush_tile(j, t0=t0):
                tloc = t0 + j
                i = j % NDEP
                gslot = j % 8
                self.tsingle(gw[i], [bw[i]], gstage[:, gslot * 128:(gslot + 1) * 128],
                             bstage, evac_eng="v")
                if gslot == 7:
                    g0 = tloc - 7
                    dst = bass_rust.AP(self.ccw_in.tensor, g0 * 128 * 128,
                                       [[128, 128], [128 * 128, 8], [1, 128]])
                    self.dma("g", dst, gstage.rearrange("p (t f) -> p t f", f=128),
                             reads=[bstage], writes=[B("ccw_in")])

            for j in range(NB):
                tloc = t0 + j
                i = j % NDEP
                ph = phi[:, tloc:tloc + 1]

                def cc(k, j=j):
                    return co3[:, k:k + 1, j:j + 1].squeeze(1)

                if j >= 1:
                    flush_tile(j - 1)
                self.ts("v", gm[i], f0c, ph, 1.0, ALU.add, ALU.is_ge, [bf0, bphi], [bm[i]])
                self.stt("v", gf[i], f0c, ph, gm[i], ALU.add, ALU.subtract,
                         [bf0, bphi, bm[i]], [bfb[i]])
                self.tt("g", gf2[i], gf[i], gf[i], ALU.mult, [bfb[i]], [bf2[i]])
                self.tt("g", gf3[i], gf2[i], gf[i], ALU.mult, [bfb[i], bf2[i]], [bf3[i]])
                self.act(gu[i], gf[i], AF.Identity, [bfb[i], bco], [bu[i]],
                         scale=cc(1), bias=cc(0))
                self.stt("v", gu[i], gf2[i], cc(2), gu[i], ALU.mult, ALU.add,
                         [bf2[i], bco, bu[i]], [bu[i]])
                self.stt("v", gu[i], gf3[i], cc(3), gu[i], ALU.mult, ALU.add,
                         [bf3[i], bco, bu[i]], [bu[i]])
                self.act(gv[i], gf[i], AF.Identity, [bfb[i], bco], [bv[i]],
                         scale=cc(9), bias=cc(8))
                self.stt("v", gv[i], gf2[i], cc(10), gv[i], ALU.mult, ALU.add,
                         [bf2[i], bco, bv[i]], [bv[i]])
                self.stt("v", gv[i], gf3[i], cc(11), gv[i], ALU.mult, ALU.add,
                         [bf3[i], bco, bv[i]], [bv[i]])
                self.tt("g", gv[i], gv[i], gm[i], ALU.mult, [bv[i], bm[i]], [bv[i]])
                self.tt("g", gw[i], gu[i], gv[i], ALU.add, [bu[i], bv[i]], [bw[i]])
                yield
            flush_prev[0] = (lambda NBm1=NB - 1: flush_tile(NBm1))
        flush_prev[0]()
        self.op("g", lambda: self.g.collective_compute(
            "AllGather", ALU.bypass, replica_groups=[list(range(NCORES))],
            ins=[self.ccw_in.opt()], outs=[self.ccw_out.opt()]),
            reads=[B("ccw_in")], writes=[B("ccw_out_src")], sem="ag_w", n=1)
        self.cc_barrier("ccw_out")
        if DBG_CCW:
            self.dma("y", self.dbg1, self.ccw_in, reads=[B("ccw_in")],
                     writes=[B("dbg1")])
        self._spline_done = True

    def cc_barrier(self, name):
        self.op("g", lambda: self.g.collective_compute(
            "AllReduce", ALU.add, replica_groups=[list(range(NCORES))],
            ins=[self.bar_in.opt()], outs=[self.bar_out.opt()]),
            reads=[self.B(name + "_src")], writes=[self.B(name)], sem=name + "_b", n=1)

    def pump_spline(self, n):
        if self._spline_done:
            return
        for _ in range(n):
            try:
                next(self._spline_iter)
            except StopIteration:
                self._spline_done = True
                break

    def layernorm(self, src_dram, src_buf, xn, xn_buf, store_to=None, store_buf=None,
                  combine=False):
        B = self.B
        rt = [self.scrA, self.scrB]
        rb = [B("scrA"), B("scrB")]
        stct = ExitStack()
        tag = self._ln_tag = getattr(self, "_ln_tag", 0) + 1
        if combine:
            ct0 = self.sb(f"ctA{tag}", [128, D], F32, st=stct)
            ct = [ct0, ct0]
            cb = [B("ctA"), B("ctA")]
        xnt = [self.xntok[:, 0:1024], self.xntok[:, 1024:2048]]
        bxnt = [B("xntok0"), B("xntok1")]
        for tt in range(8):
            i = tt % 2
            c = self.st1[:, i * 16: i * 16 + 16]
            bst = B(f"lnst{i}")
            self.dma("y", rt[i], src_dram[tt * 128:(tt + 1) * 128, :],
                     reads=[src_buf], writes=[rb[i]])
            if combine:
                self.dma("g", ct[i], self.contrib[tt * 128:(tt + 1) * 128, :],
                         reads=[B("contrib")], writes=[cb[i]])
                self.tt("v", rt[i], rt[i], ct[i], ALU.add, [rb[i], cb[i]], [rb[i]])
                self.dma("y", src_dram[tt * 128:(tt + 1) * 128, :], rt[i],
                         reads=[rb[i]], writes=[src_buf])
            if store_to is not None:
                self.dma("y", store_to[tt * 128:(tt + 1) * 128, :], rt[i],
                         reads=[rb[i]], writes=[store_buf])
            self.act(xnt[i], rt[i][:, 0:1024], AF.Square, [rb[i]], [bxnt[i], bst],
                     accum_out=c[:, 0:1])
            self.act(xnt[i], rt[i][:, 1024:2048], AF.Square, [rb[i]], [bxnt[i], bst],
                     accum_out=c[:, 9:10])
            self.op("s", lambda c=c: self.s.copy(c[:, 10:11], c[:, 9:10]), [bst], [bst])
            self.reduce(c[:, 1:2], rt[i], ALU.add, [rb[i]], [bst])
            self.tt("v", c[:, 0:1], c[:, 0:1], c[:, 10:11], ALU.add, [bst], [bst])
            self.ts("v", c[:, 2:3], c[:, 1:2], 1.0 / D, 0.0, ALU.mult, ALU.add, [bst], [bst])
            self.ts("v", c[:, 3:4], c[:, 0:1], 1.0 / D, 0.0, ALU.mult, ALU.add, [bst], [bst])
            self.tt("v", c[:, 4:5], c[:, 2:3], c[:, 2:3], ALU.mult, [bst], [bst])
            self.tt("v", c[:, 5:6], c[:, 3:4], c[:, 4:5], ALU.subtract, [bst], [bst])
            self.act(c[:, 6:7], c[:, 5:6], AF.Ln, [bst], [bst], bias=self.st1[:, 63:64])
            self.act(c[:, 7:8], c[:, 6:7], AF.Exp, [bst], [bst], scale=-0.5)
            self.stt("v", c[:, 8:9], c[:, 2:3], -1.0, c[:, 7:8], ALU.mult, ALU.mult,
                     [bst], [bst])
            for half in range(2):
                self.act(xnt[i], rt[i][:, half * 1024:(half + 1) * 1024], AF.Identity,
                         [rb[i], bst], [bxnt[i]], scale=c[:, 7:8], bias=c[:, 8:9])
                for fc4 in range(2):
                    h = self._tg_flip = 1 - getattr(self, "_tg_flip", 1)
                    ps = self.psT_h[h]
                    pbuf = self.bT[h]
                    for k in range(4):
                        fc = half * 8 + fc4 * 4 + k
                        ap = xnt[i][:, (fc4 * 4 + k) * 128:(fc4 * 4 + k + 1) * 128]
                        self.op("t", lambda ap=ap, k=k: self.t.transpose(
                            ps[:, k * 128:(k + 1) * 128], ap, self.identb),
                            reads=[bxnt[i], B("identb")], writes=[pbuf])
                    for k in range(4):
                        fc = half * 8 + fc4 * 4 + k
                        dst = xn[:, fc * TOK + tt * 128: fc * TOK + (tt + 1) * 128]
                        self.copy("s" if k % 2 == 0 else "v", dst,
                                  ps[:, k * 128:(k + 1) * 128], [pbuf], [xn_buf])
            self.pump_spline(5)
        stct.close()

    def proj_load(self, w_dram, w_cols0, no):
        OD = w_dram.tensor.shape[1]
        self._wstc = (getattr(self, "_wstc", 0) + 1) % 2
        gw = self.wst[self._wstc]
        gwb = self.B(f"wstream{self._wstc}")
        gw3 = gw.rearrange("p (f c) -> p f c", f=16)
        src = bass_rust.AP(w_dram.tensor, w_cols0,
                           [[OD, 128], [128 * OD, 16], [1, no * 128]])
        self.dma("g", gw3[:, :, 0:no * 128], src, writes=[gwb])
        return gw3, gwb

    def proj(self, xn, xn_buf, w_dram, w_cols0, n_out_chunks, dst_sb, dst_buf, dst_tok,
             bias_col=None, rope=None, pump=0, preloaded=None, hook=None):
        B = self.B
        for grp in range((n_out_chunks + 1) // 2):
            no = min(2, n_out_chunks - grp * 2)
            if preloaded is not None and grp == 0:
                gw3, gwb = preloaded
            else:
                gw3, gwb = self.proj_load(w_dram, w_cols0 + grp * 256, no)
            for o2 in range(no):
                oc = grp * 2 + o2
                ps = self.psA_h[o2]
                pb = self.bA[o2]
                for fc in range(16):
                    lhsT = gw3[:, fc:fc + 1, o2 * 128:(o2 + 1) * 128].squeeze(1)
                    for th in range(2):
                        self.mm(ps[:, th * 512:(th + 1) * 512], lhsT,
                                xn[:, fc * TOK + th * 512: fc * TOK + (th + 1) * 512],
                                start=(fc == 0), stop=(fc == 15),
                                reads=[gwb, xn_buf], writes=[pb])
                dst = dst_sb[:, (dst_tok + oc) * TOK: (dst_tok + oc + 1) * TOK]
                if rope is None:
                    if bias_col is not None:
                        self.act(dst, ps, AF.Identity, [pb, B("bias")], [dst_buf],
                                 bias=self.bias_sb[:, bias_col + oc:bias_col + oc + 1],
                                 scale=1.0)
                    else:
                        self.copy("s", dst, ps, [pb], [dst_buf])
                else:
                    piT_sb, piT_buf, cosq, sinq, tbb = rope
                    qlin = self.scrA[:, 0:1024]
                    qb = B("scrA")
                    if bias_col is not None:
                        self.act(qlin, ps, AF.Identity, [pb, B("bias")], [qb],
                                 bias=self.bias_sb[:, bias_col + oc:bias_col + oc + 1],
                                 scale=1.0)
                    else:
                        self.copy("s", qlin, ps, [pb], [qb])
                    for th in range(2):
                        self.mm(self.psB[:, th * 512:(th + 1) * 512], piT_sb,
                                qlin[:, th * 512:(th + 1) * 512], start=True, stop=True,
                                reads=[piT_buf, qb],
                                writes=[self.bBa if th == 0 else self.bBb])
                    self.tt("v", self.scrB[:, 0:1024], qlin, cosq, ALU.mult,
                            [qb, tbb], [B("scrB")])
                    self.tt("v", self.scrC[:, 0:1024], self.psB, sinq, ALU.mult,
                            [self.bBa, self.bBb, tbb], [B("scrC")])
                    self.tt("v", dst, self.scrB[:, 0:1024], self.scrC[:, 0:1024],
                            ALU.add, [B("scrB"), B("scrC")], [dst_buf])
                self.pump_spline(pump)
            if hook is not None:
                hook(grp)

    def attn_qt(self, cs):
        B = self.B
        c0 = cs[0]
        E = c0["E"]
        nQ = (E + 1023) // 1024
        mstart, mwidth, mask_ap, mbuf = c0["mask"]

        def stq_of(c):
            return self.ast[:, c["par"] * 32:(c["par"] + 1) * 32]

        def phaseA(c, Q):
            stq = stq_of(c)
            bstq = B(f"ast{c['par']}q{Q}")
            w = min(1024, E - Q * 1024)
            ps = self.psA_h[c["slot"]]
            pb = self.bA[c["slot"]]
            lhsT = c["Q_sb"][:, c["q_col"]:c["q_col"] + 128]
            for ch in range((w + 511) // 512):
                cw = min(512, w - ch * 512)
                col = Q * 1024 + ch * 512
                masked = (mstart <= col < mstart + mwidth)
                self.mm(ps[:, ch * 512: ch * 512 + cw], lhsT,
                        c["krot_sb"][:, c["kcol_of"](Q) + ch * 512:
                                     c["kcol_of"](Q) + ch * 512 + cw],
                        start=True, stop=not masked,
                        reads=[c["Q_buf"], c["krot_buf"]], writes=[pb])
                if masked:
                    self.mm(ps[:, ch * 512: ch * 512 + cw], self.identb,
                            mask_ap[:, col - mstart: col - mstart + cw],
                            start=False, stop=True,
                            reads=[B("identb"), mbuf], writes=[pb])
            self.reduce(stq[:, Q:Q + 1], ps[:, 0:w], ALU.max, [pb], [bstq], negate=True)
            sl = Q % c["Pslots"]
            pqb = B(f"{c['Pn']}_{sl}")
            self.act(c["P"][:, sl * 1024: sl * 1024 + w], ps[:, 0:w], AF.Exp,
                     [pb, bstq], [pqb, bstq],
                     scale=1.0, bias=stq[:, Q:Q + 1], accum_out=stq[:, 8 + Q:9 + Q])

        def phaseBtrans(c, Q):
            w = min(1024, E - Q * 1024)
            nkt = w // 128
            sl = Q % c["Pslots"]
            pqb = B(f"{c['Pn']}_{sl}")
            ptb = B(f"{c['PTn']}{Q % 2}")
            PT = c["PT"]
            h = self._tg_flip = 1 - getattr(self, "_tg_flip", 1)
            ps = self.psT_h[h]
            pbuf = self.bT[h]
            for i in range(nkt):
                ap = c["P"][:, (sl * 8 + i) * 128:(sl * 8 + i + 1) * 128]
                self.op("t", lambda ap=ap, i=i, ps=ps: self.t.transpose(
                    ps[:, i * 128:(i + 1) * 128], ap, self.identb),
                    reads=[pqb, B("identb")], writes=[pbuf])
            self.copy("v", PT[:, (Q % 2) * 1024:(Q % 2) * 1024 + nkt * 128],
                      ps[:, 0:nkt * 128], [pbuf], [ptb])

        def phaseBpv(c, Q):
            w = min(1024, E - Q * 1024)
            nkt = w // 128
            ptb = B(f"{c['PTn']}{Q % 2}")
            PT = c["PT"]
            ops = self.psB_a[:, 0:128] if c["slot"] == 0 else self.psB_b[:, 0:128]
            opb = self.bBa if c["slot"] == 0 else self.bBb
            for i in range(nkt):
                kt = Q * 8 + i
                self.mm(ops, PT[:, (Q % 2) * 1024 + i * 128:(Q % 2) * 1024 + (i + 1) * 128],
                        c["V_sb"][:, c["vcol_of"](kt): c["vcol_of"](kt) + 128],
                        start=(i == 0), stop=(i == nkt - 1),
                        reads=[ptb, c["V_buf"]], writes=[opb])
            self.copy("s", c["outq"][:, Q * 128:(Q + 1) * 128], ops, [opb],
                      [B(c["outq_n"])])

        for Q in range(nQ + 3):
            for c in cs:
                if Q < nQ:
                    phaseA(c, Q)
            for c in cs:
                if 0 <= Q - 2 < nQ:
                    phaseBtrans(c, Q - 2)
            for c in cs:
                if 0 <= Q - 3 < nQ:
                    phaseBpv(c, Q - 3)
            self.pump_spline(c0.get("pump", 0))

        for c in cs:
            stq = stq_of(c)
            bstq = [B(f"ast{c['par']}q{q}") for q in range(8)]
            bst = B(f"ast{c['par']}")
            outq = c["outq"]
            boutq = B(c["outq_n"])
            self.reduce(stq[:, 24:25], stq[:, 0:nQ], ALU.min, bstq[:nQ] + [bst], [bst])
            self.ts("v", stq[:, 0:nQ], stq[:, 0:nQ], stq[:, 24:25], -1.0,
                    ALU.subtract, ALU.mult, bstq[:nQ] + [bst], [bst] + bstq[:nQ])
            self.act(stq[:, 0:nQ], stq[:, 0:nQ], AF.Exp, [bst], [bst])
            self.tt("v", stq[:, 16:16 + nQ], stq[:, 0:nQ], stq[:, 8:8 + nQ], ALU.mult,
                    [bst, boutq], [bst])
            self.reduce(stq[:, 25:26], stq[:, 16:16 + nQ], ALU.add, [bst], [bst])
            self.op("v", lambda s=stq: self.v.reciprocal(s[:, 26:27], s[:, 25:26]),
                    [bst], [bst])
            self.ts("v", stq[:, 0:nQ], stq[:, 0:nQ], stq[:, 26:27], 0.0,
                    ALU.mult, ALU.bypass, [bst], [bst])
            if nQ == 1:
                self.ts("v", c["xa_ap"], outq[:, 0:128], stq[:, 0:1], 0.0,
                        ALU.mult, ALU.bypass, [boutq, bst], [B("xattn")])
            else:
                par = c["par"]
                acc = self.scrC[:, 1024 + par * 128:1152 + par * 128]
                bacc = B(f"merge_acc{par}")
                self.ts("v", acc, outq[:, 0:128], stq[:, 0:1], 0.0,
                        ALU.mult, ALU.bypass, [boutq, bst], [bacc])
                for Q in range(1, nQ - 1):
                    self.stt("v", acc, outq[:, Q * 128:(Q + 1) * 128], stq[:, Q:Q + 1],
                             acc, ALU.mult, ALU.add, [boutq, bst, bacc], [bacc])
                Q = nQ - 1
                self.stt("v", c["xa_ap"], outq[:, Q * 128:(Q + 1) * 128], stq[:, Q:Q + 1],
                         acc, ALU.mult, ALU.add, [boutq, bst, bacc], [B("xattn")])

    def xattn_tm_to_fm(self, heads=None):
        B = self.B
        for h in (range(NH) if heads is None else heads):
            hh = self._tg_flip = 1 - getattr(self, "_tg_flip", 1)
            ps = self.psT_h[hh]
            pbuf = self.bT[hh]
            for qt in range(8):
                sl = self.xattn[:, h * TOK + qt * 128: h * TOK + (qt + 1) * 128]
                self.op("t", lambda sl=sl, ps=ps, qt=qt: self.t.transpose(
                    ps[:, qt * 128:(qt + 1) * 128], sl, self.identb),
                    reads=[B("xattn"), B("identb")], writes=[pbuf])
            self.copy("v", self.xattn[:, h * TOK:(h + 1) * TOK], ps[:, 0:1024],
                      [pbuf], [B("xattn")])

    def out_proj(self, w_dram, bias_col, dst_dram, dst_buf, pump=0):
        B = self.B
        OD = w_dram.tensor.shape[1]
        for grp in range(8):
            self._wstc = (getattr(self, "_wstc", 0) + 1) % 2
            gw = self.wst[self._wstc]
            gwb = B(f"wstream{self._wstc}")
            gw3 = gw.rearrange("p (f c) -> p f c", f=16)
            src = bass_rust.AP(w_dram.tensor, grp * 256,
                               [[OD, 128], [128 * OD, 16], [1, 256]])
            self.dma("g", gw3, src, writes=[gwb])
            for o2 in range(2):
                oc = grp * 2 + o2
                ps = self.psA_h[o2]
                pb = self.bA[o2]
                for fc in range(16):
                    lhsT = gw3[:, fc:fc + 1, o2 * 128:(o2 + 1) * 128].squeeze(1)
                    for th in range(2):
                        self.mm(ps[:, th * 512:(th + 1) * 512], lhsT,
                                self.xattn[:, fc * TOK + th * 512: fc * TOK + (th + 1) * 512],
                                start=(fc == 0), stop=(fc == 15),
                                reads=[gwb, B("xattn")], writes=[pb])
                if bias_col is not None:
                    self.act(self.scrA[:, 0:1024], ps, AF.Identity, [pb, B("bias")],
                             [B("scrA")],
                             bias=self.bias_sb[:, bias_col + oc:bias_col + oc + 1], scale=1.0)
                else:
                    self.copy("s", self.scrA[:, 0:1024], ps, [pb], [B("scrA")])
                for tq in range(2):
                    items = [(self.scrA[:, (tq * 4 + k) * 128:(tq * 4 + k + 1) * 128],
                              [B("scrA")]) for k in range(4)]
                    self.tgroup(items, self.scrB[:, tq * 512:(tq + 1) * 512], B("scrB"),
                                "v", dtype="f")
                dst = bass_rust.AP(self.contrib.tensor, oc * 128,
                                   [[D, 128], [128 * D, 8], [1, 128]])
                self.dma("g", dst, self.scrB[:, 0:1024].rearrange("p (t f) -> p t f", f=128),
                         reads=[B("scrB")], writes=[B("contrib")])
                self.pump_spline(pump)

    def sub1(self):
        B = self.B
        st_sub = ExitStack()
        self.xattn = self.sb("xattn1", [128, 16 * TOK], BF16, st=st_sub)
        self.scrC = self.sb("scrC1", [128, D], F32, st=st_sub)
        self.wst = [self.sb("wstr1a", [128, 16 * 256], BF16, st=st_sub),
                    self.sb("wstr1b", [128, 16 * 256], BF16, st=st_sub)]
        self.bufs.pop("xattn", None)
        st_xn = ExitStack()
        xn = self.sb("xn1", [128, 16 * TOK], BF16, st=st_xn)
        bxn = B("xn1")
        self.layernorm(self.x_in, B("x_in"), xn, bxn,
                       store_to=self.resid, store_buf=B("resid"))
        st = ExitStack()
        sbl = lambda n, s, d=F32: self.sb(n, s, d, st=st)
        kvfm = sbl("kvfm", [128, 1024])
        kvtok = sbl("kvtok", [128, 1024])
        wl = sbl("wl", [128, 16 * 128], BF16)
        wl3 = wl.rearrange("p (f c) -> p f c", f=16)
        self.dma("g", wl3,
                 bass_rust.AP(self.w_latent.tensor, 0, [[LC, 128], [128 * LC, 16], [1, 128]]),
                 writes=[B("wl")])
        for fc in range(16):
            for th in range(2):
                self.mm(self.psA_h[0][:, th * 512:(th + 1) * 512],
                        wl3[:, fc:fc + 1, :].squeeze(1),
                        xn[:, fc * TOK + th * 512: fc * TOK + (th + 1) * 512],
                        start=(fc == 0), stop=(fc == 15), reads=[B("wl"), bxn],
                        writes=[self.bA[0]])
        self.act(kvfm, self.psA_h[0], AF.Identity, [self.bA[0], B("bias")], [B("kvfm")],
                 bias=self.bias_sb[:, 0:1], scale=1.0)
        self.dma("y", self.cc1f_in, kvfm, reads=[B("kvfm")], writes=[B("cc1f_in")])
        self.op("g", lambda: self.g.collective_compute(
            "AllGather", ALU.bypass, replica_groups=[list(range(NCORES))],
            ins=[self.cc1f_in.opt()], outs=[self.cc1f_out.opt()]),
            reads=[B("cc1f_in")], writes=[B("cc1f_out_src")], sem="ag_1f", n=1)
        for tq in range(2):
            items = [(kvfm[:, (tq * 4 + k) * 128:(tq * 4 + k + 1) * 128], [B("kvfm")])
                     for k in range(4)]
            self.tgroup(items, kvtok[:, tq * 512:(tq + 1) * 512], B("kvtok"), "v", dtype="f")
        self.dma("y", bass_rust.AP(self.cc1_in.tensor, 0, [[LC, 128], [128 * LC, 8], [1, LC]]),
                 kvtok.rearrange("p (t f) -> p t f", f=128), reads=[B("kvtok")],
                 writes=[B("cc1_in")])
        self.op("g", lambda: self.g.collective_compute(
            "AllGather", ALU.bypass, replica_groups=[list(range(NCORES))],
            ins=[self.cc1_in.opt()], outs=[self.cc1_out.opt()]),
            reads=[B("cc1_in")], writes=[B("cc1_out_src")], sem="ag_1", n=1)
        self.cc_barrier("cc1_out")
        B("cc1f_out").wtok = B("cc1_out").wtok
        self.pump_spline(24)
        st.close()

        st = ExitStack()
        sbl = lambda n, s, d=F32: self.sb(n, s, d, st=st)
        V = sbl("Vsb", [128, 64 * 128], BF16)
        krot = sbl("krot", [128, T], BF16)
        cosq = sbl("cosq", [128, 1024])
        sinq = sbl("sinq", [128, 1024])
        Qrot = sbl("Qrot", [128, 2048], BF16)
        maska = sbl("maska", [128, 1024], BF16)
        st_kb = ExitStack()
        kvx = self.sb("kvx", [128, 512], F32, st=st_kb)
        cosk = self.sb("cosk", [128, 512], F32, st=st_kb)
        sink = self.sb("sink", [128, 512], F32, st=st_kb)
        self.dma("y", cosq, self.cos_aq, writes=[B("ropeq")])
        self.dma("y", sinq, self.sin_aq, writes=[B("ropeq")])
        self.dma("y", maska, self.mask_a_d, writes=[B("maska")])
        ld0 = self.proj_load(self.w_d, 0, 2)
        self.proj(xn, bxn, self.w_d, 0, 2, Qrot, B("Qrot"), 0, bias_col=1,
                  rope=(self.piTa_s, B("piTa"), cosq, sinq, B("ropeq")), pump=1,
                  preloaded=ld0)
        nld = self.proj_load(self.w_d, 256, 2)
        self.pump_spline(24)
        for m in range(64):
            g0 = grow(m * 128)
            self.dma("g", V[:, m * 128:(m + 1) * 128],
                     bass_rust.AP(self.cc1_out.tensor, g0 * LC, [[LC, 128], [1, LC]]),
                     reads=[B("cc1_out")], writes=[B("V")])
        for ch in range(16):
            self.dma("y", cosk, self.cos_a[:, ch * 512:(ch + 1) * 512], writes=[B("ropek")])
            self.dma("y", sink, self.sin_a[:, ch * 512:(ch + 1) * 512], writes=[B("ropek")])
            src = bass_rust.AP(self.cc1f_out.tensor,
                               (4 * (ch % 2)) * 128 * TOK + (ch // 2) * 128,
                               [[TOK, 128], [128 * TOK, 4], [1, 128]])
            self.dma("y", kvx.rearrange("p (a j) -> p a j", j=128), src,
                     reads=[B("cc1f_out")], writes=[B("kvx")])
            self.mm(self.psB[:, 0:512], self.piTa_s, kvx[:, 0:512], start=True, stop=True,
                    reads=[B("piTa"), B("kvx")], writes=[self.bBa, self.bBb])
            self.tt("v", self.scrA[:, 0:512], kvx[:, 0:512], cosk, ALU.mult,
                    [B("kvx"), B("ropek")], [B("scrA")])
            self.tt("v", self.scrB[:, 0:512], self.psB[:, 0:512], sink, ALU.mult,
                    [self.bBa, self.bBb, B("ropek")], [B("scrB")])
            self.tt("v", krot[:, ch * 512:(ch + 1) * 512], self.scrA[:, 0:512],
                    self.scrB[:, 0:512], ALU.add, [B("scrA"), B("scrB")], [B("krot")])
            self.pump_spline(2)
        st_kb.close()
        P2 = [sbl(f"Pa{sx}", [128, 3072], BF16) for sx in range(2)]
        PT2 = [sbl(f"PTa{sx}", [128, 2048], BF16) for sx in range(2)]
        outq2 = [sbl(f"outq{sx}", [128, 8 * 128]) for sx in range(2)]

        it = 0
        for hp in range(8):
            if hp > 0:
                self.proj(xn, bxn, self.w_d, hp * 256, 2, Qrot, B("Qrot"), 0,
                          bias_col=1 + hp * 2,
                          rope=(self.piTa_s, B("piTa"), cosq, sinq, B("ropeq")), pump=1,
                          preloaded=nld)
                if hp + 1 < 8:
                    nld = self.proj_load(self.w_d, (hp + 1) * 256, 2)
            for qt in range(8):
                cs = []
                for sx in range(2):
                    h = hp * 2 + sx
                    cs.append(dict(
                        qt=qt, E=1024 * (qt + 1), par=(it % 2) * 2 + sx, slot=sx,
                        kcol_of=lambda Q: Q * 1024,
                        Q_sb=Qrot, Q_buf=B("Qrot"), q_col=sx * 1024 + qt * 128,
                        krot_sb=krot, krot_buf=B("krot"),
                        V_sb=V, V_buf=B("V"), vcol_of=lambda kt: kt * 128,
                        mask=(1024 * qt, 1024, maska, B("maska")),
                        P=P2[sx], Pn=f"Pa{sx}", Pslots=3, PT=PT2[sx], PTn=f"PTa{sx}",
                        outq=outq2[sx], outq_n=f"outq{sx}",
                        xa_ap=self.xattn[:, h * TOK + qt * 128:
                                         h * TOK + (qt + 1) * 128],
                        pump=1))
                self.attn_qt(cs)
                it += 1
            self.xattn_tm_to_fm([hp * 2, hp * 2 + 1])
        st.close()
        st_xn.close()
        self.out_proj(self.w_proj, 17, self.resid, B("resid"), pump=2)
        st_sub.close()

    def sub_ffn(self, first):
        B = self.B
        st_xn = ExitStack()
        xn = self.sb("xnf1" if first else "xnf2", [128, 16 * TOK], BF16, st=st_xn)
        bxn = B("xnf" + ("1" if first else "2"))
        self.layernorm(self.resid, B("resid"), xn, bxn, combine=True)
        st = ExitStack()
        sbl = lambda n, s, d=F32: self.sb(n, s, d, st=st)
        inter = sbl("inter" + ("1" if first else "2"), [128, 64 * TOK], BF16)
        wg = [sbl(f"wg0_{first}", [128, 16 * 128], BF16), sbl(f"wg1_{first}", [128, 16 * 128], BF16)]
        wu = [sbl(f"wu0_{first}", [128, 16 * 128], BF16), sbl(f"wu1_{first}", [128, 16 * 128], BF16)]
        gi = sbl(f"gi_{first}", [128, TOK], BF16)
        ui = sbl(f"ui_{first}", [128, TOK], BF16)

        def wtile_ap(gt, n=16):
            return bass_rust.AP(self.ccw_out.tensor, gt * 128 * 128,
                                [[128, 128], [128 * 128, n], [1, 128]])

        bA4 = [B(f"psA4_{k}") for k in range(4)]
        for ic in range(64):
            i2 = ic % 2
            wg3 = wg[i2].rearrange("p (f c) -> p f c", f=16)
            wu3 = wu[i2].rearrange("p (f c) -> p f c", f=16)
            self.dma("y", wg3, wtile_ap(ic * 16), reads=[B("ccw_out")], writes=[B(f"wg{i2}")])
            self.dma("g", wu3, wtile_ap(TPM + ic * 16), reads=[B("ccw_out")],
                     writes=[B(f"wu{i2}")])
            for th in range(2):
                j = ic * 2 + th
                sg, su = (2 * j) % 4, (2 * j + 1) % 4
                for fc in range(16):
                    lg = wg3[:, fc:fc + 1, :].squeeze(1)
                    self.mm(self.psA4[sg], lg,
                            xn[:, fc * TOK + th * 512: fc * TOK + (th + 1) * 512],
                            start=(fc == 0), stop=(fc == 15),
                            reads=[B(f"wg{i2}"), bxn], writes=[bA4[sg]])
                for fc in range(16):
                    lu = wu3[:, fc:fc + 1, :].squeeze(1)
                    self.mm(self.psA4[su], lu,
                            xn[:, fc * TOK + th * 512: fc * TOK + (th + 1) * 512],
                            start=(fc == 0), stop=(fc == 15),
                            reads=[B(f"wu{i2}"), bxn], writes=[bA4[su]])
                self.act(gi[:, th * 512:(th + 1) * 512], self.psA4[sg], AF.Silu,
                         [bA4[sg]], [B("gi")])
                self.copy("v", ui[:, th * 512:(th + 1) * 512], self.psA4[su],
                          [bA4[su]], [B("ui")])
                self.tt("v", inter[:, ic * TOK + th * 512: ic * TOK + (th + 1) * 512],
                        gi[:, th * 512:(th + 1) * 512], ui[:, th * 512:(th + 1) * 512],
                        ALU.mult, [B("gi"), B("ui")], [B("inter")])
        if not first:
            rres = [self.sb("rres0_f", [128, 1024], F32, st=st)] * 2
        for hc in range(16):
            h2 = hc % 2
            for icg in range(4):
                i2 = icg % 2
                wd3 = wg[i2].rearrange("p (f c) -> p f c", f=16)
                self.dma("y" if icg % 2 == 0 else "g", wd3,
                         wtile_ap(2 * TPM + hc * 64 + icg * 16),
                         reads=[B("ccw_out")], writes=[B(f"wg{i2}")])
                for i16 in range(16):
                    ic = icg * 16 + i16
                    lhsT = wd3[:, i16:i16 + 1, :].squeeze(1)
                    for th in range(2):
                        self.mm(self.psA_h[h2][:, th * 512:(th + 1) * 512], lhsT,
                                inter[:, ic * TOK + th * 512: ic * TOK + (th + 1) * 512],
                                start=(ic == 0), stop=(ic == 63),
                                reads=[B(f"wg{i2}"), B("inter")], writes=[self.bA[h2]])
            self.copy("s", self.scrA[:, 0:1024], self.psA_h[h2], [self.bA[h2]], [B("scrA")])
            for tq in range(2):
                items = [(self.scrA[:, (tq * 4 + k) * 128:(tq * 4 + k + 1) * 128],
                          [B("scrA")]) for k in range(4)]
                self.tgroup(items, self.scrB[:, tq * 512:(tq + 1) * 512], B("scrB"),
                            "v", dtype="f")
            if first:
                dst = bass_rust.AP(self.contrib.tensor, hc * 128,
                                   [[D, 128], [128 * D, 8], [1, 128]])
                self.dma("g", dst,
                         self.scrB[:, 0:1024].rearrange("p (t f) -> p t f", f=128),
                         reads=[B("scrB")], writes=[B("contrib")])
            else:
                rap = bass_rust.AP(self.resid.tensor, hc * 128,
                                   [[D, 128], [128 * D, 8], [1, 128]])
                self.dma("g", rres[h2].rearrange("p (t f) -> p t f", f=128), rap,
                         reads=[B("resid")], writes=[B("rres0")])
                self.tt("v", self.scrB[:, 0:1024], self.scrB[:, 0:1024], rres[h2],
                        ALU.add, [B("scrB"), B("rres0")], [B("scrB")])
                dst = bass_rust.AP(self.out_d.tensor, hc * 128,
                                   [[D, 128], [128 * D, 8], [1, 128]])
                self.dma("g", dst,
                         self.scrB[:, 0:1024].rearrange("p (t f) -> p t f", f=128),
                         reads=[B("scrB")], writes=[B("out")])
        st.close()
        st_xn.close()

    def sub3(self):
        B = self.B
        st0 = ExitStack()
        Q2 = self.sb("Q2", [128, 16 * TOK], BF16, st=st0)
        self.xattn = self.sb("xattn3", [128, 16 * TOK], BF16, st=st0)
        self.scrC = self.sb("scrC3", [128, D], F32, st=st0)
        self.wst = [self.sb("wstr3a", [128, 16 * 256], BF16, st=st0),
                    self.sb("wstr3b", [128, 16 * 256], BF16, st=st0)]
        self.bufs.pop("xattn", None)
        st = ExitStack()
        sbl = lambda n, s, d=F32: self.sb(n, s, d, st=st)
        kown = sbl("kown", [128, 8 * TOK], BF16)
        vown = sbl("vown", [128, 8 * TOK], BF16)
        cosq = sbl("cosq3", [128, 1024])
        sinq = sbl("sinq3", [128, 1024])
        cosk3 = sbl("cosk3", [128, 1024])
        sink3 = sbl("sink3", [128, 1024])
        st_xn = ExitStack()
        xn = self.sb("xn3", [128, 16 * TOK], BF16, st=st_xn)
        bxn = B("xn3")
        self.layernorm(self.resid, B("resid"), xn, bxn, combine=True)
        self.dma("y", cosq, self.cos_bq, writes=[B("ropeq3")])
        self.dma("y", sinq, self.sin_bq, writes=[B("ropeq3")])
        self.dma("y", cosk3, self.cos_bk, writes=[B("ropek3")])
        self.dma("y", sink3, self.sin_bk, writes=[B("ropek3")])
        rope3 = (self.piTb_s, B("piTb"), cosq, sinq, B("ropeq3"))
        rope3k = (self.piTb_s, B("piTb"), cosk3, sink3, B("ropek3"))
        self.proj(xn, bxn, self.w_pack, 2048, 8, kown, B("kown"), 0, rope=rope3k)

        def vhook(grp):
            for kvh in (2 * grp, 2 * grp + 1):
                self.dma("y", self.cc2_in[kvh][:, 0:1024],
                         kown[:, kvh * TOK:(kvh + 1) * TOK],
                         reads=[B("kown")], writes=[B(f"cc2_in{kvh}")])
                self.dma("y", self.cc2_in[kvh][:, 1024:2048],
                         vown[:, kvh * TOK:(kvh + 1) * TOK],
                         reads=[B("vown")], writes=[B(f"cc2_in{kvh}")])
                self.op("g", lambda kvh=kvh: self.g.collective_compute(
                    "AllGather", ALU.bypass, replica_groups=[list(range(NCORES))],
                    ins=[self.cc2_in[kvh].opt()], outs=[self.cc2_out[kvh].opt()]),
                    reads=[B(f"cc2_in{kvh}")], writes=[B(f"cc2_out{kvh}_src")],
                    sem="ag_2", n=1)

        self.proj(xn, bxn, self.w_pack, 3072, 8, vown, B("vown"), 0, hook=vhook)
        self.proj(xn, bxn, self.w_pack, 0, 16, Q2, B("Q2"), 0, rope=rope3)
        st_xn.close()
        self.cc_barrier("cc2x")
        for kvh in range(NKV):
            B(f"cc2_out{kvh}").wtok = B("cc2x").wtok
        st.close()

        st = ExitStack()
        sbl = lambda n, s, d=F32: self.sb(n, s, d, st=st)
        kc = sbl("kc", [128, NKV * 2048], BF16)
        vtok = sbl("vtok", [128, NKV * 2048], BF16)
        kprev = sbl("kprev", [128, NKV], F32)
        vprev = sbl("vprev", [128, NKV], F32)
        cscr = sbl("cscr", [128, 1024], BF16)
        kraw = [sbl(f"kraw{i}", [128, 1024], BF16) for i in range(2)]
        vraw = [sbl(f"vraw{i}", [128, 1024], BF16) for i in range(2)]
        maskb = sbl("maskb", [128, 2048], BF16)
        Pb2 = [sbl(f"Pb{sx}", [128, 2048], BF16) for sx in range(2)]
        PTb2 = [sbl(f"PTb{sx}", [128, 2048], BF16) for sx in range(2)]
        outq32 = [sbl(f"outq3{sx}", [128, 2 * 128]) for sx in range(2)]
        cw = sbl("cw_exp", [128, 32])
        bq, bc = B("qscr"), B("cscr")
        for kvh in range(NKV):
            self.dma("y", cw[:, kvh * 2:(kvh + 1) * 2],
                     self.conv_k[kvh * 128:(kvh + 1) * 128, :], writes=[B("cwexp")])
            self.dma("y", cw[:, 16 + kvh * 2:16 + (kvh + 1) * 2],
                     self.conv_v[kvh * 128:(kvh + 1) * 128, :], writes=[B("cwexp")])

        def conv_unit(qt, kvh):
            half = qt % 2
            for which, co in (("k", 0), ("v", 16)):
                colofs = 0 if which == "k" else 1024
                ri = kvh % 2
                raw = (kraw if which == "k" else vraw)[ri]
                rb = B(("krawb" if which == "k" else "vrawb") + str(ri))
                src = bass_rust.AP(self.cc2_out[kvh].tensor,
                                   colofs + qt * 128,
                                   [[2048, 128], [128 * 2048, 8], [1, 128]])
                self.dma("y", raw.rearrange("p (m f) -> p m f", f=128), src,
                         reads=[B(f"cc2_out{kvh}")], writes=[rb])
                pv = (kprev if which == "k" else vprev)[:, kvh:kvh + 1]
                pvb = B("kprev" if which == "k" else "vprev")
                if which == "k":
                    dst = kc[:, kvh * 2048 + half * 1024:
                             kvh * 2048 + half * 1024 + 1024]
                    db = B("kc")
                else:
                    dst = cscr
                    db = bc
                self.ts("v", dst, raw, cw[:, co + kvh * 2 + 1:co + kvh * 2 + 2], 0.0,
                        ALU.mult, ALU.add, [rb, B("cwexp")], [db])
                self.stt("v", dst[:, 1:1024], raw[:, 0:1023],
                         cw[:, co + kvh * 2:co + kvh * 2 + 1], dst[:, 1:1024],
                         ALU.mult, ALU.add, [rb, B("cwexp"), db], [db])
                if qt > 0:
                    self.stt("v", dst[:, 0:1], pv,
                             cw[:, co + kvh * 2:co + kvh * 2 + 1], dst[:, 0:1],
                             ALU.mult, ALU.add, [pvb, B("cwexp"), db], [db])
                self.copy("v", pv, raw[:, 1023:1024], [rb], [pvb])
                if which == "v":
                    for m4 in range(2):
                        items = [(cscr[:, (m4 * 4 + k) * 128:(m4 * 4 + k + 1) * 128],
                                  [bc]) for k in range(4)]
                        self.tgroup(items,
                                    vtok[:, kvh * 2048 + half * 1024 + m4 * 512:
                                         kvh * 2048 + half * 1024 + (m4 + 1) * 512],
                                    B("vtok"), "v")

        for qt in range(8):
            conv_unit(qt, 0)
            conv_unit(qt, 1)
            self.dma("y", maskb, self.mask_b_d[qt * 128:(qt + 1) * 128, :],
                     writes=[B("maskb")])
            kbase_q = qt - 1 if qt >= 1 else 0
            E = 2048 if qt >= 1 else 1024
            for hp in range(8):
                kvh = hp
                cs = []
                for sx in range(2):
                    h = hp * 2 + sx
                    cs.append(dict(
                        qt=qt, E=E, par=(hp % 2) * 2 + sx, slot=sx,
                        kcol_of=lambda Q, kvh=kvh: kvh * 2048
                        + ((kbase_q + Q) % 2) * 1024,
                        Q_sb=Q2, Q_buf=B("Q2"), q_col=h * TOK + qt * 128,
                        krot_sb=kc, krot_buf=B("kc"),
                        V_sb=vtok, V_buf=B("vtok"),
                        vcol_of=lambda kt, kvh=kvh: kvh * 2048
                        + ((kbase_q + kt // 8) % 2) * 1024 + (kt % 8) * 128,
                        mask=(0, 2048, maskb, B("maskb")),
                        P=Pb2[sx], Pn=f"Pb{sx}", Pslots=2, PT=PTb2[sx], PTn=f"PTb{sx}",
                        outq=outq32[sx], outq_n=f"outq3{sx}",
                        xa_ap=self.xattn[:, h * TOK + qt * 128:
                                         h * TOK + (qt + 1) * 128]))
                self.attn_qt(cs)
                if hp + 2 < NKV:
                    conv_unit(qt, hp + 2)
                if qt == 7:
                    self.xattn_tm_to_fm([hp * 2, hp * 2 + 1])
        st.close()
        self.out_proj(self.w_o, None, self.resid, B("resid"))
        st0.close()

    def finish(self):
        B = self.B
        ob = B("out")
        toks = ([ob.wtok] if ob.wtok else []) + ob.rtoks
        if DBG_CCW:
            db = B("dbg1")
            toks += ([db.wtok] if db.wtok else [])
        self.wait("y", toks)


def kernel(L0_attn_a_x, L0_attn_a_W_latent, L0_attn_a_b_latent, L0_attn_a_W_d,
           L0_attn_a_b_d, L0_attn_a_W_proj, L0_attn_a_b_proj, L0_attn_b_W_pack,
           L0_attn_b_W_o, L0_attn_b_conv_k, L0_attn_b_conv_v, L0_ffn_gate_cp,
           L0_ffn_up_cp, L0_ffn_down_cp):
    x = np.ascontiguousarray(np.asarray(L0_attn_a_x, np.float32).reshape(T, D))
    cos_a, sin_a, piT_a = make_rope_a()
    cos_b, sin_b, piT_b = make_rope_b()
    ident_f = np.eye(128, dtype=np.float32)
    ident_b = np.eye(128, dtype=np.float32).astype(BFNP)
    f0c = np.broadcast_to(
        (np.arange(128, dtype=np.float64) * (NCTRL - 1.0) / (NSAMP - 1.0)).astype(np.float32),
        (128, 128)).copy()
    cps = [np.asarray(L0_ffn_gate_cp, np.float32),
           np.asarray(L0_ffn_up_cp, np.float32),
           np.asarray(L0_ffn_down_cp, np.float32)]
    conv_k = np.ascontiguousarray(
        np.repeat(np.asarray(L0_attn_b_conv_k, np.float32)[:, None, :], 128, axis=1)
        .reshape(NKV * 128, 2))
    conv_v = np.ascontiguousarray(
        np.repeat(np.asarray(L0_attn_b_conv_v, np.float32)[:, None, :], 128, axis=1)
        .reshape(NKV * 128, 2))
    b_lat = np.ascontiguousarray(np.asarray(L0_attn_a_b_latent, np.float32).reshape(LC, 1))
    b_d = np.ascontiguousarray(np.asarray(L0_attn_a_b_d, np.float32).reshape(NH, 128).T)
    b_pr = np.ascontiguousarray(np.asarray(L0_attn_a_b_proj, np.float32).reshape(16, 128).T)

    in_maps = []
    for core in range(NCORES):
        rows = np.array([gpos(core, r) for r in range(TOK)])
        cp6_dev, phi_dev = spline_core_tables(core, cps)
        in_maps.append({
            "x": np.ascontiguousarray(x[rows]),
            "w_latent": np.asarray(L0_attn_a_W_latent, np.float32),
            "w_d": np.asarray(L0_attn_a_W_d, np.float32),
            "w_proj": np.asarray(L0_attn_a_W_proj, np.float32),
            "w_pack": np.asarray(L0_attn_b_W_pack, np.float32),
            "w_o": np.asarray(L0_attn_b_W_o, np.float32),
            "b_latent": b_lat, "b_d": b_d, "b_proj": b_pr,
            "conv_k": conv_k, "conv_v": conv_v,
            "cp6": cp6_dev, "phi": phi_dev, "f0c": f0c,
            "cos_a": cos_a, "sin_a": sin_a,
            "cos_aq": np.ascontiguousarray(cos_a[:, rows] * SCALE),
            "sin_aq": np.ascontiguousarray(sin_a[:, rows] * SCALE),
            "cos_bq": np.ascontiguousarray(cos_b[:, rows] * SCALE),
            "sin_bq": np.ascontiguousarray(sin_b[:, rows] * SCALE),
            "cos_bk": np.ascontiguousarray(cos_b[:, rows]),
            "sin_bk": np.ascontiguousarray(sin_b[:, rows]),
            "piT_a": piT_a, "piT_b": piT_b,
            "ident_f": ident_f, "ident_b": ident_b,
            "mask_a": make_mask_a(core), "mask_b": make_masks_b(core),
        })
    prog = Prog()
    nc = prog.build()
    res = run_bass_kernel_spmd(nc, in_maps, core_ids=list(range(NCORES)))
    global _LAST_RES
    _LAST_RES = res
    out = np.zeros((T, D), np.float32)
    for core in range(NCORES):
        o = np.asarray(res.results[core]["out"])
        rows = np.array([gpos(core, r) for r in range(TOK)])
        out[rows] = o
    try:
        prog.stack.close()
    except Exception:
        pass
    return out.reshape(1, T, D)


# version 29 (final)
# speedup vs baseline: 1.0317x; 1.0249x over previous
import numpy as np
import ml_dtypes
import bass_rust
import concourse.bass as bass
import concourse.mybir as mybir
from concourse.bass_utils import run_bass_kernel_spmd
from contextlib import ExitStack

F32 = mybir.dt.float32
BF16 = mybir.dt.bfloat16
AF = mybir.ActivationFunctionType
ALU = mybir.AluOpType
AX = mybir.AxisListType
BFNP = ml_dtypes.bfloat16

NCORES = 8
T = 8192
D = 2048
TOK = 1024
NH = 16
LC = 128
NKV = 8
HD = 128
WINDOW = 1024
INTER = 8192
NCTRL = 130156
NSAMP = INTER * D
RUN = 128
NRUNS = NSAMP // RUN
TPM = NSAMP // (128 * 128)
NTILES = 3 * TPM
TPC = NTILES // NCORES
GEN_BATCH = 96
NEG = -1.0e9
SCALE = float(1.0 / np.sqrt(128.0))
EPS = 1e-5

STOP_AFTER = 4
DBG_CCW = False
_LAST_RES = None
ENGMAP = {"t": "tensor", "v": "vector", "s": "scalar", "g": "gpsimd", "y": "sync"}


def gpos(core, row):
    return 1024 * (row // 128) + 128 * core + (row % 128)


def grow(p):
    return ((p % 1024) // 128) * 1024 + (p // 1024) * 128 + (p % 128)


def make_rope_a():
    half = LC // 2
    inv = 10000.0 ** (-np.arange(half, dtype=np.float64) / half)
    freq = np.repeat(inv, 2)
    ang = freq[:, None] * np.arange(T, dtype=np.float64)[None, :]
    cos = np.cos(ang).astype(np.float32)
    sin = np.sin(ang).astype(np.float32)
    Pi = np.zeros((LC, LC), np.float32)
    for i in range(half):
        Pi[2 * i, 2 * i + 1] = -1.0
        Pi[2 * i + 1, 2 * i] = 1.0
    return cos, sin, np.ascontiguousarray(Pi.T)


def make_rope_b():
    half = HD // 2
    inv = 1.0 / (100000.0 ** (np.arange(0, HD, 2, dtype=np.float64) / HD))
    pos = np.arange(T, dtype=np.float64)
    cos = np.zeros((HD, T), np.float64)
    ssin = np.zeros((HD, T), np.float64)
    for d in range(HD):
        f = inv[d % half]
        cos[d] = np.cos(pos * f)
        ssin[d] = np.sin(pos * f) * (-1.0 if d < half else 1.0)
    Pi2 = np.zeros((HD, HD), np.float32)
    for d in range(half):
        Pi2[d, d + half] = 1.0
        Pi2[d + half, d] = 1.0
    return cos.astype(np.float32), ssin.astype(np.float32), np.ascontiguousarray(Pi2.T)


_SPL = {}


def spline_meta():
    if "m" not in _SPL:
        t = np.linspace(0.0, NCTRL - 1.0, NSAMP, dtype=np.float64)
        n0 = np.arange(NRUNS, dtype=np.int64) * RUN
        t0 = t[n0]
        i0 = np.clip(np.floor(t0).astype(np.int64), 0, NCTRL - 2)
        phi = (t0 - i0).astype(np.float32)
        p = np.arange(128)
        runmap = np.zeros((NTILES, 128), np.int64)
        for tt in range(TPM):
            oc, hb = tt // 16, tt % 16
            runmap[tt] = (128 * oc + p) * 16 + hb
            runmap[TPM + tt] = (128 * oc + p) * 16 + hb
        for tt in range(TPM):
            hc, ic = tt // 64, tt % 64
            runmap[2 * TPM + tt] = (128 * hc + p) * 64 + ic
        _SPL["m"] = (i0, phi, runmap)
    return _SPL["m"]


def spline_core_tables(core, cps):
    i0, phi, runmap = spline_meta()
    lo = core * TPC
    cp6 = np.zeros((TPC, 128, 8), np.float32)
    phv = np.zeros((TPC, 128), np.float32)
    for j in range(TPC):
        gt = lo + j
        runs = runmap[gt]
        ii = i0[runs]
        idx = np.clip(ii[:, None] - 1 + np.arange(8)[None, :], 0, NCTRL - 1)
        cp6[j] = cps[gt // TPM][idx]
        phv[j] = phi[runs]
    cp6_dev = np.ascontiguousarray(cp6.transpose(1, 0, 2).reshape(128, TPC * 8))
    phi_dev = np.ascontiguousarray(phv.T)
    return cp6_dev, phi_dev


def make_mask_a(core):
    r = np.arange(128)[:, None]
    j = np.arange(1024)[None, :]
    return np.where(j <= 128 * core + r, 0.0, NEG).astype(BFNP)


def make_masks_b(core):
    m = np.zeros((8, 128, 2048), np.float32)
    for qt in range(8):
        kbase = 1024 * (qt - 1) if qt >= 1 else 0
        gq = 1024 * qt + 128 * core
        r = np.arange(128)[:, None]
        j = np.arange(2048)[None, :]
        k = kbase + j
        q = gq + r
        m[qt] = np.where((k <= q) & (k > q - WINDOW), 0.0, NEG)
    return np.ascontiguousarray(m.reshape(8 * 128, 2048)).astype(BFNP)


class Buf:
    __slots__ = ("wtok", "rtoks", "name")

    def __init__(self, name=""):
        self.wtok = None
        self.rtoks = []
        self.name = name


class Prog:
    def __init__(self):
        self.nc = bass.Bass("TRN2", target_bir_lowering=False)
        self.stack = ExitStack()
        self._sems = {}
        self._waited = {}
        self.bufs = {}

    def B(self, name):
        if name not in self.bufs:
            self.bufs[name] = Buf(name)
        return self.bufs[name]

    def sem(self, name):
        if name not in self._sems:
            h = self.stack.enter_context(self.nc.semaphore(name))
            self._sems[name] = [h, 0]
        return self._sems[name]

    def wait(self, engname, toks):
        eng = getattr(self.nc, ENGMAP[engname])
        best = {}
        for (name, v) in toks:
            if best.get(name, -1) < v:
                best[name] = v
        for name, v in best.items():
            key = (engname, name)
            if self._waited.get(key, 0) >= v:
                continue
            eng.wait_ge(self.sem(name)[0], v)
            self._waited[key] = v

    def op(self, engname, emit, reads=(), writes=(), sem=None, n=1):
        toks = []
        for b in reads:
            if b.wtok is not None:
                toks.append(b.wtok)
        for b in writes:
            if b.wtok is not None:
                toks.append(b.wtok)
            toks.extend(b.rtoks)
        if engname == "t":
            toks = [t for t in toks if t[0] != "c_t"]
        self.wait(engname, toks)
        instr = emit()
        se = self.sem(sem or ("c_" + engname))
        se[1] += n
        instr.then_inc(se[0], n)
        tok = (sem or ("c_" + engname), se[1])
        for b in reads:
            b.rtoks.append(tok)
            if len(b.rtoks) > 12:
                best = {}
                for (nm, v) in b.rtoks:
                    if best.get(nm, -1) < v:
                        best[nm] = v
                b.rtoks = list(best.items())
        for b in writes:
            b.wtok = tok
            b.rtoks = []
        return tok

    def dma(self, q, out, in_, reads=(), writes=(), **kw):
        eng = getattr(self.nc, ENGMAP[q])
        return self.op(q, lambda: eng.dma_start(out=out, in_=in_, **kw),
                       reads=reads, writes=writes, sem="d_" + q, n=16)

    def din(self, name, shape, dt=F32):
        return self.nc.dram_tensor(name, shape, dt, kind="ExternalInput").ap()

    def sb(self, name, shape, dt=F32, st=None):
        return (st or self.stack).enter_context(self.nc.sbuf_tensor(name, shape, dt))[:]

    def ts(self, e, out, in0, s1, s2, op0, op1, reads, writes, accum_out=None):
        eng = getattr(self.nc, ENGMAP[e])
        return self.op(e, lambda: eng.tensor_scalar(out, in0, s1, s2, op0=op0, op1=op1,
                                                    accum_out=accum_out), reads, writes)

    def tt(self, e, out, in0, in1, alu, reads, writes):
        eng = getattr(self.nc, ENGMAP[e])
        return self.op(e, lambda: eng.tensor_tensor(out, in0, in1, op=alu), reads, writes)

    def stt(self, e, out, in0, scalar, in1, op0, op1, reads, writes):
        eng = getattr(self.nc, ENGMAP[e])
        return self.op(e, lambda: eng.scalar_tensor_tensor(out, in0, scalar, in1,
                                                           op0=op0, op1=op1), reads, writes)

    def act(self, out, in_, func, reads, writes, **kw):
        return self.op("s", lambda: self.s.activation(out, in_, func, **kw), reads, writes)

    def copy(self, e, out, in_, reads, writes):
        if e == "s":
            return self.op("s", lambda: self.s.copy(out, in_), reads, writes)
        eng = getattr(self.nc, ENGMAP[e])
        return self.op(e, lambda: eng.tensor_copy(out, in_), reads, writes)

    def mm(self, out, lhsT, rhs, start, stop, reads, writes):
        return self.op("t", lambda: self.t.matmul(out, lhsT, rhs, start=start, stop=stop),
                       reads, writes)

    def reduce(self, out, in_, alu, reads, writes, negate=False):
        return self.op("v", lambda: self.v.tensor_reduce(out, in_, axis=AX.X, op=alu,
                                                         negate=negate), reads, writes)

    def tgroup(self, items, dst_ap, dst_buf, evac_eng="s", dtype="b"):
        n = len(items)
        if dtype == "b":
            h = self._tg_flip = 1 - getattr(self, "_tg_flip", 1)
            ps = self.psT_h[h]
            pbuf = self.bT[h]
            ident, ib = self.identb, self.B("identb")
        else:
            ps = self.psB_b
            pbuf = self.bBb
            ident, ib = self.identf, self.B("identf")
        for i, (ap, rb) in enumerate(items):
            self.op("t", lambda ap=ap, i=i: self.t.transpose(
                ps[:, i * 128:(i + 1) * 128], ap, ident),
                reads=list(rb) + [ib], writes=[pbuf])
        self.copy(evac_eng, dst_ap, ps[:, 0:n * 128], [pbuf], [dst_buf])

    def tsingle(self, in_ap, in_bufs, dst_ap, dst_buf, evac_eng="s", dtype="b"):
        self.tgroup([(in_ap, in_bufs)], dst_ap, dst_buf, evac_eng, dtype)

    def build(self):
        nc = self.nc
        self.t, self.v, self.s, self.g, self.y = (nc.tensor, nc.vector, nc.scalar,
                                                  nc.gpsimd, nc.sync)
        B = self.B

        self.x_in = self.din("x", [TOK, D])
        self.w_latent = self.din("w_latent", [D, LC])
        self.w_d = self.din("w_d", [D, NH * LC])
        self.w_proj = self.din("w_proj", [NH * LC, D])
        self.w_pack = self.din("w_pack", [D, 4096])
        self.w_o = self.din("w_o", [NH * HD, D])
        self.b_latent = self.din("b_latent", [128, 1])
        self.b_d = self.din("b_d", [128, NH])
        self.b_proj = self.din("b_proj", [128, 16])
        self.conv_k = self.din("conv_k", [NKV * 128, 2])
        self.conv_v = self.din("conv_v", [NKV * 128, 2])
        self.cp6_d = self.din("cp6", [128, TPC * 8])
        self.phi_d = self.din("phi", [128, TPC])
        self.f0c_d = self.din("f0c", [128, 128])
        self.cos_a = self.din("cos_a", [128, T])
        self.sin_a = self.din("sin_a", [128, T])
        self.cos_aq = self.din("cos_aq", [128, TOK])
        self.sin_aq = self.din("sin_aq", [128, TOK])
        self.cos_bq = self.din("cos_bq", [128, TOK])
        self.sin_bq = self.din("sin_bq", [128, TOK])
        self.cos_bk = self.din("cos_bk", [128, TOK])
        self.sin_bk = self.din("sin_bk", [128, TOK])
        self.piT_a = self.din("piT_a", [128, 128])
        self.piT_b = self.din("piT_b", [128, 128])
        self.ident_fd = self.din("ident_f", [128, 128])
        self.ident_bd = self.din("ident_b", [128, 128], BF16)
        self.mask_a_d = self.din("mask_a", [128, 1024], BF16)
        self.mask_b_d = self.din("mask_b", [8 * 128, 2048], BF16)

        self.out_d = nc.dram_tensor("out", [TOK, D], F32, kind="ExternalOutput").ap()
        self.resid = nc.dram_tensor("resid", [TOK, D], F32).ap()
        self.contrib = nc.dram_tensor("contrib", [TOK, D], F32).ap()
        self.cc1_in = nc.dram_tensor("cc1_in", [TOK, LC], F32).ap()
        self.cc1_out = nc.dram_tensor("cc1_out", [T, LC], F32, addr_space="Shared").ap()
        self.cc1f_in = nc.dram_tensor("cc1f_in", [128, TOK], F32).ap()
        self.cc1f_out = nc.dram_tensor("cc1f_out", [8 * 128, TOK], F32).ap()
        self.cc2_in = [nc.dram_tensor(f"cc2_in{k}", [128, 2048], BF16).ap()
                       for k in range(NKV)]
        self.cc2_out = [nc.dram_tensor(f"cc2_out{k}", [8 * 128, 2048], BF16).ap()
                        for k in range(NKV)]
        self.ccw_in = nc.dram_tensor("ccw_in", [TPC * 128, 128], BF16).ap()
        self.ccw_out = nc.dram_tensor("ccw_out", [NTILES * 128, 128], BF16, addr_space="Shared").ap()
        if DBG_CCW:
            self.dbg1 = nc.dram_tensor("dbg1", [TPC * 128, 128], BF16,
                                       kind="ExternalOutput").ap()
        self.bar_in = nc.dram_tensor("bar_in", [1, 128], F32).ap()
        self.bar_out = nc.dram_tensor("bar_out", [1, 128], F32).ap()

        sb = self.sb
        self.identb = sb("identb", [128, 128], BF16)
        self.identf = sb("identf", [128, 128], F32)
        self.piTa_s = sb("piTa_s", [128, 128], F32)
        self.piTb_s = sb("piTb_s", [128, 128], F32)
        self.st1 = sb("st1", [128, 64], F32)
        self.ast = sb("ast", [128, 128], F32)
        self.scrA = sb("scrA", [128, D], F32)
        self.scrB = sb("scrB", [128, D], F32)
        self.xntok = sb("xntok", [128, D], BF16)
        self.bias_sb = sb("bias_sb", [128, 33], F32)

        psA = self.stack.enter_context(nc.psum_tensor("psA", [128, 2048], F32))[:]
        psB = self.stack.enter_context(nc.psum_tensor("psB", [128, 1024], F32))[:]
        psT = self.stack.enter_context(nc.psum_tensor("psT", [128, 2048], BF16))[:]
        self.psA_h = [psA[:, 0:1024], psA[:, 1024:2048]]
        self.psA4 = [psA[:, k * 512:(k + 1) * 512] for k in range(4)]
        self.psB = psB
        self.psB_a = psB[:, 0:512]
        self.psB_b = psB[:, 512:1024]
        self.psT_h = [psT[:, 0:1024], psT[:, 1024:2048]]
        self.bA = [B("psA0"), B("psA1")]
        self.bBa, self.bBb = B("psB_a"), B("psB_b")
        self.bT = [B("psT0"), B("psT1")]

        self.dma("y", self.identb, self.ident_bd, writes=[B("identb")])
        self.dma("y", self.identf, self.ident_fd, writes=[B("identf")])
        self.dma("y", self.piTa_s, self.piT_a, writes=[B("piTa")])
        self.dma("y", self.piTb_s, self.piT_b, writes=[B("piTb")])
        self.dma("y", self.bias_sb[:, 0:1], self.b_latent, writes=[B("bias")])
        self.dma("y", self.bias_sb[:, 1:17], self.b_d, writes=[B("bias")])
        self.dma("y", self.bias_sb[:, 17:33], self.b_proj, writes=[B("bias")])

        self.op("v", lambda: self.v.memset(self.st1[:, 63:64], EPS), [], [B("lnst")])
        self._spline_iter = self.spline_generator()
        self._spline_done = False
        self.pump_spline(1)

        self.sub1()
        if STOP_AFTER >= 2:
            self.pump_spline(1 << 30)
            self._spline_stack.close()
            self.sub_ffn(first=True)
        if STOP_AFTER >= 3:
            self.sub3()
        if STOP_AFTER >= 4:
            self.sub_ffn(first=False)
        self.finish()
        return nc

    def spline_generator(self):
        st = ExitStack()
        self._spline_stack = st
        B = self.B
        sbl = lambda n, s, d=F32: self.sb(n, s, d, st=st)
        cp6 = sbl("sp_cp6", [128, GEN_BATCH * 8])
        phi = sbl("sp_phi", [128, TPC])
        f0c = sbl("sp_f0c", [128, 128])
        co = sbl("sp_co", [128, GEN_BATCH * 12])
        NDEP = 3
        gm = [sbl(f"sp_m{i}", [128, 128]) for i in range(NDEP)]
        gf = [sbl(f"sp_f{i}", [128, 128]) for i in range(NDEP)]
        gf2 = [sbl(f"sp_f2{i}", [128, 128]) for i in range(NDEP)]
        gf3 = [sbl(f"sp_f3{i}", [128, 128]) for i in range(NDEP)]
        gu = [sbl(f"sp_u{i}", [128, 128]) for i in range(NDEP)]
        gv = [sbl(f"sp_v{i}", [128, 128]) for i in range(NDEP)]
        gw = [sbl(f"sp_w{i}", [128, 128], BF16) for i in range(NDEP)]
        gtmp = sbl("sp_tmp", [128, GEN_BATCH])
        gstage = sbl("sp_stage", [128, 1024], BF16)
        bco, bphi, bf0, bcp = B("sp_co"), B("sp_phi"), B("sp_f0c"), B("sp_cp6")
        btmp, bstage = B("sp_tmp"), B("sp_stage")
        self.dma("y", phi, self.phi_d, writes=[bphi])
        self.dma("y", f0c, self.f0c_d, writes=[bf0])
        cp3 = cp6.rearrange("p (t e) -> p t e", e=8)
        co3 = co.rearrange("p (k t) -> p k t", k=12)
        gt1 = gtmp.unsqueeze(1)

        def pk(q, k):
            return cp3[:, :, q + k:q + k + 1].transpose([0, 2, 1])

        NB = GEN_BATCH
        flush_prev = [lambda: None]
        for batch in range(TPC // NB):
            t0 = batch * NB
            self.dma("y", cp6, self.cp6_d[:, t0 * 8:(t0 + NB) * 8], writes=[bcp])
            for q in (0, 1):
                o = q * 4
                A = co3[:, o + 0:o + 1, :]
                Bc = co3[:, o + 1:o + 2, :]
                C = co3[:, o + 2:o + 3, :]
                Dc = co3[:, o + 3:o + 4, :]
                self.copy("v", A, pk(q, 1), [bcp], [bco])
                self.tt("v", gt1, pk(q, 2), pk(q, 0), ALU.subtract, [bcp], [btmp])
                self.ts("v", Bc, gt1, 0.5, 0.0, ALU.mult, ALU.add, [btmp], [bco])
                self.stt("v", gt1, pk(q, 1), -2.5, pk(q, 0), ALU.mult, ALU.add, [bcp], [btmp])
                self.stt("v", gt1, pk(q, 2), 2.0, gt1, ALU.mult, ALU.add, [bcp, btmp], [btmp])
                self.stt("v", C, pk(q, 3), -0.5, gt1, ALU.mult, ALU.add, [bcp, btmp], [bco])
                self.stt("v", gt1, pk(q, 1), 3.0, pk(q, 0), ALU.mult, ALU.subtract, [bcp], [btmp])
                self.stt("v", gt1, pk(q, 2), -3.0, gt1, ALU.mult, ALU.add, [bcp, btmp], [btmp])
                self.tt("v", gt1, gt1, pk(q, 3), ALU.add, [bcp, btmp], [btmp])
                self.ts("v", Dc, gt1, 0.5, 0.0, ALU.mult, ALU.add, [btmp], [bco])
            for k in range(4):
                self.tt("v", co3[:, 8 + k:9 + k, :], co3[:, 4 + k:5 + k, :],
                        co3[:, k:k + 1, :], ALU.subtract, [bco], [bco])
            if batch >= 1:
                flush_prev[0]()
            yield
            bm = [B(f"sp_bm{i}") for i in range(NDEP)]
            bfb = [B(f"sp_bf{i}") for i in range(NDEP)]
            bf2 = [B(f"sp_bf2{i}") for i in range(NDEP)]
            bf3 = [B(f"sp_bf3{i}") for i in range(NDEP)]
            bu = [B(f"sp_bu{i}") for i in range(NDEP)]
            bv = [B(f"sp_bv{i}") for i in range(NDEP)]
            bw = [B(f"sp_bw{i}") for i in range(NDEP)]
            def flush_tile(j, t0=t0):
                tloc = t0 + j
                i = j % NDEP
                gslot = j % 8
                self.tsingle(gw[i], [bw[i]], gstage[:, gslot * 128:(gslot + 1) * 128],
                             bstage, evac_eng="v")
                if gslot == 7:
                    g0 = tloc - 7
                    dst = bass_rust.AP(self.ccw_in.tensor, g0 * 128 * 128,
                                       [[128, 128], [128 * 128, 8], [1, 128]])
                    self.dma("g", dst, gstage.rearrange("p (t f) -> p t f", f=128),
                             reads=[bstage], writes=[B("ccw_in")])

            for j in range(NB):
                tloc = t0 + j
                i = j % NDEP
                ph = phi[:, tloc:tloc + 1]

                def cc(k, j=j):
                    return co3[:, k:k + 1, j:j + 1].squeeze(1)

                if j >= 1:
                    flush_tile(j - 1)
                self.ts("v", gm[i], f0c, ph, 1.0, ALU.add, ALU.is_ge, [bf0, bphi], [bm[i]])
                self.stt("v", gf[i], f0c, ph, gm[i], ALU.add, ALU.subtract,
                         [bf0, bphi, bm[i]], [bfb[i]])
                self.tt("g", gf2[i], gf[i], gf[i], ALU.mult, [bfb[i]], [bf2[i]])
                self.tt("g", gf3[i], gf2[i], gf[i], ALU.mult, [bfb[i], bf2[i]], [bf3[i]])
                self.act(gu[i], gf[i], AF.Identity, [bfb[i], bco], [bu[i]],
                         scale=cc(1), bias=cc(0))
                self.stt("v", gu[i], gf2[i], cc(2), gu[i], ALU.mult, ALU.add,
                         [bf2[i], bco, bu[i]], [bu[i]])
                self.stt("v", gu[i], gf3[i], cc(3), gu[i], ALU.mult, ALU.add,
                         [bf3[i], bco, bu[i]], [bu[i]])
                self.act(gv[i], gf[i], AF.Identity, [bfb[i], bco], [bv[i]],
                         scale=cc(9), bias=cc(8))
                self.stt("v", gv[i], gf2[i], cc(10), gv[i], ALU.mult, ALU.add,
                         [bf2[i], bco, bv[i]], [bv[i]])
                self.stt("v", gv[i], gf3[i], cc(11), gv[i], ALU.mult, ALU.add,
                         [bf3[i], bco, bv[i]], [bv[i]])
                self.tt("g", gv[i], gv[i], gm[i], ALU.mult, [bv[i], bm[i]], [bv[i]])
                self.tt("g", gw[i], gu[i], gv[i], ALU.add, [bu[i], bv[i]], [bw[i]])
                yield
            flush_prev[0] = (lambda NBm1=NB - 1: flush_tile(NBm1))
        flush_prev[0]()
        self.op("g", lambda: self.g.collective_compute(
            "AllGather", ALU.bypass, replica_groups=[list(range(NCORES))],
            ins=[self.ccw_in.opt()], outs=[self.ccw_out.opt()]),
            reads=[B("ccw_in")], writes=[B("ccw_out_src")], sem="ag_w", n=1)
        self.cc_barrier("ccw_out")
        if DBG_CCW:
            self.dma("y", self.dbg1, self.ccw_in, reads=[B("ccw_in")],
                     writes=[B("dbg1")])
        self._spline_done = True

    def cc_barrier(self, name):
        self.op("g", lambda: self.g.collective_compute(
            "AllReduce", ALU.add, replica_groups=[list(range(NCORES))],
            ins=[self.bar_in.opt()], outs=[self.bar_out.opt()]),
            reads=[self.B(name + "_src")], writes=[self.B(name)], sem=name + "_b", n=1)

    def pump_spline(self, n):
        if self._spline_done:
            return
        for _ in range(n):
            try:
                next(self._spline_iter)
            except StopIteration:
                self._spline_done = True
                break

    def layernorm(self, src_dram, src_buf, xn, xn_buf, store_to=None, store_buf=None,
                  combine=False):
        B = self.B
        rt = [self.scrA, self.scrB]
        rb = [B("scrA"), B("scrB")]
        stct = ExitStack()
        tag = self._ln_tag = getattr(self, "_ln_tag", 0) + 1
        if combine:
            ct0 = self.sb(f"ctA{tag}", [128, D], F32, st=stct)
            ct = [ct0, ct0]
            cb = [B("ctA"), B("ctA")]
        xnt = [self.xntok[:, 0:1024], self.xntok[:, 1024:2048]]
        bxnt = [B("xntok0"), B("xntok1")]
        for tt in range(8):
            i = tt % 2
            c = self.st1[:, i * 16: i * 16 + 16]
            bst = B(f"lnst{i}")
            self.dma("y", rt[i], src_dram[tt * 128:(tt + 1) * 128, :],
                     reads=[src_buf], writes=[rb[i]])
            if combine:
                self.dma("g", ct[i], self.contrib[tt * 128:(tt + 1) * 128, :],
                         reads=[B("contrib")], writes=[cb[i]])
                self.tt("v", rt[i], rt[i], ct[i], ALU.add, [rb[i], cb[i]], [rb[i]])
                self.dma("y", src_dram[tt * 128:(tt + 1) * 128, :], rt[i],
                         reads=[rb[i]], writes=[src_buf])
            if store_to is not None:
                self.dma("y", store_to[tt * 128:(tt + 1) * 128, :], rt[i],
                         reads=[rb[i]], writes=[store_buf])
            self.act(xnt[i], rt[i][:, 0:1024], AF.Square, [rb[i]], [bxnt[i], bst],
                     accum_out=c[:, 0:1])
            self.act(xnt[i], rt[i][:, 1024:2048], AF.Square, [rb[i]], [bxnt[i], bst],
                     accum_out=c[:, 9:10])
            self.op("s", lambda c=c: self.s.copy(c[:, 10:11], c[:, 9:10]), [bst], [bst])
            self.reduce(c[:, 1:2], rt[i], ALU.add, [rb[i]], [bst])
            self.tt("v", c[:, 0:1], c[:, 0:1], c[:, 10:11], ALU.add, [bst], [bst])
            self.ts("v", c[:, 2:3], c[:, 1:2], 1.0 / D, 0.0, ALU.mult, ALU.add, [bst], [bst])
            self.ts("v", c[:, 3:4], c[:, 0:1], 1.0 / D, 0.0, ALU.mult, ALU.add, [bst], [bst])
            self.tt("v", c[:, 4:5], c[:, 2:3], c[:, 2:3], ALU.mult, [bst], [bst])
            self.tt("v", c[:, 5:6], c[:, 3:4], c[:, 4:5], ALU.subtract, [bst], [bst])
            self.act(c[:, 6:7], c[:, 5:6], AF.Ln, [bst], [bst], bias=self.st1[:, 63:64])
            self.act(c[:, 7:8], c[:, 6:7], AF.Exp, [bst], [bst], scale=-0.5)
            self.stt("v", c[:, 8:9], c[:, 2:3], -1.0, c[:, 7:8], ALU.mult, ALU.mult,
                     [bst], [bst])
            for half in range(2):
                self.act(xnt[i], rt[i][:, half * 1024:(half + 1) * 1024], AF.Identity,
                         [rb[i], bst], [bxnt[i]], scale=c[:, 7:8], bias=c[:, 8:9])
                for fc4 in range(2):
                    h = self._tg_flip = 1 - getattr(self, "_tg_flip", 1)
                    ps = self.psT_h[h]
                    pbuf = self.bT[h]
                    for k in range(4):
                        fc = half * 8 + fc4 * 4 + k
                        ap = xnt[i][:, (fc4 * 4 + k) * 128:(fc4 * 4 + k + 1) * 128]
                        self.op("t", lambda ap=ap, k=k: self.t.transpose(
                            ps[:, k * 128:(k + 1) * 128], ap, self.identb),
                            reads=[bxnt[i], B("identb")], writes=[pbuf])
                    for k in range(4):
                        fc = half * 8 + fc4 * 4 + k
                        dst = xn[:, fc * TOK + tt * 128: fc * TOK + (tt + 1) * 128]
                        self.copy("s" if k % 2 == 0 else "v", dst,
                                  ps[:, k * 128:(k + 1) * 128], [pbuf], [xn_buf])
            self.pump_spline(2)
        stct.close()

    def proj_load(self, w_dram, w_cols0, no):
        OD = w_dram.tensor.shape[1]
        self._wstc = (getattr(self, "_wstc", 0) + 1) % 2
        gw = self.wst[self._wstc]
        gwb = self.B(f"wstream{self._wstc}")
        gw3 = gw.rearrange("p (f c) -> p f c", f=16)
        src = bass_rust.AP(w_dram.tensor, w_cols0,
                           [[OD, 128], [128 * OD, 16], [1, no * 128]])
        self.dma("g", gw3[:, :, 0:no * 128], src, writes=[gwb])
        return gw3, gwb

    def proj(self, xn, xn_buf, w_dram, w_cols0, n_out_chunks, dst_sb, dst_buf, dst_tok,
             bias_col=None, rope=None, pump=0, preloaded=None, hook=None):
        B = self.B
        for grp in range((n_out_chunks + 1) // 2):
            no = min(2, n_out_chunks - grp * 2)
            if preloaded is not None and grp == 0:
                gw3, gwb = preloaded
            else:
                gw3, gwb = self.proj_load(w_dram, w_cols0 + grp * 256, no)
            for o2 in range(no):
                oc = grp * 2 + o2
                ps = self.psA_h[o2]
                pb = self.bA[o2]
                for fc in range(16):
                    lhsT = gw3[:, fc:fc + 1, o2 * 128:(o2 + 1) * 128].squeeze(1)
                    for th in range(2):
                        self.mm(ps[:, th * 512:(th + 1) * 512], lhsT,
                                xn[:, fc * TOK + th * 512: fc * TOK + (th + 1) * 512],
                                start=(fc == 0), stop=(fc == 15),
                                reads=[gwb, xn_buf], writes=[pb])
                dst = dst_sb[:, (dst_tok + oc) * TOK: (dst_tok + oc + 1) * TOK]
                if rope is None:
                    if bias_col is not None:
                        self.act(dst, ps, AF.Identity, [pb, B("bias")], [dst_buf],
                                 bias=self.bias_sb[:, bias_col + oc:bias_col + oc + 1],
                                 scale=1.0)
                    else:
                        self.copy("s", dst, ps, [pb], [dst_buf])
                else:
                    piT_sb, piT_buf, cosq, sinq, tbb = rope
                    qlin = self.scrA[:, 0:1024]
                    qb = B("scrA")
                    if bias_col is not None:
                        self.act(qlin, ps, AF.Identity, [pb, B("bias")], [qb],
                                 bias=self.bias_sb[:, bias_col + oc:bias_col + oc + 1],
                                 scale=1.0)
                    else:
                        self.copy("s", qlin, ps, [pb], [qb])
                    for th in range(2):
                        self.mm(self.psB[:, th * 512:(th + 1) * 512], piT_sb,
                                qlin[:, th * 512:(th + 1) * 512], start=True, stop=True,
                                reads=[piT_buf, qb],
                                writes=[self.bBa if th == 0 else self.bBb])
                    self.tt("v", self.scrB[:, 0:1024], qlin, cosq, ALU.mult,
                            [qb, tbb], [B("scrB")])
                    self.tt("v", self.scrC[:, 0:1024], self.psB, sinq, ALU.mult,
                            [self.bBa, self.bBb, tbb], [B("scrC")])
                    self.tt("v", dst, self.scrB[:, 0:1024], self.scrC[:, 0:1024],
                            ALU.add, [B("scrB"), B("scrC")], [dst_buf])
                self.pump_spline(pump)
            if hook is not None:
                hook(grp)

    def attn_qt(self, cs):
        B = self.B
        c0 = cs[0]
        E = c0["E"]
        nQ = (E + 1023) // 1024
        mstart, mwidth, mask_ap, mbuf = c0["mask"]

        def stq_of(c):
            return self.ast[:, c["par"] * 32:(c["par"] + 1) * 32]

        def phaseA(c, Q):
            stq = stq_of(c)
            bstq = B(f"ast{c['par']}q{Q}")
            w = min(1024, E - Q * 1024)
            ps = self.psA_h[c["slot"]]
            pb = self.bA[c["slot"]]
            lhsT = c["Q_sb"][:, c["q_col"]:c["q_col"] + 128]
            for ch in range((w + 511) // 512):
                cw = min(512, w - ch * 512)
                col = Q * 1024 + ch * 512
                masked = (mstart <= col < mstart + mwidth)
                self.mm(ps[:, ch * 512: ch * 512 + cw], lhsT,
                        c["krot_sb"][:, c["kcol_of"](Q) + ch * 512:
                                     c["kcol_of"](Q) + ch * 512 + cw],
                        start=True, stop=not masked,
                        reads=[c["Q_buf"], c["krot_buf"]], writes=[pb])
                if masked:
                    self.mm(ps[:, ch * 512: ch * 512 + cw], self.identb,
                            mask_ap[:, col - mstart: col - mstart + cw],
                            start=False, stop=True,
                            reads=[B("identb"), mbuf], writes=[pb])
            self.reduce(stq[:, Q:Q + 1], ps[:, 0:w], ALU.max, [pb], [bstq], negate=True)
            sl = Q % c["Pslots"]
            pqb = B(f"{c['Pn']}_{sl}")
            self.act(c["P"][:, sl * 1024: sl * 1024 + w], ps[:, 0:w], AF.Exp,
                     [pb, bstq], [pqb, bstq],
                     scale=1.0, bias=stq[:, Q:Q + 1], accum_out=stq[:, 8 + Q:9 + Q])

        def phaseBtrans(c, Q):
            w = min(1024, E - Q * 1024)
            nkt = w // 128
            sl = Q % c["Pslots"]
            pqb = B(f"{c['Pn']}_{sl}")
            ptb = B(f"{c['PTn']}{Q % 2}")
            PT = c["PT"]
            h = self._tg_flip = 1 - getattr(self, "_tg_flip", 1)
            ps = self.psT_h[h]
            pbuf = self.bT[h]
            for i in range(nkt):
                ap = c["P"][:, (sl * 8 + i) * 128:(sl * 8 + i + 1) * 128]
                self.op("t", lambda ap=ap, i=i, ps=ps: self.t.transpose(
                    ps[:, i * 128:(i + 1) * 128], ap, self.identb),
                    reads=[pqb, B("identb")], writes=[pbuf])
            self.copy("v", PT[:, (Q % 2) * 1024:(Q % 2) * 1024 + nkt * 128],
                      ps[:, 0:nkt * 128], [pbuf], [ptb])

        def phaseBpv(c, Q):
            w = min(1024, E - Q * 1024)
            nkt = w // 128
            ptb = B(f"{c['PTn']}{Q % 2}")
            PT = c["PT"]
            ops = self.psB_a[:, 0:128] if c["slot"] == 0 else self.psB_b[:, 0:128]
            opb = self.bBa if c["slot"] == 0 else self.bBb
            for i in range(nkt):
                kt = Q * 8 + i
                self.mm(ops, PT[:, (Q % 2) * 1024 + i * 128:(Q % 2) * 1024 + (i + 1) * 128],
                        c["V_sb"][:, c["vcol_of"](kt): c["vcol_of"](kt) + 128],
                        start=(i == 0), stop=(i == nkt - 1),
                        reads=[ptb, c["V_buf"]], writes=[opb])
            self.copy("s", c["outq"][:, Q * 128:(Q + 1) * 128], ops, [opb],
                      [B(c["outq_n"])])

        for Q in range(nQ + 3):
            for c in cs:
                if Q < nQ:
                    phaseA(c, Q)
            for c in cs:
                if 0 <= Q - 2 < nQ:
                    phaseBtrans(c, Q - 2)
            for c in cs:
                if 0 <= Q - 3 < nQ:
                    phaseBpv(c, Q - 3)
            self.pump_spline(c0.get("pump", 0))

        for c in cs:
            stq = stq_of(c)
            bstq = [B(f"ast{c['par']}q{q}") for q in range(8)]
            bst = B(f"ast{c['par']}")
            outq = c["outq"]
            boutq = B(c["outq_n"])
            self.reduce(stq[:, 24:25], stq[:, 0:nQ], ALU.min, bstq[:nQ] + [bst], [bst])
            self.ts("v", stq[:, 0:nQ], stq[:, 0:nQ], stq[:, 24:25], -1.0,
                    ALU.subtract, ALU.mult, bstq[:nQ] + [bst], [bst] + bstq[:nQ])
            self.act(stq[:, 0:nQ], stq[:, 0:nQ], AF.Exp, [bst], [bst])
            self.tt("v", stq[:, 16:16 + nQ], stq[:, 0:nQ], stq[:, 8:8 + nQ], ALU.mult,
                    [bst, boutq], [bst])
            self.reduce(stq[:, 25:26], stq[:, 16:16 + nQ], ALU.add, [bst], [bst])
            self.op("v", lambda s=stq: self.v.reciprocal(s[:, 26:27], s[:, 25:26]),
                    [bst], [bst])
            self.ts("v", stq[:, 0:nQ], stq[:, 0:nQ], stq[:, 26:27], 0.0,
                    ALU.mult, ALU.bypass, [bst], [bst])
            if nQ == 1:
                self.ts("v", c["xa_ap"], outq[:, 0:128], stq[:, 0:1], 0.0,
                        ALU.mult, ALU.bypass, [boutq, bst], [B("xattn")])
            else:
                par = c["par"]
                acc = self.scrC[:, 1024 + par * 128:1152 + par * 128]
                bacc = B(f"merge_acc{par}")
                self.ts("v", acc, outq[:, 0:128], stq[:, 0:1], 0.0,
                        ALU.mult, ALU.bypass, [boutq, bst], [bacc])
                for Q in range(1, nQ - 1):
                    self.stt("v", acc, outq[:, Q * 128:(Q + 1) * 128], stq[:, Q:Q + 1],
                             acc, ALU.mult, ALU.add, [boutq, bst, bacc], [bacc])
                Q = nQ - 1
                self.stt("v", c["xa_ap"], outq[:, Q * 128:(Q + 1) * 128], stq[:, Q:Q + 1],
                         acc, ALU.mult, ALU.add, [boutq, bst, bacc], [B("xattn")])

    def xattn_tm_to_fm(self, heads=None):
        B = self.B
        for h in (range(NH) if heads is None else heads):
            hh = self._tg_flip = 1 - getattr(self, "_tg_flip", 1)
            ps = self.psT_h[hh]
            pbuf = self.bT[hh]
            for qt in range(8):
                sl = self.xattn[:, h * TOK + qt * 128: h * TOK + (qt + 1) * 128]
                self.op("t", lambda sl=sl, ps=ps, qt=qt: self.t.transpose(
                    ps[:, qt * 128:(qt + 1) * 128], sl, self.identb),
                    reads=[B("xattn"), B("identb")], writes=[pbuf])
            self.copy("v", self.xattn[:, h * TOK:(h + 1) * TOK], ps[:, 0:1024],
                      [pbuf], [B("xattn")])

    def out_proj(self, w_dram, bias_col, dst_dram, dst_buf, pump=0):
        B = self.B
        OD = w_dram.tensor.shape[1]
        for grp in range(8):
            self._wstc = (getattr(self, "_wstc", 0) + 1) % 2
            gw = self.wst[self._wstc]
            gwb = B(f"wstream{self._wstc}")
            gw3 = gw.rearrange("p (f c) -> p f c", f=16)
            src = bass_rust.AP(w_dram.tensor, grp * 256,
                               [[OD, 128], [128 * OD, 16], [1, 256]])
            self.dma("g", gw3, src, writes=[gwb])
            for o2 in range(2):
                oc = grp * 2 + o2
                ps = self.psA_h[o2]
                pb = self.bA[o2]
                for fc in range(16):
                    lhsT = gw3[:, fc:fc + 1, o2 * 128:(o2 + 1) * 128].squeeze(1)
                    for th in range(2):
                        self.mm(ps[:, th * 512:(th + 1) * 512], lhsT,
                                self.xattn[:, fc * TOK + th * 512: fc * TOK + (th + 1) * 512],
                                start=(fc == 0), stop=(fc == 15),
                                reads=[gwb, B("xattn")], writes=[pb])
                if bias_col is not None:
                    self.act(self.scrA[:, 0:1024], ps, AF.Identity, [pb, B("bias")],
                             [B("scrA")],
                             bias=self.bias_sb[:, bias_col + oc:bias_col + oc + 1], scale=1.0)
                else:
                    self.copy("s", self.scrA[:, 0:1024], ps, [pb], [B("scrA")])
                for tq in range(2):
                    items = [(self.scrA[:, (tq * 4 + k) * 128:(tq * 4 + k + 1) * 128],
                              [B("scrA")]) for k in range(4)]
                    self.tgroup(items, self.scrB[:, tq * 512:(tq + 1) * 512], B("scrB"),
                                "v", dtype="f")
                dst = bass_rust.AP(self.contrib.tensor, oc * 128,
                                   [[D, 128], [128 * D, 8], [1, 128]])
                self.dma("g", dst, self.scrB[:, 0:1024].rearrange("p (t f) -> p t f", f=128),
                         reads=[B("scrB")], writes=[B("contrib")])
                self.pump_spline(pump)

    def sub1(self):
        B = self.B
        st_sub = ExitStack()
        self.xattn = self.sb("xattn1", [128, 16 * TOK], BF16, st=st_sub)
        self.scrC = self.sb("scrC1", [128, D], F32, st=st_sub)
        self.wst = [self.sb("wstr1a", [128, 16 * 256], BF16, st=st_sub),
                    self.sb("wstr1b", [128, 16 * 256], BF16, st=st_sub)]
        self.bufs.pop("xattn", None)
        st_xn = ExitStack()
        xn = self.sb("xn1", [128, 16 * TOK], BF16, st=st_xn)
        bxn = B("xn1")
        self.layernorm(self.x_in, B("x_in"), xn, bxn,
                       store_to=self.resid, store_buf=B("resid"))
        st = ExitStack()
        sbl = lambda n, s, d=F32: self.sb(n, s, d, st=st)
        kvfm = sbl("kvfm", [128, 1024])
        kvtok = sbl("kvtok", [128, 1024])
        wl = sbl("wl", [128, 16 * 128], BF16)
        wl3 = wl.rearrange("p (f c) -> p f c", f=16)
        self.dma("g", wl3,
                 bass_rust.AP(self.w_latent.tensor, 0, [[LC, 128], [128 * LC, 16], [1, 128]]),
                 writes=[B("wl")])
        for fc in range(16):
            for th in range(2):
                self.mm(self.psA_h[0][:, th * 512:(th + 1) * 512],
                        wl3[:, fc:fc + 1, :].squeeze(1),
                        xn[:, fc * TOK + th * 512: fc * TOK + (th + 1) * 512],
                        start=(fc == 0), stop=(fc == 15), reads=[B("wl"), bxn],
                        writes=[self.bA[0]])
        self.act(kvfm, self.psA_h[0], AF.Identity, [self.bA[0], B("bias")], [B("kvfm")],
                 bias=self.bias_sb[:, 0:1], scale=1.0)
        self.dma("y", self.cc1f_in, kvfm, reads=[B("kvfm")], writes=[B("cc1f_in")])
        self.op("g", lambda: self.g.collective_compute(
            "AllGather", ALU.bypass, replica_groups=[list(range(NCORES))],
            ins=[self.cc1f_in.opt()], outs=[self.cc1f_out.opt()]),
            reads=[B("cc1f_in")], writes=[B("cc1f_out_src")], sem="ag_1f", n=1)
        for tq in range(2):
            items = [(kvfm[:, (tq * 4 + k) * 128:(tq * 4 + k + 1) * 128], [B("kvfm")])
                     for k in range(4)]
            self.tgroup(items, kvtok[:, tq * 512:(tq + 1) * 512], B("kvtok"), "v", dtype="f")
        self.dma("y", bass_rust.AP(self.cc1_in.tensor, 0, [[LC, 128], [128 * LC, 8], [1, LC]]),
                 kvtok.rearrange("p (t f) -> p t f", f=128), reads=[B("kvtok")],
                 writes=[B("cc1_in")])
        self.op("g", lambda: self.g.collective_compute(
            "AllGather", ALU.bypass, replica_groups=[list(range(NCORES))],
            ins=[self.cc1_in.opt()], outs=[self.cc1_out.opt()]),
            reads=[B("cc1_in")], writes=[B("cc1_out_src")], sem="ag_1", n=1)
        self.cc_barrier("cc1_out")
        B("cc1f_out").wtok = B("cc1_out").wtok
        st.close()

        st = ExitStack()
        sbl = lambda n, s, d=F32: self.sb(n, s, d, st=st)
        V = sbl("Vsb", [128, 64 * 128], BF16)
        krot = sbl("krot", [128, T], BF16)
        cosq = sbl("cosq", [128, 1024])
        sinq = sbl("sinq", [128, 1024])
        Qrot = sbl("Qrot", [128, 2048], BF16)
        maska = sbl("maska", [128, 1024], BF16)
        st_kb = ExitStack()
        kvx = self.sb("kvx", [128, 512], F32, st=st_kb)
        cosk = self.sb("cosk", [128, 512], F32, st=st_kb)
        sink = self.sb("sink", [128, 512], F32, st=st_kb)
        self.dma("y", cosq, self.cos_aq, writes=[B("ropeq")])
        self.dma("y", sinq, self.sin_aq, writes=[B("ropeq")])
        self.dma("y", maska, self.mask_a_d, writes=[B("maska")])
        ld0 = self.proj_load(self.w_d, 0, 2)
        self.proj(xn, bxn, self.w_d, 0, 2, Qrot, B("Qrot"), 0, bias_col=1,
                  rope=(self.piTa_s, B("piTa"), cosq, sinq, B("ropeq")), pump=1,
                  preloaded=ld0)
        nld = self.proj_load(self.w_d, 256, 2)
        for m in range(64):
            g0 = grow(m * 128)
            self.dma("g", V[:, m * 128:(m + 1) * 128],
                     bass_rust.AP(self.cc1_out.tensor, g0 * LC, [[LC, 128], [1, LC]]),
                     reads=[B("cc1_out")], writes=[B("V")])
        for ch in range(16):
            self.dma("y", cosk, self.cos_a[:, ch * 512:(ch + 1) * 512], writes=[B("ropek")])
            self.dma("y", sink, self.sin_a[:, ch * 512:(ch + 1) * 512], writes=[B("ropek")])
            src = bass_rust.AP(self.cc1f_out.tensor,
                               (4 * (ch % 2)) * 128 * TOK + (ch // 2) * 128,
                               [[TOK, 128], [128 * TOK, 4], [1, 128]])
            self.dma("y", kvx.rearrange("p (a j) -> p a j", j=128), src,
                     reads=[B("cc1f_out")], writes=[B("kvx")])
            self.mm(self.psB[:, 0:512], self.piTa_s, kvx[:, 0:512], start=True, stop=True,
                    reads=[B("piTa"), B("kvx")], writes=[self.bBa, self.bBb])
            self.tt("v", self.scrA[:, 0:512], kvx[:, 0:512], cosk, ALU.mult,
                    [B("kvx"), B("ropek")], [B("scrA")])
            self.tt("v", self.scrB[:, 0:512], self.psB[:, 0:512], sink, ALU.mult,
                    [self.bBa, self.bBb, B("ropek")], [B("scrB")])
            self.tt("v", krot[:, ch * 512:(ch + 1) * 512], self.scrA[:, 0:512],
                    self.scrB[:, 0:512], ALU.add, [B("scrA"), B("scrB")], [B("krot")])
            self.pump_spline(2)
        st_kb.close()
        P2 = [sbl(f"Pa{sx}", [128, 3072], BF16) for sx in range(2)]
        PT2 = [sbl(f"PTa{sx}", [128, 2048], BF16) for sx in range(2)]
        outq2 = [sbl(f"outq{sx}", [128, 8 * 128]) for sx in range(2)]

        it = 0
        for hp in range(8):
            if hp > 0:
                self.proj(xn, bxn, self.w_d, hp * 256, 2, Qrot, B("Qrot"), 0,
                          bias_col=1 + hp * 2,
                          rope=(self.piTa_s, B("piTa"), cosq, sinq, B("ropeq")), pump=1,
                          preloaded=nld)
                if hp + 1 < 8:
                    nld = self.proj_load(self.w_d, (hp + 1) * 256, 2)
            for qt in range(8):
                cs = []
                for sx in range(2):
                    h = hp * 2 + sx
                    cs.append(dict(
                        qt=qt, E=1024 * (qt + 1), par=(it % 2) * 2 + sx, slot=sx,
                        kcol_of=lambda Q: Q * 1024,
                        Q_sb=Qrot, Q_buf=B("Qrot"), q_col=sx * 1024 + qt * 128,
                        krot_sb=krot, krot_buf=B("krot"),
                        V_sb=V, V_buf=B("V"), vcol_of=lambda kt: kt * 128,
                        mask=(1024 * qt, 1024, maska, B("maska")),
                        P=P2[sx], Pn=f"Pa{sx}", Pslots=3, PT=PT2[sx], PTn=f"PTa{sx}",
                        outq=outq2[sx], outq_n=f"outq{sx}",
                        xa_ap=self.xattn[:, h * TOK + qt * 128:
                                         h * TOK + (qt + 1) * 128],
                        pump=1))
                self.attn_qt(cs)
                it += 1
            self.xattn_tm_to_fm([hp * 2, hp * 2 + 1])
        st.close()
        st_xn.close()
        self.out_proj(self.w_proj, 17, self.resid, B("resid"), pump=2)
        st_sub.close()

    def sub_ffn(self, first):
        B = self.B
        st_xn = ExitStack()
        xn = self.sb("xnf1" if first else "xnf2", [128, 16 * TOK], BF16, st=st_xn)
        bxn = B("xnf" + ("1" if first else "2"))
        self.layernorm(self.resid, B("resid"), xn, bxn, combine=True)
        st = ExitStack()
        sbl = lambda n, s, d=F32: self.sb(n, s, d, st=st)
        inter = sbl("inter" + ("1" if first else "2"), [128, 64 * TOK], BF16)
        wg = [sbl(f"wg0_{first}", [128, 16 * 128], BF16), sbl(f"wg1_{first}", [128, 16 * 128], BF16)]
        wu = [sbl(f"wu0_{first}", [128, 16 * 128], BF16), sbl(f"wu1_{first}", [128, 16 * 128], BF16)]
        gi = sbl(f"gi_{first}", [128, TOK], BF16)

        def wtile_ap(gt, n=16):
            return bass_rust.AP(self.ccw_out.tensor, gt * 128 * 128,
                                [[128, 128], [128 * 128, n], [1, 128]])

        bA4 = [B(f"psA4_{k}") for k in range(4)]
        for ic in range(64):
            i2 = ic % 2
            wg3 = wg[i2].rearrange("p (f c) -> p f c", f=16)
            wu3 = wu[i2].rearrange("p (f c) -> p f c", f=16)
            self.dma("y", wg3, wtile_ap(ic * 16), reads=[B("ccw_out")], writes=[B(f"wg{i2}")])
            self.dma("g", wu3, wtile_ap(TPM + ic * 16), reads=[B("ccw_out")],
                     writes=[B(f"wu{i2}")])
            for th in range(2):
                j = ic * 2 + th
                sg, su = (2 * j) % 4, (2 * j + 1) % 4
                for fc in range(16):
                    lg = wg3[:, fc:fc + 1, :].squeeze(1)
                    self.mm(self.psA4[sg], lg,
                            xn[:, fc * TOK + th * 512: fc * TOK + (th + 1) * 512],
                            start=(fc == 0), stop=(fc == 15),
                            reads=[B(f"wg{i2}"), bxn], writes=[bA4[sg]])
                for fc in range(16):
                    lu = wu3[:, fc:fc + 1, :].squeeze(1)
                    self.mm(self.psA4[su], lu,
                            xn[:, fc * TOK + th * 512: fc * TOK + (th + 1) * 512],
                            start=(fc == 0), stop=(fc == 15),
                            reads=[B(f"wu{i2}"), bxn], writes=[bA4[su]])
                self.act(gi[:, th * 512:(th + 1) * 512], self.psA4[sg], AF.Silu,
                         [bA4[sg]], [B("gi")])
                self.stt("v", inter[:, ic * TOK + th * 512: ic * TOK + (th + 1) * 512],
                         self.psA4[su], 1.0, gi[:, th * 512:(th + 1) * 512],
                         ALU.mult, ALU.mult, [bA4[su], B("gi")], [B("inter")])
        if not first:
            rres = [self.sb("rres0_f", [128, 1024], F32, st=st)] * 2
        for hc in range(16):
            h2 = hc % 2
            for icg in range(4):
                i2 = icg % 2
                wd3 = wg[i2].rearrange("p (f c) -> p f c", f=16)
                self.dma(("y", "g", "s")[icg % 3], wd3,
                         wtile_ap(2 * TPM + hc * 64 + icg * 16),
                         reads=[B("ccw_out")], writes=[B(f"wg{i2}")])
                for i16 in range(16):
                    ic = icg * 16 + i16
                    lhsT = wd3[:, i16:i16 + 1, :].squeeze(1)
                    for th in range(2):
                        self.mm(self.psA_h[h2][:, th * 512:(th + 1) * 512], lhsT,
                                inter[:, ic * TOK + th * 512: ic * TOK + (th + 1) * 512],
                                start=(ic == 0), stop=(ic == 63),
                                reads=[B(f"wg{i2}"), B("inter")], writes=[self.bA[h2]])
            self.copy("s", self.scrA[:, 0:1024], self.psA_h[h2], [self.bA[h2]], [B("scrA")])
            for tq in range(2):
                items = [(self.scrA[:, (tq * 4 + k) * 128:(tq * 4 + k + 1) * 128],
                          [B("scrA")]) for k in range(4)]
                self.tgroup(items, self.scrB[:, tq * 512:(tq + 1) * 512], B("scrB"),
                            "v", dtype="f")
            if first:
                dst = bass_rust.AP(self.contrib.tensor, hc * 128,
                                   [[D, 128], [128 * D, 8], [1, 128]])
                self.dma("g", dst,
                         self.scrB[:, 0:1024].rearrange("p (t f) -> p t f", f=128),
                         reads=[B("scrB")], writes=[B("contrib")])
            else:
                rap = bass_rust.AP(self.resid.tensor, hc * 128,
                                   [[D, 128], [128 * D, 8], [1, 128]])
                self.dma("g", rres[h2].rearrange("p (t f) -> p t f", f=128), rap,
                         reads=[B("resid")], writes=[B("rres0")])
                self.tt("v", self.scrB[:, 0:1024], self.scrB[:, 0:1024], rres[h2],
                        ALU.add, [B("scrB"), B("rres0")], [B("scrB")])
                dst = bass_rust.AP(self.out_d.tensor, hc * 128,
                                   [[D, 128], [128 * D, 8], [1, 128]])
                self.dma("g", dst,
                         self.scrB[:, 0:1024].rearrange("p (t f) -> p t f", f=128),
                         reads=[B("scrB")], writes=[B("out")])
        st.close()
        st_xn.close()

    def sub3(self):
        B = self.B
        st0 = ExitStack()
        Q2 = self.sb("Q2", [128, 16 * TOK], BF16, st=st0)
        self.xattn = self.sb("xattn3", [128, 16 * TOK], BF16, st=st0)
        self.scrC = self.sb("scrC3", [128, D], F32, st=st0)
        self.wst = [self.sb("wstr3a", [128, 16 * 256], BF16, st=st0),
                    self.sb("wstr3b", [128, 16 * 256], BF16, st=st0)]
        self.bufs.pop("xattn", None)
        st = ExitStack()
        sbl = lambda n, s, d=F32: self.sb(n, s, d, st=st)
        kown = sbl("kown", [128, 8 * TOK], BF16)
        vown = sbl("vown", [128, 8 * TOK], BF16)
        cosq = sbl("cosq3", [128, 1024])
        sinq = sbl("sinq3", [128, 1024])
        cosk3 = sbl("cosk3", [128, 1024])
        sink3 = sbl("sink3", [128, 1024])
        st_xn = ExitStack()
        xn = self.sb("xn3", [128, 16 * TOK], BF16, st=st_xn)
        bxn = B("xn3")
        self.layernorm(self.resid, B("resid"), xn, bxn, combine=True)
        self.dma("y", cosq, self.cos_bq, writes=[B("ropeq3")])
        self.dma("y", sinq, self.sin_bq, writes=[B("ropeq3")])
        self.dma("y", cosk3, self.cos_bk, writes=[B("ropek3")])
        self.dma("y", sink3, self.sin_bk, writes=[B("ropek3")])
        rope3 = (self.piTb_s, B("piTb"), cosq, sinq, B("ropeq3"))
        rope3k = (self.piTb_s, B("piTb"), cosk3, sink3, B("ropek3"))
        self.proj(xn, bxn, self.w_pack, 2048, 8, kown, B("kown"), 0, rope=rope3k)

        def vhook(grp):
            for kvh in (2 * grp, 2 * grp + 1):
                self.dma("y", self.cc2_in[kvh][:, 0:1024],
                         kown[:, kvh * TOK:(kvh + 1) * TOK],
                         reads=[B("kown")], writes=[B(f"cc2_in{kvh}")])
                self.dma("y", self.cc2_in[kvh][:, 1024:2048],
                         vown[:, kvh * TOK:(kvh + 1) * TOK],
                         reads=[B("vown")], writes=[B(f"cc2_in{kvh}")])
                self.op("g", lambda kvh=kvh: self.g.collective_compute(
                    "AllGather", ALU.bypass, replica_groups=[list(range(NCORES))],
                    ins=[self.cc2_in[kvh].opt()], outs=[self.cc2_out[kvh].opt()]),
                    reads=[B(f"cc2_in{kvh}")], writes=[B(f"cc2_out{kvh}_src")],
                    sem="ag_2", n=1)

        self.proj(xn, bxn, self.w_pack, 3072, 8, vown, B("vown"), 0, hook=vhook)
        self.proj(xn, bxn, self.w_pack, 0, 16, Q2, B("Q2"), 0, rope=rope3)
        st_xn.close()
        self.cc_barrier("cc2x")
        for kvh in range(NKV):
            B(f"cc2_out{kvh}").wtok = B("cc2x").wtok
        st.close()

        st = ExitStack()
        sbl = lambda n, s, d=F32: self.sb(n, s, d, st=st)
        kc = sbl("kc", [128, NKV * 2048], BF16)
        vtok = sbl("vtok", [128, NKV * 2048], BF16)
        kprev = sbl("kprev", [128, NKV], F32)
        vprev = sbl("vprev", [128, NKV], F32)
        cscr = sbl("cscr", [128, 1024], BF16)
        kraw = [sbl(f"kraw{i}", [128, 1024], BF16) for i in range(2)]
        vraw = [sbl(f"vraw{i}", [128, 1024], BF16) for i in range(2)]
        maskb = sbl("maskb", [128, 2048], BF16)
        Pb2 = [sbl(f"Pb{sx}", [128, 2048], BF16) for sx in range(2)]
        PTb2 = [sbl(f"PTb{sx}", [128, 2048], BF16) for sx in range(2)]
        outq32 = [sbl(f"outq3{sx}", [128, 2 * 128]) for sx in range(2)]
        cw = sbl("cw_exp", [128, 32])
        bq, bc = B("qscr"), B("cscr")
        for kvh in range(NKV):
            self.dma("y", cw[:, kvh * 2:(kvh + 1) * 2],
                     self.conv_k[kvh * 128:(kvh + 1) * 128, :], writes=[B("cwexp")])
            self.dma("y", cw[:, 16 + kvh * 2:16 + (kvh + 1) * 2],
                     self.conv_v[kvh * 128:(kvh + 1) * 128, :], writes=[B("cwexp")])

        def conv_unit(qt, kvh):
            half = qt % 2
            for which, co in (("k", 0), ("v", 16)):
                colofs = 0 if which == "k" else 1024
                ri = kvh % 2
                raw = (kraw if which == "k" else vraw)[ri]
                rb = B(("krawb" if which == "k" else "vrawb") + str(ri))
                src = bass_rust.AP(self.cc2_out[kvh].tensor,
                                   colofs + qt * 128,
                                   [[2048, 128], [128 * 2048, 8], [1, 128]])
                self.dma("y", raw.rearrange("p (m f) -> p m f", f=128), src,
                         reads=[B(f"cc2_out{kvh}")], writes=[rb])
                pv = (kprev if which == "k" else vprev)[:, kvh:kvh + 1]
                pvb = B("kprev" if which == "k" else "vprev")
                if which == "k":
                    dst = kc[:, kvh * 2048 + half * 1024:
                             kvh * 2048 + half * 1024 + 1024]
                    db = B("kc")
                else:
                    dst = cscr
                    db = bc
                self.ts("v", dst, raw, cw[:, co + kvh * 2 + 1:co + kvh * 2 + 2], 0.0,
                        ALU.mult, ALU.add, [rb, B("cwexp")], [db])
                self.stt("v", dst[:, 1:1024], raw[:, 0:1023],
                         cw[:, co + kvh * 2:co + kvh * 2 + 1], dst[:, 1:1024],
                         ALU.mult, ALU.add, [rb, B("cwexp"), db], [db])
                if qt > 0:
                    self.stt("v", dst[:, 0:1], pv,
                             cw[:, co + kvh * 2:co + kvh * 2 + 1], dst[:, 0:1],
                             ALU.mult, ALU.add, [pvb, B("cwexp"), db], [db])
                self.copy("v", pv, raw[:, 1023:1024], [rb], [pvb])
                if which == "v":
                    for m4 in range(2):
                        items = [(cscr[:, (m4 * 4 + k) * 128:(m4 * 4 + k + 1) * 128],
                                  [bc]) for k in range(4)]
                        self.tgroup(items,
                                    vtok[:, kvh * 2048 + half * 1024 + m4 * 512:
                                         kvh * 2048 + half * 1024 + (m4 + 1) * 512],
                                    B("vtok"), "v")

        for qt in range(8):
            conv_unit(qt, 0)
            conv_unit(qt, 1)
            self.dma("y", maskb, self.mask_b_d[qt * 128:(qt + 1) * 128, :],
                     writes=[B("maskb")])
            kbase_q = qt - 1 if qt >= 1 else 0
            E = 2048 if qt >= 1 else 1024
            for hp in range(8):
                kvh = hp
                cs = []
                for sx in range(2):
                    h = hp * 2 + sx
                    cs.append(dict(
                        qt=qt, E=E, par=(hp % 2) * 2 + sx, slot=sx,
                        kcol_of=lambda Q, kvh=kvh: kvh * 2048
                        + ((kbase_q + Q) % 2) * 1024,
                        Q_sb=Q2, Q_buf=B("Q2"), q_col=h * TOK + qt * 128,
                        krot_sb=kc, krot_buf=B("kc"),
                        V_sb=vtok, V_buf=B("vtok"),
                        vcol_of=lambda kt, kvh=kvh: kvh * 2048
                        + ((kbase_q + kt // 8) % 2) * 1024 + (kt % 8) * 128,
                        mask=(0, 2048, maskb, B("maskb")),
                        P=Pb2[sx], Pn=f"Pb{sx}", Pslots=2, PT=PTb2[sx], PTn=f"PTb{sx}",
                        outq=outq32[sx], outq_n=f"outq3{sx}",
                        xa_ap=self.xattn[:, h * TOK + qt * 128:
                                         h * TOK + (qt + 1) * 128]))
                self.attn_qt(cs)
                if hp + 2 < NKV:
                    conv_unit(qt, hp + 2)
                if qt == 7:
                    self.xattn_tm_to_fm([hp * 2, hp * 2 + 1])
        st.close()
        self.out_proj(self.w_o, None, self.resid, B("resid"))
        st0.close()

    def finish(self):
        B = self.B
        ob = B("out")
        toks = ([ob.wtok] if ob.wtok else []) + ob.rtoks
        if DBG_CCW:
            db = B("dbg1")
            toks += ([db.wtok] if db.wtok else [])
        self.wait("y", toks)


def kernel(L0_attn_a_x, L0_attn_a_W_latent, L0_attn_a_b_latent, L0_attn_a_W_d,
           L0_attn_a_b_d, L0_attn_a_W_proj, L0_attn_a_b_proj, L0_attn_b_W_pack,
           L0_attn_b_W_o, L0_attn_b_conv_k, L0_attn_b_conv_v, L0_ffn_gate_cp,
           L0_ffn_up_cp, L0_ffn_down_cp):
    x = np.ascontiguousarray(np.asarray(L0_attn_a_x, np.float32).reshape(T, D))
    cos_a, sin_a, piT_a = make_rope_a()
    cos_b, sin_b, piT_b = make_rope_b()
    ident_f = np.eye(128, dtype=np.float32)
    ident_b = np.eye(128, dtype=np.float32).astype(BFNP)
    f0c = np.broadcast_to(
        (np.arange(128, dtype=np.float64) * (NCTRL - 1.0) / (NSAMP - 1.0)).astype(np.float32),
        (128, 128)).copy()
    cps = [np.asarray(L0_ffn_gate_cp, np.float32),
           np.asarray(L0_ffn_up_cp, np.float32),
           np.asarray(L0_ffn_down_cp, np.float32)]
    conv_k = np.ascontiguousarray(
        np.repeat(np.asarray(L0_attn_b_conv_k, np.float32)[:, None, :], 128, axis=1)
        .reshape(NKV * 128, 2))
    conv_v = np.ascontiguousarray(
        np.repeat(np.asarray(L0_attn_b_conv_v, np.float32)[:, None, :], 128, axis=1)
        .reshape(NKV * 128, 2))
    b_lat = np.ascontiguousarray(np.asarray(L0_attn_a_b_latent, np.float32).reshape(LC, 1))
    b_d = np.ascontiguousarray(np.asarray(L0_attn_a_b_d, np.float32).reshape(NH, 128).T)
    b_pr = np.ascontiguousarray(np.asarray(L0_attn_a_b_proj, np.float32).reshape(16, 128).T)

    in_maps = []
    for core in range(NCORES):
        rows = np.array([gpos(core, r) for r in range(TOK)])
        cp6_dev, phi_dev = spline_core_tables(core, cps)
        in_maps.append({
            "x": np.ascontiguousarray(x[rows]),
            "w_latent": np.asarray(L0_attn_a_W_latent, np.float32),
            "w_d": np.asarray(L0_attn_a_W_d, np.float32),
            "w_proj": np.asarray(L0_attn_a_W_proj, np.float32),
            "w_pack": np.asarray(L0_attn_b_W_pack, np.float32),
            "w_o": np.asarray(L0_attn_b_W_o, np.float32),
            "b_latent": b_lat, "b_d": b_d, "b_proj": b_pr,
            "conv_k": conv_k, "conv_v": conv_v,
            "cp6": cp6_dev, "phi": phi_dev, "f0c": f0c,
            "cos_a": cos_a, "sin_a": sin_a,
            "cos_aq": np.ascontiguousarray(cos_a[:, rows] * SCALE),
            "sin_aq": np.ascontiguousarray(sin_a[:, rows] * SCALE),
            "cos_bq": np.ascontiguousarray(cos_b[:, rows] * SCALE),
            "sin_bq": np.ascontiguousarray(sin_b[:, rows] * SCALE),
            "cos_bk": np.ascontiguousarray(cos_b[:, rows]),
            "sin_bk": np.ascontiguousarray(sin_b[:, rows]),
            "piT_a": piT_a, "piT_b": piT_b,
            "ident_f": ident_f, "ident_b": ident_b,
            "mask_a": make_mask_a(core), "mask_b": make_masks_b(core),
        })
    prog = Prog()
    nc = prog.build()
    res = run_bass_kernel_spmd(nc, in_maps, core_ids=list(range(NCORES)))
    global _LAST_RES
    _LAST_RES = res
    out = np.zeros((T, D), np.float32)
    for core in range(NCORES):
        o = np.asarray(res.results[core]["out"])
        rows = np.array([gpos(core, r) for r in range(TOK)])
        out[rows] = o
    try:
        prog.stack.close()
    except Exception:
        pass
    return out.reshape(1, T, D)
